# version 30
# baseline (speedup 1.0000x reference)
import numpy as np
import concourse.bass as bass
import concourse.mybir as mybir
from concourse.bass_utils import run_bass_kernel_spmd

F32 = mybir.dt.float32
F16 = mybir.dt.float16
I16 = mybir.dt.int16
AF = mybir.ActivationFunctionType
OP = mybir.AluOpType
AX = mybir.AxisListType

NB = 2
L = 2048
FD = 1024
NFT = 8
NST = 16
NDLT = 64
INNER = 4096
NMT_I = 32
CH1 = 512
NCH1 = L // CH1
CH2 = 512
NCH2 = L // CH2
TFC = 512
NTFC = L // TFC
EPS = 1e-5
ENGS = ("tensor", "vector", "scalar", "gpsimd", "sync")


class B:
    def __init__(self, upto="full", taps=()):
        self.upto = upto
        self.taps = set(taps)
        self.tapd = {}
        self.nc = bass.Bass("TRN2", target_bir_lowering=False)
        self._ctx = []
        self.mk = {}
        self._pend = {}
        self._io()
        self._alloc()
        self._prog()

    def _io(self):
        nc = self.nc
        self.x_h = nc.dram_tensor("x", [NB, L, FD], F32, kind="ExternalInput").ap()
        shp = {"w_in": [FD, FD], "b_in": [FD], "conv_w": [FD, FD, 3], "conv_b": [FD],
               "A_log": [FD, NST], "D": [FD], "w_B": [NST, FD], "b_B": [NST],
               "w_C": [NST, FD], "b_C": [NST], "w_d1": [NDLT, FD], "b_d1": [NDLT],
               "w_d2": [FD, NDLT], "b_d2": [FD], "w_out": [FD, FD], "b_out": [FD],
               "w_gate": [INNER, FD], "w_up": [INNER, FD], "w_down": [FD, INNER]}
        self.w_h = {n: nc.dram_tensor(n, s, F32, kind="ExternalInput").ap() for n, s in shp.items()}
        self.out_h = nc.dram_tensor("out", [NB, L, FD], F32, kind="ExternalOutput").ap()
        self.x16 = nc.dram_tensor("x16_scr", [NB, L, FD], F16, kind="Internal").ap()
        self.ws_in = nc.dram_tensor("ws_in", [128, NFT * NFT * 128], F16, kind="Internal").ap()
        self.ws_conv = nc.dram_tensor("ws_conv", [NFT, 128, 3 * NFT * 128], F16, kind="Internal").ap()
        self.ws_out = nc.dram_tensor("ws_out", [NFT, 128, NFT * 128], F16, kind="Internal").ap()
        self.ws_g = nc.dram_tensor("ws_g", [NMT_I, 128, NFT * 128], F16, kind="Internal").ap()
        self.ws_u = nc.dram_tensor("ws_u", [NMT_I, 128, NFT * 128], F16, kind="Internal").ap()
        self.ws_d = nc.dram_tensor("ws_d", [NFT, 128, NMT_I * 128], F16, kind="Internal").ap()
        self.rows_t = nc.dram_tensor("rows_scr", [1 << 17], F32, kind="Internal").ap()
        self.RO = {}
        o = 0
        for nm, sz in (("a0", L), ("b0", L), ("Bm", NST * L), ("Cm", NST * L),
                       ("rsg", L), ("mrsg", L), ("aq", L), ("bq2", L), ("cg", L),
                       ("cu", L), ("s2", L), ("cd", L), ("rb", 8), ("sw", 8)):
            self.RO[nm] = o
            o += sz

    def row(self, nm, a, b2):
        return self.rows_t[self.RO[nm] + a: self.RO[nm] + b2].unsqueeze(0)

    def _alloc(self):
        nc = self.nc
        def sb(name, shape, dt):
            return nc.alloc_sbuf_tensor(name, shape, dt).ap()
        self.X1 = sb("x1", [128, NFT * L], F16)
        self.W = sb("wreg", [128, NFT * (L + 2) + NFT * L], F16)
        self.S = sb("sreg", [128, 33280], F16)
        self.SF = sb("sf32", [128, 5632], F32)
        self.S32 = sb("s32", [128, 2048], F32)
        self.G32 = sb("g32", [128, 512], F32)
        self.GS = sb("gsml", [128, 1920], F16)
        self.I16T = sb("i16", [128, 512], I16)
        self.CAR = sb("carry", [128, 128], F16)
        self.ONE = sb("ones", [128, 8], F16)
        self.IDN = sb("idn", [128, 128], F16)
        self.IDN32 = sb("idn32", [128, 128], F32)
        self.PS = [nc.alloc_psum_tensor("ps%d" % i, [128, 512], F32).ap() for i in range(6)]
        self.PT16 = nc.alloc_psum_tensor("pt16", [128, 512], F16).ap()
        self.PT32 = nc.alloc_psum_tensor("pt32", [128, 512], F32).ap()
        self.GO = {"negA": 0, "D": 128, "b_in": 136, "cb": 144, "b_d2": 152,
                   "b_out": 160, "cI": 168, "c0": 176, "cL": 184, "alpha": 192,
                   "beta": 200, "beta0": 208, "betaL": 216, "SvU": 224, "SvU2": 256,
                   "nwsum": 288, "rb": 296, "bB": 298, "bC": 299, "b_d1": 300,
                   "sw": 304, "half": 312, "eps": 314, "tmp": 320}
        self.TO = {"sum32": 0, "otm": 512, "rowst": 1536, "p1a": 1664, "gn": 1792,
                   "ff": 1856}
        self.GSO = {"wBT": 0, "wCT": 128, "wd1T": 256, "wd2T": 768, "sp": 1792}

    def u_ft(self, ft):
        return self.W[:, ft * (L + 2): (ft + 1) * (L + 2)]

    def vu_ft(self, ft):
        o = NFT * (L + 2)
        return self.W[:, o + ft * L: o + (ft + 1) * L]

    def _enter(self, cm):
        v = cm.__enter__()
        self._ctx.append(cm)
        return v


    def sec(self, eng, fn):
        self._pend.setdefault(eng, []).append(fn)

    def _flushq(self, eng):
        fns = self._pend.get(eng)
        if not fns:
            return
        self._pend[eng] = []
        def big(e, fns=fns):
            for f in fns:
                f(e)
        getattr(self.block, eng)(big)

    def flush_all(self):
        for eng in list(self._pend):
            self._flushq(eng)

    def inc(self, nm, k):
        self.cnt[nm] += k
        return self.cnt[nm]

    def _op(self, eng, nm, fn, inc, skip_own=False):
        before = self.cnt[nm]
        if inc:
            self.inc(nm, inc)
        sem = self.sem[nm]
        def f(e, fn=fn, inc=inc, sem=sem, before=before, skip_own=skip_own):
            if before and not skip_own:
                e.wait_ge(sem, before)
            r = fn(e)
            if inc:
                r.then_inc(sem, inc)
        self.sec(eng, f)

    def vop(self, fn, inc=1):
        self._op("vector", "v", fn, inc)

    def sop(self, fn, inc=1):
        self._op("scalar", "s", fn, inc)

    def gop(self, fn, inc=1):
        self._op("gpsimd", "p", fn, inc)

    def peop(self, fn, inc=1):
        self._op("tensor", "pe", fn, inc, skip_own=True)

    def dma(self, queue, nm, out_ap, in_ap, transpose=False):
        n = self.inc(nm, 16)
        sem = self.sem[nm]
        def f(e, out_ap=out_ap, in_ap=in_ap, sem=sem, transpose=transpose):
            if transpose:
                e.dma_start_transpose(out_ap, in_ap).then_inc(sem, 16)
            else:
                e.dma_start(out=out_ap, in_=in_ap).then_inc(sem, 16)
        self.sec(queue, f)
        return n

    def store(self, out_ap, in_ap, nm="sto0", queue="sync"):
        n = self.inc(nm, 16)
        sem = self.sem[nm]
        def f(e, out_ap=out_ap, in_ap=in_ap, sem=sem):
            e.dma_start(out=out_ap, in_=in_ap).then_inc(sem, 16)
        self.sec(queue, f)
        return n

    def wait(self, eng, nm, val=None):
        val = self.cnt[nm] if val is None else val
        if val <= 0:
            return
        sem = self.sem[nm]
        self.sec(eng, lambda e, sem=sem, val=val: e.wait_ge(sem, val))

    def mark(self, name, nm):
        self.mk[name] = (nm, self.cnt[nm])

    def mwait(self, eng, name):
        if name in self.mk:
            nm, v = self.mk[name]
            self.wait(eng, nm, v)

    def barrier(self):
        self.flush_all()
        self.cnt["bar"] += len(ENGS)
        tgt = self.cnt["bar"]
        sem = self.sem["bar"]
        for eng in ENGS:
            def f(e, sem=sem, tgt=tgt):
                e.sem_inc(sem, 1)
                e.wait_ge(sem, tgt)
            self.sec(eng, f)

    def tap(self, name, src_ap, shape, dtype=None):
        if name not in self.taps:
            return
        dtype = src_ap.dtype if dtype is None else dtype
        t = self.nc.dram_tensor("tap_" + name, list(shape), dtype, kind="ExternalOutput").ap()
        self.tapd[name] = True
        for nm in ("v", "s", "pe", "p", "str", "sts0", "sts1"):
            self.wait("sync", nm)
        self.store(t, src_ap)
        self.wait("sync", "sto0")
        self.wait("gpsimd", "sto0")

    def _prog(self):
        self.block = self._enter(self.nc.Block())
        self.sem = {nm: self._enter(self.nc.semaphore(nm)) for nm in
                    ("l0", "l1", "l2", "l3", "lw", "g0", "g1", "str", "sts0",
                     "sts1", "sto0", "sto1", "pe", "v", "s", "p", "bar")}
        self.cnt = {nm: 0 for nm in self.sem}
        self.prelude()
        self.p0()
        if self.upto == "p0":
            return self.finish()
        for b in range(NB):
            self.barrier()
            self.p1a(b)
            if self.upto == "p1a" and b == 0:
                return self.finish()
            self.barrier()
            self.p1b(b)
            if self.upto == "p1b" and b == 0:
                return self.finish()
            self.chanstats(b)
            self.barrier()
            self.p2(b)
            if self.upto == "p2" and b == 0:
                return self.finish()
            self.barrier()
            self.ffn(b)
            if self.upto == "ffn" and b == 0:
                return self.finish()
        self.finish()

    def prelude(self):
        self.vop(lambda v: v.memset(self.ONE[:], 1.0))
        self.vop(lambda v: v.memset(self.G32[:], 0.0))
        self.vop(lambda v: v.memset(self.S32[:], 0.0))
        self.vop(lambda v: v.memset(
            self.G32[:, self.GO["half"]:self.GO["half"] + 1], 0.5))
        self.vop(lambda v: v.memset(
            self.G32[:, self.GO["eps"]:self.GO["eps"] + 1], EPS), inc=1)
        self.wait("sync", "v")
        self.wait("gpsimd", "v")
        self.vop(lambda v: v.memset(self.CAR[:], 0.0))
        self.vop(lambda v: v.memset(self.IDN[:], 1.0), inc=1)
        self.wait("gpsimd", "v")
        self.gop(lambda g: g.affine_select(self.IDN[:], self.IDN[:], [[1, 128]],
                                           OP.is_equal, 0.0, base=0,
                                           channel_multiplier=-1), inc=1)
        self.wait("vector", "p")
        self.vop(lambda v: v.tensor_copy(self.IDN32[:], self.IDN[:]), inc=1)
        self.tap("idn", self.IDN32[:], (128, 128))

    def p0(self):
        for b2 in range(NB):
            for c in range(4):
                self.dma("gpsimd", "g0", self.x16[b2, c * 512:(c + 1) * 512, :],
                         self.x_h[b2, c * 512:(c + 1) * 512, :])
        self.p0_small()
        self.barrier()
        ga = self.p0_attn_units()
        gq = self.p0_quant_units()
        done = object()
        qi = 0
        alive_a, alive_q = True, True
        while alive_q or alive_a:
            if alive_q:
                alive_q = next(gq, done) is not done
                qi += 1
            if alive_a and (qi % 4 == 0 or not alive_q):
                alive_a = next(ga, done) is not done
        self.p0_dk()
        self.barrier()

    def p0_small(self):
        g = self.G32
        GO = self.GO
        for name, off in (("D", GO["D"]), ("b_in", GO["b_in"]), ("conv_b", GO["cb"]),
                          ("b_d2", GO["b_d2"]), ("b_out", GO["b_out"])):
            for c in range(NFT):
                self.dma("sync", "l0", g[:, off + c: off + c + 1],
                         self.w_h[name][c * 128:(c + 1) * 128].unsqueeze(1))
        self.dma("sync", "l0", g[0:NST, GO["bB"]:GO["bB"] + 1], self.w_h["b_B"].unsqueeze(1))
        self.dma("sync", "l0", g[0:NST, GO["bC"]:GO["bC"] + 1], self.w_h["b_C"].unsqueeze(1))
        self.dma("sync", "l0", g[0:NDLT, GO["b_d1"]:GO["b_d1"] + 1], self.w_h["b_d1"].unsqueeze(1))
        tmp = g[:, GO["tmp"]:GO["tmp"] + NST]
        for c in range(NFT):
            self.mwait("sync", "negA_done")
            self.dma("sync", "l0", tmp, self.w_h["A_log"][c * 128:(c + 1) * 128, :])
            self.wait("scalar", "l0")
            self.sop(lambda s, c=c: s.activation(
                g[:, GO["negA"] + c * NST: GO["negA"] + (c + 1) * NST], tmp,
                AF.Exp), inc=1)
            self.mark("negA_done", "s")
        self.wait("vector", "s")
        self.vop(lambda v: v.tensor_scalar_mul(
            g[:, GO["negA"]:GO["negA"] + 128], g[:, GO["negA"]:GO["negA"] + 128], -1.0),
            inc=1)
        S = self.S
        stg = S[:, 0:1024]
        GSO = self.GSO
        for (wname, gso, m) in (("w_B", "wBT", NST), ("w_C", "wCT", NST), ("w_d1", "wd1T", NDLT)):
            self.mwait("gpsimd", "smallw_done")
            self.dma("gpsimd", "g1", stg[0:m, 0:FD], self.w_h[wname])
            self.wait("tensor", "g1")
            self.wait("tensor", "v")
            for kt in range(NFT):
                self.mwait("tensor", "smallev")
                self.peop(lambda t, kt=kt, m=m: t.transpose(
                    self.PT16[0:128, 0:m], stg[0:m, kt * 128:(kt + 1) * 128], self.IDN[0:m, 0:m]),
                    inc=1)
                self.wait("vector", "pe")
                self.vop(lambda v, kt=kt, gso=gso, m=m: v.tensor_copy(
                    self.GS[:, GSO[gso] + kt * m: GSO[gso] + (kt + 1) * m],
                    self.PT16[0:128, 0:m]), inc=1)
                self.mark("smallev", "v")
            self.mark("smallw_done", "v")
        self.mwait("gpsimd", "smallw_done")
        self.dma("gpsimd", "g1", stg[:, 0:NFT * NDLT].rearrange("p (t d) -> p t d", d=NDLT),
                 self.w_h["w_d2"].rearrange("(t p) d -> p t d", p=128))
        self.wait("tensor", "g1")
        for mt in range(NFT):
            self.mwait("tensor", "smallev")
            self.peop(lambda t, mt=mt: t.transpose(
                self.PT16[0:NDLT, 0:128], stg[:, mt * NDLT:(mt + 1) * NDLT], self.IDN[:]), inc=1)
            self.wait("vector", "pe")
            self.vop(lambda v, mt=mt: v.tensor_copy(
                self.GS[0:NDLT, GSO["wd2T"] + mt * 128: GSO["wd2T"] + (mt + 1) * 128],
                self.PT16[0:NDLT, 0:128]), inc=1)
            self.mark("smallev", "v")
        self.wait("vector", "l0")
        self.vop(lambda v: v.tensor_copy(self.GS[:, GSO["sp"]:GSO["sp"] + 8],
                                         g[:, GO["b_in"]:GO["b_in"] + 8]), inc=1)

    def p0_quant_units(self):
        g = self.G32
        GO = self.GO
        S = self.S
        st32 = [self.SF[:, 0:1024], self.SF[:, 1024:2048], self.SF[:, 4608:5632]]
        u32_b = [self.SF[:, 2048:3072], self.SF[:, 3584:4608]]
        wqn_b = [S[:, 0:1024], S[:, 1024:2048]]
        fb = S[:, 2048:2560]
        mk = S[:, 2560:3072]
        mk2 = S[:, 3072:3584]
        ev16 = [S[:, 3584:4096], S[:, 4096:4608]]
        t0 = g[0:1, GO["tmp"]:GO["tmp"] + 1]
        t1 = g[0:1, GO["tmp"] + 1:GO["tmp"] + 2]
        t2 = g[0:1, GO["tmp"] + 2:GO["tmp"] + 3]
        acc = g[:, GO["tmp"] + 4:GO["tmp"] + 5]
        red = g[:, GO["tmp"] + 5:GO["tmp"] + 6]
        hcol = self.GS[:, self.GSO["sp"] + 8:self.GSO["sp"] + 9]
        for wi, (wname, scr) in enumerate((("w_gate", self.ws_g), ("w_up", self.ws_u),
                                           ("w_down", self.ws_d))):
            wh = self.w_h[wname]
            nrow, ncolw = wh.shape
            ntile = nrow // 128
            csub = ncolw // 1024
            nu = ntile * csub
            ld_cnt = {}

            def issue_p1(u):
                i, cs = divmod(u, csub)
                lsem = "l%d" % (u % 3)
                self.mwait("sync", "q_red%d" % (u % 3))
                self.dma("sync", lsem, st32[u % 3],
                         wh[i * 128:(i + 1) * 128, cs * 1024:(cs + 1) * 1024])
                ld_cnt[u] = self.cnt[lsem]
            self.vop(lambda v: v.memset(acc, 0.0), inc=1)
            issue_p1(0)
            issue_p1(1)
            for u in range(nu):
                st = st32[u % 3]
                if u + 2 < nu:
                    issue_p1(u + 2)
                self.wait("vector", "l%d" % (u % 3), ld_cnt[u])
                self.vop(lambda v, st=st: v.tensor_reduce(
                    red, st, AX.X, OP.add, apply_absolute_value=True))
                self.vop(lambda v: v.tensor_tensor(acc, acc, red, OP.add), inc=1)
                self.mark("q_red%d" % (u % 3), "v")
                yield
            self.vop(lambda v: v.tensor_copy(hcol, acc), inc=1)
            self.wait("tensor", "v")
            self.peop(lambda t: t.matmul(self.PS[4][0:1, 0:1], self.ONE[:, 0:1], hcol,
                                         start=True, stop=True), inc=1)
            self.wait("vector", "pe")
            self.vop(lambda v: v.tensor_copy(t0, self.PS[4][0:1, 0:1]), inc=1)
            self.wait("tensor", "v")
            self.vop(lambda v, n=nrow * ncolw: v.tensor_scalar(t1, t0, 1.0 / n, 1e-5, OP.mult, OP.max))
            self.vop(lambda v: v.reciprocal(t2, t1), inc=1)
            self.wait("sync", "v")
            self.store(self.rows_t[self.RO["sw"] + wi:self.RO["sw"] + wi + 1].unsqueeze(0), t2, nm="str")
            self.store(self.rows_t[self.RO["sw"] + 3 + wi:self.RO["sw"] + 4 + wi].unsqueeze(0), t1, nm="str")
            self.wait("gpsimd", "str")
            self.dma("gpsimd", "g0", g[:, GO["sw"] + wi:GO["sw"] + wi + 1],
                     self.rows_t[self.RO["sw"] + wi:self.RO["sw"] + wi + 1]
                     .unsqueeze(0).partition_broadcast(128))
            self.dma("gpsimd", "g0", g[:, GO["sw"] + 3 + wi:GO["sw"] + 4 + wi],
                     self.rows_t[self.RO["sw"] + 3 + wi:self.RO["sw"] + 4 + wi]
                     .unsqueeze(0).partition_broadcast(128))
            self.wait("scalar", "g0")
            self.wait("vector", "g0")
            def issue_p2(u):
                i, cs = divmod(u, csub)
                lsem = "l%d" % (u % 3)
                self.mwait("sync", "q_quant%d" % (u % 3))
                self.dma("sync", lsem, st32[u % 3],
                         wh[i * 128:(i + 1) * 128, cs * 1024:(cs + 1) * 1024])
                ld_cnt[u] = self.cnt[lsem]
            chain_cnt = {}

            def evac_spill(u):
                i, cs = divmod(u, csub)
                sl0 = cs * 1024
                wqn = wqn_b[u % 2]
                self.wait("tensor", "v", chain_cnt[u])
                for half in range(2):
                    self.mwait("tensor", "q_ev")
                    for j in range(4):
                        blk = half * 4 + j
                        self.peop(lambda t, blk=blk, j=j, wqn=wqn: t.transpose(
                            self.PT16[:, j * 128:(j + 1) * 128],
                            wqn[:, blk * 128:(blk + 1) * 128], self.IDN[:]))
                    if half == 1:
                        self.mark("q_wq%d" % (u % 2), "pe")
                    self.wait("scalar", "pe", self.cnt["pe"])
                    self.mwait("scalar", "q_st%d" % half)
                    self.sop(lambda s, half=half: s.copy(ev16[half], self.PT16[:, 0:512]), inc=1)
                    self.mark("q_ev", "s")
                    self.wait("sync", "s", self.cnt["s"])
                    bj0 = sl0 // 128 + half * 4
                    self.store(scr[i, :, bj0 * 128:bj0 * 128 + 512],
                               ev16[half][:, 0:512], nm="sts%d" % half)
                    self.mark("q_st%d" % half, "sts%d" % half)
            issue_p2(0)
            issue_p2(1)
            for u in range(nu):
                i, cs = divmod(u, csub)
                st = st32[u % 3]
                u32 = u32_b[u % 2]
                wqn = wqn_b[u % 2]
                if u + 2 < nu:
                    issue_p2(u + 2)
                if True:
                    self.wait("scalar", "l%d" % (u % 3), ld_cnt[u])
                    self.mwait("scalar", "q_u%d" % (u % 2))
                    self.sop(lambda s, st=st, wi=wi, u32=u32: s.activation(
                        u32, st, AF.Identity,
                        bias=g[:, GO["half"]:GO["half"] + 1],
                        scale=g[:, GO["sw"] + wi:GO["sw"] + wi + 1]), inc=1)
                    self.mark("q_quant%d" % (u % 3), "s")
                    self.wait("vector", "s", self.cnt["s"])
                    self.mwait("vector", "q_wq%d" % (u % 2))
                    for hc in range(2):
                        sl = slice(hc * 512, (hc + 1) * 512)
                        self.vop(lambda v, sl=sl, u32=u32: v.tensor_copy(
                            self.I16T[:, 0:512], u32[:, sl]))
                        self.vop(lambda v: v.tensor_copy(fb, self.I16T[:, 0:512]))
                        self.vop(lambda v, sl=sl, u32=u32: v.tensor_tensor(
                            mk, fb, u32[:, sl], OP.is_gt))
                        self.vop(lambda v, sl=sl, u32=u32: v.scalar_tensor_tensor(
                            mk2, fb, 1.0, u32[:, sl], OP.add, OP.is_le))
                        self.vop(lambda v: v.tensor_tensor(fb, fb, mk, OP.subtract))
                        self.vop(lambda v: v.tensor_tensor(fb, fb, mk2, OP.add))
                        self.vop(lambda v, sl=sl, wqn=wqn: v.tensor_scalar(
                            wqn[:, sl], fb, 1.0, -1.0, OP.min, OP.max), inc=1)
                    self.mark("q_u%d" % (u % 2), "v")
                    chain_cnt[u] = self.cnt["v"]
                    if u > 0:
                        evac_spill(u - 1)
                yield
            evac_spill(nu - 1)

    def p0_attn_units(self):
        g = self.G32
        GO = self.GO
        S = self.S
        nat_b = [S[:, 13312:16384], S[:, 16384:19456]]
        ev16 = [S[:, 19456:19968], S[:, 19968:20480]]
        natsep = S[:, 20480:21504]
        units = [(wname, tag, mt) for (wname, tag) in (("w_in", "in"), ("w_out", "out"))
                 for mt in range(NFT)]
        units += [("conv_w", "conv", mt) for mt in range(NFT)]
        g1_cnt = {}

        def issue(i):
            wname, tag, mt = units[i]
            nat = nat_b[i % 2]
            self.mwait("gpsimd", "aw_natS%d" % (i % 2))
            self.mwait("gpsimd", "aw_natV%d" % (i % 2))
            if tag == "conv":
                self.dma("gpsimd", "g1", nat[:, 0:3072],
                         self.w_h["conv_w"][mt * 128:(mt + 1) * 128, :, :]
                         .rearrange("p i k -> p (i k)"))
            else:
                self.dma("gpsimd", "g1", nat[:, 0:1024],
                         self.w_h[wname][mt * 128:(mt + 1) * 128, :])
            g1_cnt[i] = self.cnt["g1"]
        issue(0)
        issue(1)
        for i, (wname, tag, mt) in enumerate(units):
            nat = nat_b[i % 2]
            if tag != "conv":
                self.wait("tensor", "g1", g1_cnt[i])
                for half in range(2):
                    self.mwait("tensor", "q_ev")
                    for j in range(4):
                        kt = half * 4 + j
                        self.peop(lambda t, kt=kt, j=j, nat=nat: t.transpose(
                            self.PT16[:, j * 128:(j + 1) * 128],
                            nat[:, kt * 128:(kt + 1) * 128], self.IDN[:]))
                    self.wait("scalar", "pe")
                    self.mwait("scalar", "aw_st%d" % half)
                    self.sop(lambda s, half=half, ev16=ev16: s.copy(
                        ev16[half], self.PT16[:, 0:512]), inc=1)
                    self.mark("q_ev", "s")
                    self.wait("gpsimd", "s")
                    kt0 = half * 4
                    if tag == "in":
                        self.store(self.ws_in[:, (mt * 8 + kt0) * 128:(mt * 8 + kt0) * 128 + 512],
                                   ev16[half][:, 0:512], nm="sts%d" % half, queue="gpsimd")
                    else:
                        self.store(self.ws_out[mt, :, kt0 * 128:kt0 * 128 + 512],
                                   ev16[half][:, 0:512], nm="sts%d" % half, queue="gpsimd")
                    self.mark("aw_st%d" % half, "sts%d" % half)
                if tag == "out":
                    self.wait("vector", "g1", g1_cnt[i])
                    self.vop(lambda v, mt=mt, nat=nat: v.tensor_reduce(
                        g[:, GO["nwsum"] + mt:GO["nwsum"] + mt + 1], nat[:, 0:1024],
                        AX.X, OP.add, negate=True), inc=1)
                self.mark("aw_natS%d" % (i % 2), "s")
                self.mark("aw_natV%d" % (i % 2), "v")
                if i + 2 < len(units):
                    issue(i + 2)
                yield
            else:
                self.wait("vector", "g1", g1_cnt[i])
                for k in range(3):
                    self.mwait("vector", "aw_sep")
                    self.vop(lambda v, k=k, nat=nat, natsep=natsep: v.tensor_copy(
                        natsep, nat[:, k:3072:3]), inc=1)
                    self.wait("tensor", "v")
                    for half in range(2):
                        self.mwait("tensor", "q_ev")
                        for j in range(4):
                            kt = half * 4 + j
                            self.peop(lambda t, kt=kt, j=j, natsep=natsep: t.transpose(
                                self.PT16[:, j * 128:(j + 1) * 128],
                                natsep[:, kt * 128:(kt + 1) * 128], self.IDN[:]))
                        if half == 1:
                            self.mark("aw_sep", "pe")
                        self.wait("scalar", "pe")
                        self.mwait("scalar", "aw_st%d" % half)
                        self.sop(lambda s, half=half, ev16=ev16: s.copy(
                            ev16[half], self.PT16[:, 0:512]), inc=1)
                        self.mark("q_ev", "s")
                        self.wait("gpsimd", "s")
                        kt0 = half * 4
                        self.store(self.ws_conv[mt, :, (k * 8 + kt0) * 128:(k * 8 + kt0) * 128 + 512],
                                   ev16[half][:, 0:512], nm="sts%d" % half, queue="gpsimd")
                        self.mark("aw_st%d" % half, "sts%d" % half)
                    yield
                self.mark("aw_natS%d" % (i % 2), "s")
                self.mark("aw_natV%d" % (i % 2), "v")
                if i + 2 < len(units):
                    issue(i + 2)

    def p0_dk(self):
        g = self.G32
        GO = self.GO
        S = self.S
        self.wait("sync", "sts0")
        self.wait("sync", "sts1")
        self.wait("tensor", "v")
        binh = self.GS[:, self.GSO["sp"]:self.GSO["sp"] + 8]
        cw1 = S[:, 20480:21504]
        for k in range(3):
            for mt in range(NFT):
                self.mwait("sync", "dk_mm")
                self.dma("sync", "l0", cw1, self.ws_conv[mt, :, k * 1024:(k + 1) * 1024])
                self.wait("tensor", "l0")
                for kt in range(NFT):
                    self.peop(lambda t, kt=kt, mt=mt: t.matmul(
                        self.PS[4][:, mt:mt + 1], cw1[:, kt * 128:(kt + 1) * 128],
                        binh[:, kt:kt + 1], start=(kt == 0), stop=(kt == NFT - 1)))
                self.mark("dk_mm", "pe")
            self.wait("vector", "pe")
            self.vop(lambda v, k=k: v.tensor_copy(
                g[:, GO["tmp"] + 8 * k:GO["tmp"] + 8 * (k + 1)], self.PS[4][:, 0:8]), inc=1)
            self.wait("tensor", "v")
        d0 = g[:, GO["tmp"]:GO["tmp"] + 8]
        d1 = g[:, GO["tmp"] + 8:GO["tmp"] + 16]
        d2 = g[:, GO["tmp"] + 16:GO["tmp"] + 24]
        cI = g[:, GO["cI"]:GO["cI"] + 8]
        c0 = g[:, GO["c0"]:GO["c0"] + 8]
        cL = g[:, GO["cL"]:GO["cL"] + 8]
        cb = g[:, GO["cb"]:GO["cb"] + 8]
        self.wait("vector", "l0")
        self.vop(lambda v: v.tensor_tensor(cI, cb, d0, OP.add))
        self.vop(lambda v: v.tensor_tensor(cI, cI, d1, OP.add))
        self.vop(lambda v: v.tensor_tensor(cI, cI, d2, OP.add))
        self.vop(lambda v: v.tensor_tensor(c0, cI, d0, OP.subtract))
        self.vop(lambda v: v.tensor_tensor(cL, cI, d2, OP.subtract), inc=1)
        self.tap("d0", d0, (128, 8))
        self.tap("cb", cb, (128, 8))
        self.tap("cI", cI, (128, 8))
        self.tap("negA", g[:, GO["negA"]:GO["negA"] + 128], (128, 128))
        self.tap("nwsum", g[:, GO["nwsum"]:GO["nwsum"] + 8], (128, 8))

    def p1a(self, b):
        g = self.G32
        GO = self.GO
        S = self.S
        s32 = self.S32
        T = self.TO["p1a"]
        buf_b = [S[:, 0:1024], S[:, 1024:2048]]
        sq_b = [S[:, 2048:3072], S[:, 3072:4096]]
        sx = s32[:, T:T + 16]
        sxx = s32[:, T + 16:T + 32]
        a0 = s32[:, T + 32:T + 48]
        b0 = s32[:, T + 48:T + 64]
        w = s32[:, T + 64:T + 80]
        mu = s32[:, T + 80:T + 96]
        var = s32[:, T + 96:T + 112]
        tt = s32[:, T + 112:T + 128]
        self.wait("sync", "g0")
        self.wait("sync", "g1")
        ld = {}

        def issue(t):
            lsem = "l%d" % (t % 2)
            self.mwait("sync", "p1a_buf%d" % (t % 2))
            self.dma("sync", lsem, buf_b[t % 2], self.x16[b, t * 128:(t + 1) * 128, :])
            ld[t] = self.cnt[lsem]
        issue(0)
        issue(1)
        for t in range(16):
            buf = buf_b[t % 2]
            sq = sq_b[t % 2]
            self.wait("scalar", "l%d" % (t % 2), ld[t])
            self.mwait("scalar", "p1a_sq%d" % (t % 2))
            self.sop(lambda s, buf=buf, sq=sq: s.activation(sq, buf, AF.Square), inc=1)
            self.wait("vector", "l%d" % (t % 2), ld[t])
            self.wait("vector", "s")
            self.vop(lambda v, buf=buf, t=t: v.tensor_reduce(
                sx[:, t:t + 1], buf, AX.X, OP.add))
            self.vop(lambda v, sq=sq, t=t: v.tensor_reduce(
                sxx[:, t:t + 1], sq, AX.X, OP.add), inc=1)
            self.mark("p1a_buf%d" % (t % 2), "v")
            self.mark("p1a_sq%d" % (t % 2), "v")
            if t + 2 < 16:
                issue(t + 2)
        self.wait("tensor", "v")
        self.vop(lambda v: v.tensor_scalar_mul(mu, sx, 1.0 / FD))
        self.vop(lambda v: v.tensor_scalar_mul(var, sxx, 1.0 / FD))
        self.vop(lambda v: v.tensor_tensor(tt[:, 0:16], mu, mu, OP.mult))
        self.vop(lambda v: v.tensor_tensor(var, var, tt[:, 0:16], OP.subtract), inc=1)
        self.wait("scalar", "v")
        self.sop(lambda s: s.activation(tt[:, 0:16], var, AF.Sqrt, bias=g[:, GO["eps"]:GO["eps"] + 1]), inc=1)
        self.wait("vector", "s")
        self.vop(lambda v: v.reciprocal(a0, tt[:, 0:16]))
        self.vop(lambda v: v.tensor_tensor(b0, mu, a0, OP.mult))
        self.vop(lambda v: v.tensor_scalar_mul(b0, b0, -1.0))
        self.vop(lambda v: v.tensor_scalar_add(tt[:, 0:16], var, EPS))
        self.vop(lambda v: v.reciprocal(tt[:, 0:16], tt[:, 0:16]))
        self.vop(lambda v: v.tensor_tensor(w, var, tt[:, 0:16], OP.mult))
        self.vop(lambda v: v.tensor_reduce(tt[:, 0:1], w, AX.X, OP.add))
        self.vop(lambda v: v.tensor_copy(self.GS[:, self.GSO["sp"] + 8:self.GSO["sp"] + 9],
                                         tt[:, 0:1]), inc=1)
        self.wait("tensor", "v")
        self.peop(lambda t: t.matmul(self.PS[4][0:1, 16:17], self.ONE[:, 0:1],
                                     self.GS[:, self.GSO["sp"] + 8:self.GSO["sp"] + 9],
                                     start=True, stop=True), inc=1)
        self.wait("vector", "pe")
        rbt = tt[0:1, 4:5]
        self.vop(lambda v, rbt=rbt: v.tensor_copy(rbt, self.PS[4][0:1, 16:17]))
        self.vop(lambda v, rbt=rbt: v.tensor_scalar_mul(rbt, rbt, 1.0 / L), inc=1)
        self.wait("scalar", "v")
        self.sop(lambda s, rbt=rbt: s.activation(rbt, rbt, AF.Sqrt,
                 bias=g[0:1, GO["eps"]:GO["eps"] + 1]), inc=1)
        self.wait("vector", "s")
        self.vop(lambda v, rbt=rbt: v.reciprocal(rbt, rbt), inc=1)
        self.wait("tensor", "v")
        self.peop(lambda t: t.transpose(self.PT32[0:32, 0:128], s32[:, T + 32:T + 64],
                                        self.IDN32[:]), inc=1)
        self.wait("vector", "pe")
        rowst = s32[0:32, self.TO["rowst"]:self.TO["rowst"] + 128]
        self.vop(lambda v, rowst=rowst: v.tensor_copy(rowst, self.PT32[0:32, 0:128]), inc=1)
        self.wait("sync", "v")
        self.store(self.row("a0", 0, L), rowst[0:16, :], nm="str")
        self.store(self.row("b0", 0, L), rowst[16:32, :], nm="str")
        self.store(self.row("rb", 0, 1), tt[0:1, 4:5], nm="str")
        self.wait("gpsimd", "str")
        self.dma("gpsimd", "g0", g[:, GO["rb"]:GO["rb"] + 1],
                 self.row("rb", 0, 1).partition_broadcast(128))
        self.wait("vector", "g0")
        self.vop(lambda v: v.tensor_tensor(g[:, GO["rb"] + 1:GO["rb"] + 2],
                                           g[:, GO["rb"]:GO["rb"] + 1],
                                           g[:, GO["rb"]:GO["rb"] + 1], OP.mult), inc=1)
        self.tap("a0row", self.row("a0", 0, L), (1, L))
        self.tap("b0row", self.row("b0", 0, L), (1, L))
        self.tap("rb", g[:, GO["rb"]:GO["rb"] + 2], (128, 2))

    def p1b(self, b):
        g = self.G32
        GO = self.GO
        S = self.S
        x2 = S[:, 0:4096]
        xz = [S[:, 4096 + i * 4096: 4096 + (i + 1) * 4096] for i in range(2)]
        bc = [S[:, 12288 + i * 512: 12288 + (i + 1) * 512] for i in range(4)]
        w_inT = S[:, 14336:22528]
        cring = [S[:, 22528 + i * 3072: 22528 + (i + 1) * 3072] for i in range(2)]
        self.wait("sync", "sts0")
        self.wait("sync", "sts1")
        for ft in range(NFT):
            self.vop(lambda v, ft=ft: v.memset(self.u_ft(ft)[:, 0:1], 0.0))
            self.vop(lambda v, ft=ft: v.memset(self.u_ft(ft)[:, L + 1:L + 2], 0.0))
        self.dma("sync", "lw", w_inT, self.ws_in)
        w_inT_ld = self.cnt["lw"]
        for c in range(NCH1):
            buf = xz[c % 2]
            lsem = "l%d" % (c % 2)
            gsem = "g%d" % (c % 2)
            self.mwait("sync", "p1b_buf%d" % (c % 2))
            for ft in range(NFT):
                self.dma("sync", lsem, buf[:, ft * CH1:(ft + 1) * CH1],
                         self.x16[b, c * CH1:(c + 1) * CH1, ft * 128:(ft + 1) * 128],
                         transpose=True)
            ba = bc[2 * (c % 2)]
            bb2 = bc[2 * (c % 2) + 1]
            self.mwait("gpsimd", "p1b_bc%d" % (c % 2))
            self.dma("gpsimd", gsem, ba, self.row("a0", c * CH1, (c + 1) * CH1).partition_broadcast(128))
            self.dma("gpsimd", gsem, bb2, self.row("b0", c * CH1, (c + 1) * CH1).partition_broadcast(128))
            self.wait("vector", lsem)
            self.wait("vector", gsem)
            for ft in range(NFT):
                sl = slice(ft * CH1, (ft + 1) * CH1)
                self.vop(lambda v, buf=buf, sl=sl, ba=ba: v.tensor_tensor(
                    buf[:, sl], buf[:, sl], ba, OP.mult))
                self.vop(lambda v, buf=buf, sl=sl, bb2=bb2: v.tensor_tensor(
                    buf[:, sl], buf[:, sl], bb2, OP.add))
            self.mark("p1b_bc%d" % (c % 2), "v")
            zdone = self.cnt["v"]
            self.wait("tensor", "v", zdone)
            self.wait("tensor", "lw", w_inT_ld)
            for mt in range(NFT):
                ps = self.PS[mt % 2]
                self.mwait("tensor", "p1b_ps%d" % (mt % 2))
                for kt in range(NFT):
                    self.peop(lambda t, mt=mt, kt=kt, buf=buf, ps=ps: t.matmul(
                        ps[:, 0:CH1], w_inT[:, (mt * 8 + kt) * 128:(mt * 8 + kt + 1) * 128],
                        buf[:, kt * CH1:(kt + 1) * CH1],
                        start=(kt == 0), stop=(kt == NFT - 1)))
                self.wait("scalar", "pe")
                self.sop(lambda s, mt=mt, c=c, ps=ps: s.copy(
                    self.u_ft(mt)[:, 1 + c * CH1: 1 + (c + 1) * CH1], ps[:, 0:CH1]), inc=1)
                self.mark("p1b_ps%d" % (mt % 2), "s")
            self.mark("p1b_buf%d" % (c % 2), "pe")
        u_done = self.cnt["s"]
        for mt in range(NFT):
            ring = cring[mt % 2]
            lsem = "l%d" % (2 + mt % 2)
            self.mwait("sync", "p1b_ring%d" % (mt % 2))
            self.dma("sync", lsem, ring, self.ws_conv[mt])
            self.wait("tensor", lsem)
            if mt == 0:
                self.wait("tensor", "s", u_done)
                self.wait("tensor", "v")
            for c in range(NCH1):
                ps = self.PS[2 + (c % 2)]
                self.mwait("tensor", "p1b_cps%d" % (c % 2))
                for k in range(3):
                    for kt in range(NFT):
                        off = 1 + c * CH1 + (k - 1)
                        self.peop(lambda t, k=k, kt=kt, off=off, ps=ps, ring=ring: t.matmul(
                            ps[:, 0:CH1], ring[:, (k * 8 + kt) * 128:(k * 8 + kt + 1) * 128],
                            self.u_ft(kt)[:, off:off + CH1],
                            start=(k == 0 and kt == 0), stop=(k == 2 and kt == NFT - 1)))
                self.wait("scalar", "pe")
                self.mwait("scalar", "p1b_sq%d" % (c % 2))
                self.sop(lambda s, mt=mt, c=c, ps=ps: s.activation(
                    self.vu_ft(mt)[:, c * CH1:(c + 1) * CH1], ps[:, 0:CH1], AF.Identity,
                    accum_out=g[:, GO["SvU"] + mt * 4 + c:GO["SvU"] + mt * 4 + c + 1]))
                self.sop(lambda s, mt=mt, c=c, ps=ps: s.activation(
                    x2[:, (c % 2) * 512:(c % 2) * 512 + CH1], ps[:, 0:CH1], AF.Square,
                    accum_out=g[:, GO["SvU2"] + mt * 4 + c:GO["SvU2"] + mt * 4 + c + 1]), inc=1)
                self.mark("p1b_cps%d" % (c % 2), "s")
                self.mark("p1b_sq%d" % (c % 2), "s")
            self.mark("p1b_ring%d" % (mt % 2), "pe")
        self.tap("u", self.U[:, 0:2050], (128, 2050)) if False else None
        self.tap("u", self.u_ft(0)[:, 0:2050], (128, 2050))
        self.tap("vU", self.vu_ft(0), (128, L))
        self.tap("SvU", g[:, GO["SvU"]:GO["SvU"] + 64], (128, 64))

    @property
    def U(self):
        return self.W

    def chanstats(self, b):
        g = self.G32
        GO = self.GO
        self.wait("vector", "s")
        t = lambda i: g[:, GO["tmp"] + 8 * i:GO["tmp"] + 8 * (i + 1)]
        vu0, vuL, sv, sv2, S1, S2, mu, tA, tB = (t(i) for i in range(9))
        rb = g[:, GO["rb"]:GO["rb"] + 1]
        rb2 = g[:, GO["rb"] + 1:GO["rb"] + 2]
        cI = g[:, GO["cI"]:GO["cI"] + 8]
        c0 = g[:, GO["c0"]:GO["c0"] + 8]
        cL = g[:, GO["cL"]:GO["cL"] + 8]
        al = g[:, GO["alpha"]:GO["alpha"] + 8]
        be = g[:, GO["beta"]:GO["beta"] + 8]
        be0 = g[:, GO["beta0"]:GO["beta0"] + 8]
        beL = g[:, GO["betaL"]:GO["betaL"] + 8]
        for ft in range(NFT):
            self.vop(lambda v, ft=ft: v.tensor_copy(vu0[:, ft:ft + 1], self.vu_ft(ft)[:, 0:1]))
            self.vop(lambda v, ft=ft: v.tensor_copy(vuL[:, ft:ft + 1], self.vu_ft(ft)[:, L - 1:L]))
        sAll = g[:, GO["SvU"]:GO["SvU"] + 32].rearrange("p (m c) -> p m c", c=4)
        s2All = g[:, GO["SvU2"]:GO["SvU2"] + 32].rearrange("p (m c) -> p m c", c=4)
        self.vop(lambda v: v.tensor_tensor(sv, sAll[:, :, 0], sAll[:, :, 1], OP.add))
        self.vop(lambda v: v.tensor_tensor(sv, sv, sAll[:, :, 2], OP.add))
        self.vop(lambda v: v.tensor_tensor(sv, sv, sAll[:, :, 3], OP.add))
        self.vop(lambda v: v.tensor_tensor(sv2, s2All[:, :, 0], s2All[:, :, 1], OP.add))
        self.vop(lambda v: v.tensor_tensor(sv2, sv2, s2All[:, :, 2], OP.add))
        self.vop(lambda v: v.tensor_tensor(sv2, sv2, s2All[:, :, 3], OP.add))
        self.vop(lambda v: v.tensor_scalar_mul(tA, cI, float(L - 2)))
        self.vop(lambda v: v.scalar_tensor_tensor(S1, sv, rb, tA, OP.mult, OP.add))
        self.vop(lambda v: v.tensor_tensor(S1, S1, c0, OP.add))
        self.vop(lambda v: v.tensor_tensor(S1, S1, cL, OP.add))
        self.vop(lambda v: v.tensor_tensor(tA, sv, vu0, OP.subtract))
        self.vop(lambda v: v.tensor_tensor(tA, tA, vuL, OP.subtract))
        self.vop(lambda v: v.tensor_tensor(tA, tA, cI, OP.mult))
        self.vop(lambda v: v.tensor_tensor(tB, c0, vu0, OP.mult))
        self.vop(lambda v: v.tensor_tensor(tA, tA, tB, OP.add))
        self.vop(lambda v: v.tensor_tensor(tB, cL, vuL, OP.mult))
        self.vop(lambda v: v.tensor_tensor(tA, tA, tB, OP.add))
        self.vop(lambda v: v.tensor_scalar_mul(tA, tA, 2.0))
        self.vop(lambda v: v.tensor_scalar(tA, tA, rb, None, OP.mult))
        self.vop(lambda v: v.scalar_tensor_tensor(S2, sv2, rb2, tA, OP.mult, OP.add))
        self.vop(lambda v: v.tensor_tensor(tB, cI, cI, OP.mult))
        self.vop(lambda v: v.tensor_scalar_mul(tB, tB, float(L - 2)))
        self.vop(lambda v: v.tensor_tensor(S2, S2, tB, OP.add))
        self.vop(lambda v: v.tensor_tensor(tB, c0, c0, OP.mult))
        self.vop(lambda v: v.tensor_tensor(S2, S2, tB, OP.add))
        self.vop(lambda v: v.tensor_tensor(tB, cL, cL, OP.mult))
        self.vop(lambda v: v.tensor_tensor(S2, S2, tB, OP.add))
        self.vop(lambda v: v.tensor_scalar_mul(mu, S1, 1.0 / L))
        self.vop(lambda v: v.tensor_scalar_mul(tB, S2, 1.0 / L))
        self.vop(lambda v: v.tensor_tensor(tA, mu, mu, OP.mult))
        self.vop(lambda v: v.tensor_tensor(tB, tB, tA, OP.subtract), inc=1)
        self.wait("scalar", "v")
        self.sop(lambda s: s.activation(tB, tB, AF.Sqrt, bias=g[:, GO["eps"]:GO["eps"] + 1]), inc=1)
        self.wait("vector", "s")
        self.vop(lambda v: v.reciprocal(tA, tB))
        self.vop(lambda v: v.tensor_scalar(al, tA, rb, None, OP.mult))
        self.vop(lambda v: v.tensor_tensor(tB, cI, mu, OP.subtract))
        self.vop(lambda v: v.tensor_tensor(be, tB, tA, OP.mult))
        self.vop(lambda v: v.tensor_tensor(tB, c0, mu, OP.subtract))
        self.vop(lambda v: v.tensor_tensor(be0, tB, tA, OP.mult))
        self.vop(lambda v: v.tensor_tensor(tB, cL, mu, OP.subtract))
        self.vop(lambda v: v.tensor_tensor(beL, tB, tA, OP.mult), inc=1)
        self.tap("alpha", al, (128, 8))
        self.tap("beta", be, (128, 8))

    def p2(self, b):
        g = self.G32
        GO = self.GO
        S = self.S
        s32 = self.S32
        GN = self.TO["gn"]
        O = {"h": 0, "dlt": 8 * CH2, "dH": 16 * CH2, "y": 24 * CH2, "ab0": 32 * CH2,
             "ab1": 33 * CH2, "bx": 34 * CH2, "so": 35 * CH2, "tm": 36 * CH2,
             "bc": 37 * CH2, "wout": 41 * CH2, "evt": 45 * CH2}
        h_t = lambda ft: S[:, O["h"] + ft * CH2: O["h"] + (ft + 1) * CH2]
        d_t = lambda ft: S[:, O["dlt"] + ft * CH2: O["dlt"] + (ft + 1) * CH2]
        dh_t = lambda ft: S[:, O["dH"] + ft * CH2: O["dH"] + (ft + 1) * CH2]
        y_t = lambda ft: S[:, O["y"] + ft * CH2: O["y"] + (ft + 1) * CH2]
        ab_ = [S[:, O["ab0"]:O["ab0"] + CH2], S[:, O["ab1"]:O["ab1"] + CH2]]
        bx_ = S[:, O["bx"]:O["bx"] + CH2]
        so_ = S[:, O["so"]:O["so"] + CH2]
        tm_ = S[:, O["tm"]:O["tm"] + CH2]
        bcp = [S[:, O["bc"] + i * CH2:O["bc"] + (i + 1) * CH2] for i in range(4)]
        wring = [S[:, O["wout"] + i * 1024: O["wout"] + (i + 1) * 1024] for i in range(2)]
        ev1 = S[:, O["evt"]:O["evt"] + 512]
        ev2 = S[:, O["evt"] + 512:O["evt"] + 1024]
        u2s = S[:, O["evt"] + 1024:O["evt"] + 1536]
        gsq = S[:, O["ab0"]:O["ab0"] + CH2]
        h_cnt = {}
        h_pe_free = {0: 0}

        def emit_h(ch2):
            self.wait("scalar", "pe", h_pe_free[ch2])
            self.mwait("scalar", "p2_hfree")
            for ft in range(NFT):
                self.sop(lambda s, ft=ft, ch2=ch2: s.activation(
                    h_t(ft), self.vu_ft(ft)[:, ch2 * CH2:(ch2 + 1) * CH2], AF.Silu,
                    scale=g[:, GO["alpha"] + ft:GO["alpha"] + ft + 1],
                    bias=g[:, GO["beta"] + ft:GO["beta"] + ft + 1]))
                if ch2 == 0:
                    self.sop(lambda s, ft=ft: s.activation(
                        h_t(ft)[:, 0:1], self.vu_ft(ft)[:, 0:1], AF.Silu,
                        scale=g[:, GO["alpha"] + ft:GO["alpha"] + ft + 1],
                        bias=g[:, GO["beta0"] + ft:GO["beta0"] + ft + 1]))
                if ch2 == NCH2 - 1:
                    self.sop(lambda s, ft=ft: s.activation(
                        h_t(ft)[:, CH2 - 1:CH2], self.vu_ft(ft)[:, L - 1:L], AF.Silu,
                        scale=g[:, GO["alpha"] + ft:GO["alpha"] + ft + 1],
                        bias=g[:, GO["betaL"] + ft:GO["betaL"] + ft + 1]))
                elif ft == NFT - 1:
                    self.sop(lambda s: s.copy(ev1[:, 0:1], h_t(0)[:, 0:1]), inc=1)
            h_cnt[ch2] = self.cnt["s"]

        def emit_bmd(ch2):
            self.wait("tensor", "s", h_cnt[ch2])
            stg = s32[0:NST, self.TO["sum32"]:self.TO["sum32"] + 512]
            for (gso, bof, ro) in (("wBT", GO["bB"], "Bm"), ("wCT", GO["bC"], "Cm")):
                for gg in range(CH2 // 512):
                    self.mwait("tensor", "p2_ps1")
                    for kt in range(NFT):
                        self.peop(lambda t, gso=gso, kt=kt, gg=gg, ch2=ch2: t.matmul(
                            self.PS[1][0:NST, 0:512],
                            self.GS[:, self.GSO[gso] + kt * NST: self.GSO[gso] + (kt + 1) * NST],
                            h_t(kt)[:, gg * 512:(gg + 1) * 512],
                            start=(kt == 0), stop=(kt == NFT - 1)))
                    self.wait("vector", "pe")
                    self.mwait("vector", "p2_stg")
                    self.vop(lambda v, bof=bof, stg=stg: v.tensor_scalar(
                        stg, self.PS[1][0:NST, 0:512], g[0:NST, bof:bof + 1], None, OP.add),
                        inc=1)
                    self.mark("p2_ps1", "v")
                    self.wait("sync", "v")
                    ro_ap = self.rows_t[self.RO[ro]:self.RO[ro] + NST * L].rearrange(
                        "(n l) -> n l", n=NST)[:, ch2 * CH2 + gg * 512: ch2 * CH2 + (gg + 1) * 512]
                    self.store(ro_ap, stg, nm="str")
                    self.mark("p2_stg", "str")
            for gg in range(CH2 // 512):
                self.mwait("tensor", "p2_ps1")
                for kt in range(NFT):
                    self.peop(lambda t, kt=kt, gg=gg: t.matmul(
                        self.PS[1][0:NDLT, 0:512],
                        self.GS[:, self.GSO["wd1T"] + kt * NDLT: self.GSO["wd1T"] + (kt + 1) * NDLT],
                        h_t(kt)[:, gg * 512:(gg + 1) * 512],
                        start=(kt == 0), stop=(kt == NFT - 1)))
                self.wait("vector", "pe")
                self.mwait("vector", "p2_u2s")
                self.vop(lambda v: v.tensor_scalar(
                    u2s[0:NDLT, :], self.PS[1][0:NDLT, 0:512],
                    g[0:NDLT, GO["b_d1"]:GO["b_d1"] + 1], None, OP.add), inc=1)
                self.mark("p2_ps1", "v")
                u2done = self.cnt["v"]
                self.wait("tensor", "v", u2done)
                for mt in range(NFT):
                    ps = self.PS[0]
                    self.mwait("tensor", "p2_dpsa")
                    self.peop(lambda t, mt=mt, ps=ps: t.matmul(
                        ps[:, 0:512],
                        self.GS[0:NDLT, self.GSO["wd2T"] + mt * 128: self.GSO["wd2T"] + (mt + 1) * 128],
                        u2s[0:NDLT, 0:512], start=True, stop=True), inc=1)
                    if mt == NFT - 1:
                        self.mark("p2_u2s", "pe")
                    self.wait("scalar", "pe")
                    self.mwait("scalar", "p2_dev")
                    self.sop(lambda s, mt=mt, ps=ps: s.activation(
                        ev1, ps[:, 0:512], AF.Abs,
                        bias=g[:, GO["b_d2"] + mt:GO["b_d2"] + mt + 1]))
                    self.sop(lambda s: s.activation(ev2, ev1, AF.Exp, scale=-1.0))
                    self.sop(lambda s: s.activation(ev2, ev2, AF.Ln, bias=1.0), inc=1)
                    self.wait("vector", "s")
                    self.vop(lambda v, mt=mt, ps=ps: v.tensor_scalar(
                        ev1, ps[:, 0:512], g[:, GO["b_d2"] + mt:GO["b_d2"] + mt + 1], 0.0,
                        OP.add, OP.max))
                    self.vop(lambda v, mt=mt, gg=gg: v.tensor_tensor(
                        d_t(mt)[:, gg * 512:(gg + 1) * 512], ev1, ev2, OP.add), inc=1)
                    self.mark("p2_dpsa", "v")
                    self.mark("p2_dev", "v")
            h_pe_free[ch2 + 1] = self.cnt["pe"]
        emit_h(0)
        emit_bmd(0)
        for ch in range(NCH2):
            self.wait("vector", "pe")
            self.wait("scalar", "pe")
            self.wait("vector", "s", h_cnt[ch])
            for ft in range(NFT):
                self.vop(lambda v, ft=ft: v.tensor_tensor(dh_t(ft), d_t(ft), h_t(ft), OP.mult))
                self.vop(lambda v, ft=ft: v.tensor_scalar(
                    y_t(ft), h_t(ft), g[:, GO["D"] + ft:GO["D"] + ft + 1], None, OP.mult))
            dh_done = self.cnt["v"]
            self.mark("p2_hfree", "v")
            self.mwait("scalar", "p2_evdone")
            self.mwait("vector", "p2_evdone")
            self.wait("gpsimd", "str")
            self.wait("scalar", "v", dh_done)
            NPV = 12
            SO1, TMPP, YPO = 28672, 29184, 24576
            so_b = [so_, S[:, SO1:SO1 + CH2]]
            tm_p = S[:, TMPP:TMPP + CH2]
            yp_t = lambda ft: S[:, YPO + ft * CH2: YPO + (ft + 1) * CH2]
            bcp6 = bcp + [S[:, SO1 + 1024 + i * CH2: SO1 + 1024 + (i + 1) * CH2]
                          for i in range(2)]
            bc_cnt = {}
            def issue_bc(n):
                bmb = bcp6[2 * (n % 3)]
                cmb = bcp6[2 * (n % 3) + 1]
                gsem = "g%d" % (n % 2)
                self.mwait("gpsimd", "p2_bcv%d" % (n % 3))
                self.mwait("gpsimd", "p2_bcp%d" % (n % 3))
                self.dma("gpsimd", gsem, bmb, self.rows_t[
                    self.RO["Bm"] + n * L + ch * CH2: self.RO["Bm"] + n * L + (ch + 1) * CH2]
                    .unsqueeze(0).partition_broadcast(128))
                self.dma("gpsimd", gsem, cmb, self.rows_t[
                    self.RO["Cm"] + n * L + ch * CH2: self.RO["Cm"] + n * L + (ch + 1) * CH2]
                    .unsqueeze(0).partition_broadcast(128))
                bc_cnt[n] = self.cnt[gsem]
            self.mwait("gpsimd", "p2_ypfree")
            issue_bc(0)
            issue_bc(1)
            it = 0
            for n in range(NST):
                bmb = bcp6[2 * (n % 3)]
                cmb = bcp6[2 * (n % 3) + 1]
                gsem = "g%d" % (n % 2)
                if n + 2 < NST:
                    issue_bc(n + 2)
                self.wait("vector", gsem, bc_cnt[n])
                on_pool = n >= NPV
                if on_pool:
                    self.wait("gpsimd", gsem, bc_cnt[n])
                for ft in range(NFT):
                    ab = ab_[it % 2]
                    so_c = so_b[it % 2]
                    self.mwait("scalar", "p2_ab%d" % (it % 2))
                    self.sop(lambda s, ab=ab, ft=ft, n=n: s.activation(
                        ab, d_t(ft), AF.Exp,
                        scale=g[:, GO["negA"] + ft * NST + n: GO["negA"] + ft * NST + n + 1]),
                        inc=1)
                    sdone = self.cnt["s"]
                    self.wait("vector", "s", sdone)
                    self.mwait("vector", "p2_so%d" % (it % 2))
                    self.vop(lambda v, ft=ft, bmb=bmb: v.tensor_tensor(bx_, dh_t(ft), bmb, OP.mult))
                    ini = 0.0 if ch == 0 else self.CAR[:, n * NFT + ft:n * NFT + ft + 1]
                    self.vop(lambda v, ab=ab, ini=ini, so_c=so_c: v.tensor_tensor_scan(
                        so_c, ab, bx_, ini, OP.mult, OP.add), inc=1)
                    self.mark("p2_ab%d" % (it % 2), "v")
                    tts_v = self.cnt["v"]
                    if ch < NCH2 - 1:
                        self.vop(lambda v, n=n, ft=ft, so_c=so_c: v.tensor_copy(
                            self.CAR[:, n * NFT + ft:n * NFT + ft + 1], so_c[:, CH2 - 1:CH2]))
                    if on_pool:
                        self.wait("gpsimd", "v", tts_v)
                        if n == NPV:
                            self.gop(lambda p, ft=ft, so_c=so_c, cmb=cmb: p.tensor_tensor(
                                yp_t(ft), so_c, cmb, OP.mult))
                        else:
                            self.gop(lambda p, so_c=so_c, cmb=cmb: p.tensor_tensor(
                                tm_p, so_c, cmb, OP.mult))
                            self.gop(lambda p, ft=ft: p.tensor_tensor(
                                yp_t(ft), yp_t(ft), tm_p, OP.add))
                        self.mark("p2_so%d" % (it % 2), "p")
                        if ft == NFT - 1:
                            self.mark("p2_bcp%d" % (n % 3), "p")
                    else:
                        self.vop(lambda v, cmb=cmb, so_c=so_c: v.tensor_tensor(
                            tm_, so_c, cmb, OP.mult))
                        self.vop(lambda v, ft=ft: v.tensor_tensor(y_t(ft), y_t(ft), tm_, OP.add))
                    it += 1
                self.mark("p2_bcv%d" % (n % 3), "v")
                if not on_pool:
                    self.mark("p2_bcp%d" % (n % 3), "p")
            if NPV < NST:
                pool_done = self.cnt["p"]
                self.wait("vector", "p", pool_done)
                for ft in range(NFT):
                    self.vop(lambda v, ft=ft: v.tensor_tensor(y_t(ft), y_t(ft), yp_t(ft), OP.add))
            merge_done = self.cnt["v"]
            if ch + 1 < NCH2:
                emit_h(ch + 1)
            for ft in range(NFT):
                self.sop(lambda s, ft=ft, ch=ch: s.activation(
                    d_t(ft), self.u_ft(ft)[:, 1 + ch * CH2: 1 + (ch + 1) * CH2], AF.Silu,
                    scale=g[:, GO["rb"]:GO["rb"] + 1],
                    bias=g[:, GO["b_in"] + ft:GO["b_in"] + ft + 1]))
            self.wait("vector", "s")
            for ft in range(NFT):
                self.vop(lambda v, ft=ft: v.tensor_tensor(dh_t(ft), y_t(ft), d_t(ft), OP.mult))
            g_done = self.cnt["v"]
            g_t = dh_t
            self.wait("tensor", "v", g_done)
            self.wait("scalar", "v", g_done)
            ncol = CH2 // 128
            for tb in range(ncol):
                for ft in range(NFT):
                    self.peop(lambda t, ft=ft, tb=tb: t.matmul(
                        self.PS[4][:, tb:tb + 1], g_t(ft)[:, tb * 128:(tb + 1) * 128],
                        self.ONE[:, 0:1], start=(ft == 0), stop=(ft == NFT - 1)))
            for ft in range(NFT):
                self.mwait("scalar", "p2_gsq")
                self.sop(lambda s, ft=ft: s.activation(gsq, g_t(ft), AF.Square), inc=1)
                self.wait("tensor", "s")
                for tb in range(ncol):
                    self.peop(lambda t, ft=ft, tb=tb, ncol=ncol: t.matmul(
                        self.PS[5][:, ft * ncol + tb: ft * ncol + tb + 1],
                        gsq[:, tb * 128:(tb + 1) * 128],
                        self.ONE[:, 0:1], start=True, stop=True))
                self.mark("p2_gsq", "pe")
            self.wait("vector", "pe")
            sg = s32[:, GN:GN + ncol]
            sgg = s32[:, GN + 8:GN + 8 + ncol]
            mug = s32[:, GN + 16:GN + 16 + ncol]
            rsg = s32[:, GN + 24:GN + 24 + ncol]
            mrs = s32[:, GN + 32:GN + 32 + ncol]
            tg = s32[:, GN + 40:GN + 40 + ncol]
            self.vop(lambda v: v.tensor_copy(sg, self.PS[4][:, 0:ncol]))
            self.vop(lambda v: v.tensor_copy(sgg, self.PS[5][:, 0:ncol]))
            for ft in range(1, NFT):
                self.vop(lambda v, ft=ft, ncol=ncol: v.tensor_tensor(
                    sgg, sgg, self.PS[5][:, ft * ncol:(ft + 1) * ncol], OP.add))
            self.vop(lambda v: v.tensor_copy(sg, sg), inc=1)
            self.wait("tensor", "v")
            self.vop(lambda v: v.tensor_scalar_mul(mug, sg, 1.0 / FD))
            self.vop(lambda v: v.tensor_scalar_mul(tg, sgg, 1.0 / FD))
            self.vop(lambda v: v.tensor_tensor(rsg, mug, mug, OP.mult))
            self.vop(lambda v: v.tensor_tensor(tg, tg, rsg, OP.subtract), inc=1)
            self.wait("scalar", "v")
            self.sop(lambda s: s.activation(tg, tg, AF.Sqrt, bias=g[:, GO["eps"]:GO["eps"] + 1]), inc=1)
            self.wait("vector", "s")
            self.vop(lambda v: v.reciprocal(rsg, tg))
            self.vop(lambda v: v.tensor_tensor(mrs, mug, rsg, OP.mult), inc=1)
            self.wait("tensor", "v")
            self.peop(lambda t: t.transpose(self.PT32[0:16, 0:128], s32[:, GN + 24:GN + 40],
                                            self.IDN32[:]), inc=1)
            self.wait("vector", "pe")
            rowst = s32[0:16, self.TO["rowst"]:self.TO["rowst"] + 128]
            self.vop(lambda v, rowst=rowst: v.tensor_copy(rowst, self.PT32[0:16, 0:128]), inc=1)
            self.wait("sync", "v")
            self.store(self.row("rsg", ch * CH2, (ch + 1) * CH2), rowst[0:ncol, :], nm="str")
            self.store(self.row("mrsg", ch * CH2, (ch + 1) * CH2), rowst[8:8 + ncol, :], nm="str")
            rows_st = self.cnt["str"]
            if ch + 1 < NCH2:
                emit_bmd(ch + 1)
            self.wait("gpsimd", "str", rows_st)
            self.mwait("gpsimd", "p2_bc0")
            self.mwait("gpsimd", "p2_bc1")
            rsb = self.SF[:, 0:CH2]
            mrb = self.SF[:, CH2:2 * CH2]
            a1b = self.SF[:, 2 * CH2:3 * CH2]
            b0b = self.SF[:, 3 * CH2:4 * CH2]
            self.dma("gpsimd", "g0", rsb, self.row("rsg", ch * CH2, (ch + 1) * CH2).partition_broadcast(128))
            self.dma("gpsimd", "g0", mrb, self.row("mrsg", ch * CH2, (ch + 1) * CH2).partition_broadcast(128))
            self.dma("gpsimd", "g0", a1b, self.row("a0", ch * CH2, (ch + 1) * CH2).partition_broadcast(128))
            self.dma("gpsimd", "g0", b0b, self.row("b0", ch * CH2, (ch + 1) * CH2).partition_broadcast(128))
            self.wait("vector", "g0")
            self.vop(lambda v, a1b=a1b: v.tensor_scalar_add(a1b, a1b, 1.0), inc=1)
            self.wait("sync", "v", merge_done)
            for ft in range(NFT):
                self.dma("sync", "l2", yp_t(ft),
                         self.x16[b, ch * CH2:(ch + 1) * CH2, ft * 128:(ft + 1) * 128],
                         transpose=True)
            xres_ld = self.cnt["l2"]
            x_t = yp_t
            for mt in range(NFT):
                ring = wring[mt % 2]
                lsem = "l%d" % (mt % 2)
                self.mwait("sync", "p2_wr%d" % (mt % 2))
                self.dma("sync", lsem, ring, self.ws_out[mt])
                self.wait("tensor", lsem)
                for gg in range(CH2 // 512):
                    ps = self.PS[2 + (mt % 2)]
                    self.mwait("tensor", "p2_wps%d" % (mt % 2))
                    for kt in range(NFT):
                        self.peop(lambda t, kt=kt, gg=gg, ring=ring, ps=ps: t.matmul(
                            ps[:, 0:512], ring[:, kt * 128:(kt + 1) * 128],
                            g_t(kt)[:, gg * 512:(gg + 1) * 512],
                            start=(kt == 0), stop=(kt == NFT - 1)))
                    if gg == CH2 // 512 - 1:
                        self.mark("p2_wr%d" % (mt % 2), "pe")
                    self.wait("vector", "pe")
                    if mt == 0 and gg == 0:
                        self.wait("vector", "l2", xres_ld)
                    gsl = slice(gg * 512, (gg + 1) * 512)
                    e132 = self.SF[:, 4 * CH2:4 * CH2 + 512]
                    e232 = self.SF[:, 4 * CH2 + 512:4 * CH2 + 1024]
                    self.mwait("vector", "p2_ev")
                    self.vop(lambda v, ps=ps, rsb=rsb, gsl=gsl, e132=e132: v.tensor_tensor(
                        e132, ps[:, 0:512], rsb[:, gsl], OP.mult))
                    self.vop(lambda v, mt=mt, mrb=mrb, gsl=gsl, e132=e132: v.scalar_tensor_tensor(
                        e132, mrb[:, gsl], g[:, GO["nwsum"] + mt:GO["nwsum"] + mt + 1], e132,
                        OP.mult, OP.add), inc=1)
                    self.mark("p2_wps%d" % (mt % 2), "v")
                    self.vop(lambda v, mt=mt, a1b=a1b, gsl=gsl, e232=e232: v.tensor_tensor(
                        e232, x_t(mt)[:, gsl], a1b[:, gsl], OP.mult))
                    self.vop(lambda v, b0b=b0b, gsl=gsl, e232=e232: v.tensor_tensor(
                        e232, e232, b0b[:, gsl], OP.add))
                    self.vop(lambda v, mt=mt, ch=ch, gsl=gsl, e132=e132, e232=e232:
                             v.scalar_tensor_tensor(
                        self.X1[:, mt * L + ch * CH2 + gsl.start: mt * L + ch * CH2 + gsl.stop],
                        e132, g[:, GO["b_out"] + mt:GO["b_out"] + mt + 1], e232, OP.add, OP.add),
                        inc=1)
                    self.mark("p2_ev", "v")
            self.mark("p2_ypfree", "v")
            self.mark("p2_bc0", "v")
            self.mark("p2_bc1", "v")
            self.mark("p2_evdone", "v")
            self.flush_all()
        self.tap("h", S[:, O["dH"]:O["dH"] + CH2], (128, CH2))
        self.tap("y", S[:, O["y"]:O["y"] + CH2], (128, CH2))
        self.tap("x1", self.X1[:, 0:L], (128, L))
        self.tap("x1full", self.X1[:], (128, NFT * L))

    def ffn(self, b):
        g = self.G32
        GO = self.GO
        S = self.S
        s32 = self.S32
        FF = self.TO["ff"]
        O = {"xq": 0, "gu": 4096, "wq": 20480, "tq": 28672, "amax": 30208,
             "ev": 30720, "sq": 31744}
        xq_t = lambda kt: S[:, O["xq"] + kt * TFC: O["xq"] + (kt + 1) * TFC]
        gu_t = lambda mt: S[:, O["gu"] + mt * TFC: O["gu"] + (mt + 1) * TFC]
        wq_ring = [S[:, O["wq"] + i * 1024: O["wq"] + (i + 1) * 1024] for i in range(8)]
        tq = [S[:, O["tq"] + i * 512: O["tq"] + (i + 1) * 512] for i in range(3)]
        amax = S[:, O["amax"]:O["amax"] + 512]
        ev1 = S[:, O["ev"]:O["ev"] + 512]
        ev2 = S[:, O["ev"] + 512:O["ev"] + 1024]
        sqb = S[:, O["sq"]:O["sq"] + 512]
        aq_b = [self.SF[:, 0:512], self.SF[:, 512:1024]]
        bq_b = [self.SF[:, 1024:1536], self.SF[:, 1536:2048]]
        cg_b = [self.SF[:, 2048:2560], self.SF[:, 2560:3072]]
        cu_b = [self.SF[:, 3072:3584], self.SF[:, 3584:4096]]
        s232 = self.SF[:, 4096:4608]
        ev32 = self.SF[:, 4608:5120]
        cdb32 = self.SF[:, 5120:5632]
        aq16 = S[:, 32256:32768]
        bq16 = S[:, 32768:33280]
        s2b16 = tq[0]
        sum32 = s32[:, self.TO["sum32"]:self.TO["sum32"] + 512]
        otm = [s32[:, self.TO["otm"] + i * 512: self.TO["otm"] + (i + 1) * 512] for i in range(2)]
        self.wait("sync", "sts0")
        self.wait("sync", "sts1")
        rowst = s32[0:16, self.TO["rowst"]:self.TO["rowst"] + 128]
        tt = s32[:, FF + 52:FF + 56]
        ab_cnt = {}
        xq_cnt = {}

        def emit_stats(tc2, vguard):
            x1s = lambda ft, tc2=tc2: self.X1[:, ft * L + tc2 * TFC: ft * L + (tc2 + 1) * TFC]
            self.wait("scalar", "v", vguard)
            self.mwait("scalar", "ff_gps0")
            self.mwait("scalar", "ff_gps1")
            for ft in range(NFT):
                self.mwait("scalar", "ff_sq")
                self.sop(lambda s, ft=ft, x1s=x1s: s.activation(sqb, x1s(ft), AF.Square), inc=1)
                self.wait("tensor", "s")
                for tb in range(4):
                    self.peop(lambda t, ft=ft, tb=tb, x1s=x1s: t.matmul(
                        self.PS[0][:, ft * 4 + tb: ft * 4 + tb + 1],
                        x1s(ft)[:, tb * 128:(tb + 1) * 128],
                        self.ONE[:, 0:1], start=True, stop=True))
                for tb in range(4):
                    self.peop(lambda t, ft=ft, tb=tb: t.matmul(
                        self.PS[1][:, ft * 4 + tb: ft * 4 + tb + 1],
                        sqb[:, tb * 128:(tb + 1) * 128],
                        self.ONE[:, 0:1], start=True, stop=True))
                self.mark("ff_sq", "pe")
            self.mwait("vector", "ff_ev")
            self.vop(lambda v, x1s=x1s: v.tensor_tensor(ev1, x1s(0), x1s(1), OP.max))
            self.vop(lambda v, x1s=x1s: v.tensor_tensor(ev2, x1s(0), x1s(1), OP.min))
            for ft in range(2, NFT):
                self.vop(lambda v, ft=ft, x1s=x1s: v.tensor_tensor(ev1, ev1, x1s(ft), OP.max))
                self.vop(lambda v, ft=ft, x1s=x1s: v.tensor_tensor(ev2, ev2, x1s(ft), OP.min))
            mmx_done = self.cnt["v"]
            self.wait("tensor", "v", mmx_done)
            for blk in range(4):
                self.peop(lambda t, blk=blk: t.transpose(
                    self.PT16[:, 0:128], ev1[:, blk * 128:(blk + 1) * 128], self.IDN[:]), inc=1)
                self.wait("vector", "pe")
                self.vop(lambda v, blk=blk: v.tensor_reduce(
                    s32[:, FF + 8 + blk:FF + 9 + blk], self.PT16[:, 0:128], AX.X, OP.max), inc=1)
                self.wait("tensor", "v")
                self.peop(lambda t, blk=blk: t.transpose(
                    self.PT16[:, 0:128], ev2[:, blk * 128:(blk + 1) * 128], self.IDN[:]), inc=1)
                self.wait("vector", "pe")
                self.vop(lambda v, blk=blk: v.tensor_reduce(
                    s32[:, FF + 12 + blk:FF + 13 + blk], self.PT16[:, 0:128], AX.X, OP.min),
                    inc=1)
                self.wait("tensor", "v")
            self.mark("ff_ev", "v")
            sx = s32[:, FF:FF + 4]
            sxx = s32[:, FF + 4:FF + 8]
            mx = s32[:, FF + 8:FF + 12]
            nmn = s32[:, FF + 12:FF + 16]
            mu = s32[:, FF + 16:FF + 20]
            var = s32[:, FF + 20:FF + 24]
            rst = s32[:, FF + 24:FF + 28]
            am = s32[:, FF + 28:FF + 32]
            s1 = s32[:, FF + 32:FF + 36]
            aq = s32[:, FF + 36:FF + 40]
            bq2 = s32[:, FF + 40:FF + 44]
            cgc = s32[:, FF + 44:FF + 48]
            cuc = s32[:, FF + 48:FF + 52]
            tt = s32[:, FF + 52:FF + 56]
            self.vop(lambda v: v.tensor_copy(sx, self.PS[0][:, 0:4]))
            self.vop(lambda v: v.tensor_copy(sxx, self.PS[1][:, 0:4]))
            for ft in range(1, NFT):
                self.vop(lambda v, ft=ft: v.tensor_tensor(
                    sx, sx, self.PS[0][:, ft * 4:(ft + 1) * 4], OP.add))
                self.vop(lambda v, ft=ft: v.tensor_tensor(
                    sxx, sxx, self.PS[1][:, ft * 4:(ft + 1) * 4], OP.add))
            self.vop(lambda v: v.tensor_copy(sx, sx), inc=1)
            self.mark("ff_gps0", "v")
            self.mark("ff_gps1", "v")
            self.wait("tensor", "v")
            self.vop(lambda v: v.tensor_scalar_mul(mu, sx, 1.0 / FD))
            self.vop(lambda v: v.tensor_scalar_mul(var, sxx, 1.0 / FD))
            self.vop(lambda v: v.tensor_tensor(tt, mu, mu, OP.mult))
            self.vop(lambda v: v.tensor_tensor(var, var, tt, OP.subtract), inc=1)
            self.wait("scalar", "v")
            self.sop(lambda s: s.activation(tt, var, AF.Sqrt, bias=g[:, GO["eps"]:GO["eps"] + 1]), inc=1)
            self.wait("vector", "s")
            self.vop(lambda v: v.reciprocal(rst, tt))
            self.vop(lambda v: v.tensor_tensor(am, mx, mu, OP.subtract))
            self.vop(lambda v: v.tensor_tensor(tt, mu, nmn, OP.subtract))
            self.vop(lambda v: v.tensor_tensor(am, am, tt, OP.max))
            self.vop(lambda v: v.tensor_tensor(am, am, rst, OP.mult))
            self.vop(lambda v: v.tensor_scalar_max(am, am, 1e-5))
            self.vop(lambda v: v.reciprocal(tt, am))
            self.vop(lambda v: v.tensor_scalar_mul(s1, tt, 127.0))
            self.vop(lambda v: v.tensor_tensor(aq, rst, s1, OP.mult))
            self.vop(lambda v: v.tensor_tensor(bq2, mu, aq, OP.mult))
            self.vop(lambda v: v.tensor_scalar(bq2, bq2, -1.0, 0.5, OP.mult, OP.add))
            self.vop(lambda v: v.reciprocal(tt, s1))
            self.vop(lambda v: v.tensor_scalar(cgc, tt, g[:, GO["sw"] + 3:GO["sw"] + 4], None, OP.mult))
            self.vop(lambda v: v.tensor_scalar(cuc, tt, g[:, GO["sw"] + 4:GO["sw"] + 5], None, OP.mult),
                     inc=1)
            self.wait("tensor", "v")
            self.peop(lambda t: t.transpose(self.PT32[0:16, 0:128], s32[:, FF + 36:FF + 52],
                                            self.IDN32[:]), inc=1)
            self.wait("vector", "pe")
            rowst = s32[0:16, self.TO["rowst"]:self.TO["rowst"] + 128]
            self.mwait("vector", "ff_rowst")
            self.vop(lambda v, rowst=rowst: v.tensor_copy(rowst, self.PT32[0:16, 0:128]), inc=1)
            self.wait("sync", "v")
            for (i, nm) in ((0, "aq"), (4, "bq2"), (8, "cg"), (12, "cu")):
                self.store(self.row(nm, tc2 * TFC, (tc2 + 1) * TFC), rowst[i:i + 4, :], nm="str")
            self.mark("ff_rowst", "str")
            self.wait("gpsimd", "str", self.cnt["str"])
            self.mwait("gpsimd", "ff_ab%d" % (tc2 % 2))
            for (bt, nm) in ((aq_b, "aq"), (bq_b, "bq2"), (cg_b, "cg"), (cu_b, "cu")):
                self.dma("gpsimd", "g0", bt[tc2 % 2],
                         self.row(nm, tc2 * TFC, (tc2 + 1) * TFC).partition_broadcast(128))
            ab_cnt[tc2] = self.cnt["g0"]

        def emit_xq(tc2):
            x1s = lambda ft, tc2=tc2: self.X1[:, ft * L + tc2 * TFC: ft * L + (tc2 + 1) * TFC]
            aqb, bqb = aq_b[tc2 % 2], bq_b[tc2 % 2]
            self.wait("vector", "g0", ab_cnt[tc2])
            self.vop(lambda v, aqb=aqb: v.tensor_copy(aq16, aqb))
            self.vop(lambda v, bqb=bqb: v.tensor_copy(bq16, bqb), inc=1)
            for ft in range(NFT):
                self.vop(lambda v, ft=ft, x1s=x1s: v.tensor_tensor(tq[2], x1s(ft), aq16, OP.mult))
                self.vop(lambda v: v.tensor_tensor(tq[2], tq[2], bq16, OP.add))
                self.vop(lambda v: v.tensor_copy(self.I16T[:, 0:512], tq[2]))
                self.vop(lambda v: v.tensor_copy(ev2, self.I16T[:, 0:512]))
                self.vop(lambda v: v.tensor_tensor(sqb, ev2, tq[2], OP.is_gt))
                self.vop(lambda v: v.scalar_tensor_tensor(ev1, ev2, 1.0, tq[2], OP.add, OP.is_le))
                self.vop(lambda v: v.tensor_tensor(ev2, ev2, sqb, OP.subtract))
                self.vop(lambda v, ft=ft: v.tensor_tensor(xq_t(ft), ev2, ev1, OP.add))
            xq_cnt[tc2] = self.cnt["v"]
        emit_stats(0, self.cnt["v"])
        emit_xq(0)
        for tc in range(NTFC):
            x1_t = lambda ft: self.X1[:, ft * L + tc * TFC: ft * L + (tc + 1) * TFC]
            self.wait("scalar", "v")
            self.wait("scalar", "pe")
            self.wait("vector", "pe")
            cgb, cub = cg_b[tc % 2], cu_b[tc % 2]
            if tc > 0:
                emit_xq(tc)
            self.wait("tensor", "v", xq_cnt[tc])
            ring_cnt = {}

            def issue_gu(mt):
                lsem = "l%d" % (mt % 4)
                self.mwait("sync", "ff_ring%d" % (mt % 4))
                self.dma("sync", lsem, wq_ring[(2 * mt) % 8], self.ws_g[mt])
                self.dma("sync", lsem, wq_ring[(2 * mt + 1) % 8], self.ws_u[mt])
                ring_cnt[mt] = self.cnt[lsem]
            issue_gu(0)
            issue_gu(1)
            issue_gu(2)
            for mt in range(NMT_I):
                rg = wq_ring[(2 * mt) % 8]
                ru = wq_ring[(2 * mt + 1) % 8]
                lsem = "l%d" % (mt % 4)
                if mt + 3 < NMT_I:
                    issue_gu(mt + 3)
                self.wait("tensor", lsem, ring_cnt[mt])
                psg = self.PS[mt % 2]
                psu = self.PS[2 + mt % 2]
                self.mwait("tensor", "ff_gps%d" % (mt % 2))
                for kt in range(NFT):
                    self.peop(lambda t, kt=kt, rg=rg, psg=psg: t.matmul(
                        psg[:, 0:512], rg[:, kt * 128:(kt + 1) * 128], xq_t(kt),
                        start=(kt == 0), stop=(kt == NFT - 1)))
                for kt in range(NFT):
                    self.peop(lambda t, kt=kt, ru=ru, psu=psu: t.matmul(
                        psu[:, 0:512], ru[:, kt * 128:(kt + 1) * 128], xq_t(kt),
                        start=(kt == 0), stop=(kt == NFT - 1)))
                self.mark("ff_ring%d" % (mt % 4), "pe")
                self.wait("vector", "pe")
                self.mwait("vector", "ff_gev")
                self.vop(lambda v, psg=psg, cgb=cgb: v.tensor_tensor(ev1, psg[:, 0:512], cgb, OP.mult),
                         inc=1)
                self.wait("scalar", "v")
                self.sop(lambda s: s.activation(ev2, ev1, AF.Silu), inc=1)
                self.vop(lambda v, psu=psu, cub=cub: v.tensor_tensor(sqb, psu[:, 0:512], cub, OP.mult),
                         inc=1)
                self.mark("ff_gps%d" % (mt % 2), "v")
                self.wait("vector", "s")
                self.vop(lambda v, mt=mt: v.tensor_tensor(gu_t(mt), ev2, sqb, OP.mult))
                amin = tq[1]
                if mt == 0:
                    self.vop(lambda v: v.tensor_copy(amax, gu_t(0)))
                    self.vop(lambda v, amin=amin: v.tensor_copy(amin, gu_t(0)), inc=1)
                else:
                    self.vop(lambda v, mt=mt: v.tensor_tensor(amax, amax, gu_t(mt), OP.max))
                    self.vop(lambda v, mt=mt, amin=amin: v.tensor_tensor(
                        amin, amin, gu_t(mt), OP.min), inc=1)
            self.vop(lambda v: v.tensor_scalar_mul(tq[1], tq[1], -1.0))
            self.vop(lambda v: v.tensor_tensor(amax, amax, tq[1], OP.max), inc=1)
            gu_done = self.cnt["v"]
            self.mark("ff_cgcu", "v")
            self.mark("ff_s2g", "v")
            self.mark("ff_ab%d" % (tc % 2), "v")
            if tc == NTFC - 1 and ("gupre" in self.taps):
                self.barrier()
                self.tap("gupre", S[:, O["gu"]:O["gu"] + TFC], (128, TFC))
                self.tap("gupre7", S[:, O["gu"] + 7 * TFC:O["gu"] + 8 * TFC], (128, TFC))
                self.barrier()
            self.wait("tensor", "v", gu_done)
            for blk in range(4):
                self.peop(lambda t, blk=blk: t.transpose(
                    self.PT16[:, 0:128], amax[:, blk * 128:(blk + 1) * 128], self.IDN[:]), inc=1)
                self.wait("vector", "pe")
                self.vop(lambda v, blk=blk: v.tensor_reduce(
                    s32[:, FF + 56 + blk:FF + 57 + blk], self.PT16[:, 0:128], AX.X, OP.max), inc=1)
                self.wait("tensor", "v")
            am2 = s32[:, FF + 56:FF + 60]
            s2c = s32[:, FF + 60:FF + 64]
            cdc = s32[:, FF + 64:FF + 68]
            self.vop(lambda v: v.tensor_scalar_max(am2, am2, 1e-5))
            self.vop(lambda v: v.reciprocal(tt, am2))
            self.vop(lambda v: v.tensor_scalar_mul(s2c, tt, 127.0))
            self.vop(lambda v: v.reciprocal(tt, s2c))
            self.vop(lambda v: v.tensor_scalar(cdc, tt, g[:, GO["sw"] + 5:GO["sw"] + 6], None, OP.mult),
                     inc=1)
            self.wait("tensor", "v")
            self.peop(lambda t: t.transpose(self.PT32[0:8, 0:128], s32[:, FF + 60:FF + 68],
                                            self.IDN32[:]), inc=1)
            self.wait("vector", "pe")
            self.mwait("vector", "ff_rowst")
            self.vop(lambda v, rowst=rowst: v.tensor_copy(rowst[0:8, :], self.PT32[0:8, 0:128]),
                     inc=1)
            self.wait("sync", "v")
            self.store(self.row("s2", tc * TFC, (tc + 1) * TFC), rowst[0:4, :], nm="str")
            self.store(self.row("cd", tc * TFC, (tc + 1) * TFC), rowst[4:8, :], nm="str")
            self.mark("ff_rowst", "str")
            r_st = self.cnt["str"]
            self.wait("gpsimd", "str", r_st)
            s2b = s232
            self.mwait("gpsimd", "ff_s2g")
            self.dma("gpsimd", "g0", s2b, self.row("s2", tc * TFC, (tc + 1) * TFC).partition_broadcast(128))
            self.wait("vector", "g0")
            cdb = cdb32
            self.mwait("gpsimd", "ff_cgcu")
            self.dma("gpsimd", "g1", cdb, self.row("cd", tc * TFC, (tc + 1) * TFC).partition_broadcast(128))
            for par in range(4):
                self.mwait("sync", "ff_ring%d" % par)
            self.mwait("sync", "ff_dr0")
            self.mwait("sync", "ff_dr1")
            for q in range(4):
                self.dma("sync", "l0", wq_ring[2 * q], self.ws_d[0, :, q * 1024:(q + 1) * 1024])
                self.dma("sync", "l1", wq_ring[2 * q + 1], self.ws_d[1, :, q * 1024:(q + 1) * 1024])
            self.wait("vector", "s")
            self.vop(lambda v, s2b=s2b: v.tensor_copy(s2b16, s2b), inc=1)
            gq_cnt = []
            for mt in range(NMT_I):
                self.vop(lambda v, mt=mt: v.tensor_tensor(tq[2], gu_t(mt), s2b16, OP.mult))
                self.vop(lambda v: v.tensor_single_scalar(tq[2], tq[2], 0.5, OP.add))
                self.vop(lambda v: v.tensor_copy(self.I16T[:, 0:512], tq[2]))
                self.vop(lambda v: v.tensor_copy(ev2, self.I16T[:, 0:512]))
                self.vop(lambda v: v.tensor_tensor(sqb, ev2, tq[2], OP.is_gt))
                self.vop(lambda v: v.scalar_tensor_tensor(ev1, ev2, 1.0, tq[2], OP.add, OP.is_le))
                self.vop(lambda v: v.tensor_tensor(ev2, ev2, sqb, OP.subtract))
                self.vop(lambda v, mt=mt: v.tensor_tensor(gu_t(mt), ev2, ev1, OP.add))
                gq_cnt.append(self.cnt["v"])
            gq_done = self.cnt["v"]
            gq_t = gu_t
            self.wait("vector", "g1")

            def epi(mtm, ps):
                self.wait("vector", "pe")
                self.mwait("vector", "ff_sum")
                self.vop(lambda v, ps=ps, cdb=cdb: v.tensor_tensor(
                    sum32, ps[:, 0:512], cdb, OP.mult), inc=1)
                self.mark("ff_dps%d" % (mtm % 2), "v")
                self.vop(lambda v, mtm=mtm, tc=tc: v.tensor_tensor(
                    sum32, sum32,
                    self.X1[:, mtm * L + tc * TFC: mtm * L + (tc + 1) * TFC], OP.add), inc=1)
                sum_done = self.cnt["v"]
                self.wait("tensor", "v", sum_done)
                for tb in range(4):
                    self.peop(lambda t, tb=tb: t.transpose(
                        self.PT32[:, tb * 128:(tb + 1) * 128], sum32[:, tb * 128:(tb + 1) * 128],
                        self.IDN32[:]))
                self.mark("ff_sum", "pe")
                self.wait("vector", "pe")
                ot = otm[mtm % 2]
                self.wait("vector", "sto0")
                self.wait("vector", "sto1")
                self.vop(lambda v, ot=ot: v.tensor_copy(ot, self.PT32[:, 0:512]), inc=1)
                self.wait("tensor", "v")
                self.wait("sync", "v")
                dst = self.out_h[b, tc * TFC:(tc + 1) * TFC, mtm * 128:(mtm + 1) * 128] \
                    .rearrange("(tb p) c -> p tb c", p=128)
                osrc = ot.rearrange("p (tb c) -> p tb c", c=128)
                self.store(dst, osrc, nm="sto%d" % (mtm % 2))
                self.mark("ff_ot%d" % (mtm % 2), "sto%d" % (mtm % 2))

            self.wait("tensor", "l0")
            self.wait("tensor", "l1")
            self.mwait("tensor", "ff_dps0")
            self.mwait("tensor", "ff_dps1")
            for kt in range(NMT_I):
                self.wait("tensor", "v", gq_cnt[kt])
                for w in range(2):
                    ring = wq_ring[2 * (kt // 8) + w]
                    self.peop(lambda t, kt=kt, ring=ring, w=w: t.matmul(
                        self.PS[4 + w][:, 0:512], ring[:, (kt % 8) * 128:((kt % 8) + 1) * 128],
                        gq_t(kt), start=(kt == 0), stop=(kt == NMT_I - 1)))
            self.mark("ff_dr0", "pe")
            self.mark("ff_dr1", "pe")

            def issue_dn(mtm):
                lsem = "l%d" % (mtm % 2)
                self.mwait("sync", "ff_dr%d" % (mtm % 2))
                for q in range(4):
                    self.dma("sync", lsem, wq_ring[2 * q + mtm % 2],
                             self.ws_d[mtm, :, q * 1024:(q + 1) * 1024])
                ring_cnt[("d", mtm)] = self.cnt[lsem]
            issue_dn(2)
            issue_dn(3)
            epi(0, self.PS[4])
            epi(1, self.PS[5])
            if tc + 1 < NTFC:
                emit_stats(tc + 1, gq_done)
            for mtm in range(2, NFT):
                lsem = "l%d" % (mtm % 2)
                self.wait("tensor", lsem, ring_cnt[("d", mtm)])
                ps = self.PS[4 + (mtm % 2)]
                self.mwait("tensor", "ff_dps%d" % (mtm % 2))
                for kt in range(NMT_I):
                    ring = wq_ring[2 * (kt // 8) + mtm % 2]
                    self.peop(lambda t, kt=kt, ring=ring, ps=ps: t.matmul(
                        ps[:, 0:512], ring[:, (kt % 8) * 128:((kt % 8) + 1) * 128], gq_t(kt),
                        start=(kt == 0), stop=(kt == NMT_I - 1)))
                self.mark("ff_dr%d" % (mtm % 2), "pe")
                if mtm + 2 < NFT:
                    issue_dn(mtm + 2)
                epi(mtm, ps)
            self.flush_all()
        self.tap("xq", S[:, O["xq"]:O["xq"] + TFC], (128, TFC))
        self.tap("xq7", S[:, O["xq"] + 7 * TFC:O["xq"] + 8 * TFC], (128, TFC))
        self.tap("m1", sqb, (128, TFC))
        self.tap("m2", tq[2], (128, TFC))
        self.tap("uev", ev32, (128, TFC))
        self.tap("fev", ev2, (128, TFC))
        self.tap("gu", S[:, O["gu"]:O["gu"] + TFC], (128, TFC))
        self.tap("amax", amax, (128, TFC))
        self.tap("s2row", self.row("s2", 0, L), (1, L))
        self.tap("cgrow", self.row("cg", 0, L), (1, L))

    def finish(self):
        for nm in ("sto0", "sto1", "str", "sts0", "sts1", "l0", "l1", "l2", "l3",
                   "lw", "g0", "g1"):
            self.wait("gpsimd", nm)
        self.flush_all()
        for cm in reversed(self._ctx):
            cm.__exit__(None, None, None)


_BLD_CACHE = {}


def build(upto="full", taps=()):
    key = (upto, tuple(sorted(taps)))
    if key not in _BLD_CACHE:
        _BLD_CACHE[key] = B(upto=upto, taps=taps)
    return _BLD_CACHE[key]


def make_in_maps(inputs):
    x = np.ascontiguousarray(inputs["L0_attn_x"], dtype=np.float32)
    wmap = {}
    for k, v in inputs.items():
        if k == "L0_attn_x":
            continue
        nm = k.replace("L0_attn_", "").replace("L0_ffn_", "")
        wmap[nm] = np.ascontiguousarray(v, dtype=np.float32)
    in_maps = []
    for i in range(8):
        m = {"x": x[2 * i:2 * i + 2]}
        m.update(wmap)
        in_maps.append(m)
    return in_maps


def kernel(**inputs):
    bld = build()
    in_maps = make_in_maps(inputs)
    res = run_bass_kernel_spmd(bld.nc, in_maps, core_ids=list(range(8)))
    return np.concatenate([r["out"] for r in res.results], axis=0)


# version 31
# speedup vs baseline: 1.0057x; 1.0057x over previous
import numpy as np
import concourse.bass as bass
import concourse.mybir as mybir
from concourse.bass_utils import run_bass_kernel_spmd

F32 = mybir.dt.float32
F16 = mybir.dt.float16
I16 = mybir.dt.int16
AF = mybir.ActivationFunctionType
OP = mybir.AluOpType
AX = mybir.AxisListType

NB = 2
L = 2048
FD = 1024
NFT = 8
NST = 16
NDLT = 64
INNER = 4096
NMT_I = 32
CH1 = 512
NCH1 = L // CH1
CH2 = 512
NCH2 = L // CH2
TFC = 512
NTFC = L // TFC
EPS = 1e-5
ENGS = ("tensor", "vector", "scalar", "gpsimd", "sync")


class B:
    def __init__(self, upto="full", taps=()):
        self.upto = upto
        self.taps = set(taps)
        self.tapd = {}
        self.nc = bass.Bass("TRN2", target_bir_lowering=False)
        self._ctx = []
        self.mk = {}
        self._pend = {}
        self._io()
        self._alloc()
        self._prog()

    def _io(self):
        nc = self.nc
        self.x_h = nc.dram_tensor("x", [NB, L, FD], F32, kind="ExternalInput").ap()
        shp = {"w_in": [FD, FD], "b_in": [FD], "conv_w": [FD, FD, 3], "conv_b": [FD],
               "A_log": [FD, NST], "D": [FD], "w_B": [NST, FD], "b_B": [NST],
               "w_C": [NST, FD], "b_C": [NST], "w_d1": [NDLT, FD], "b_d1": [NDLT],
               "w_d2": [FD, NDLT], "b_d2": [FD], "w_out": [FD, FD], "b_out": [FD],
               "w_gate": [INNER, FD], "w_up": [INNER, FD], "w_down": [FD, INNER]}
        self.w_h = {n: nc.dram_tensor(n, s, F32, kind="ExternalInput").ap() for n, s in shp.items()}
        self.out_h = nc.dram_tensor("out", [NB, L, FD], F32, kind="ExternalOutput").ap()
        self.x16 = nc.dram_tensor("x16_scr", [NB, L, FD], F16, kind="Internal").ap()
        self.ws_in = nc.dram_tensor("ws_in", [128, NFT * NFT * 128], F16, kind="Internal").ap()
        self.ws_conv = nc.dram_tensor("ws_conv", [NFT, 128, 3 * NFT * 128], F16, kind="Internal").ap()
        self.ws_out = nc.dram_tensor("ws_out", [NFT, 128, NFT * 128], F16, kind="Internal").ap()
        self.ws_g = nc.dram_tensor("ws_g", [NMT_I, 128, NFT * 128], F16, kind="Internal").ap()
        self.ws_u = nc.dram_tensor("ws_u", [NMT_I, 128, NFT * 128], F16, kind="Internal").ap()
        self.ws_d = nc.dram_tensor("ws_d", [NFT, 128, NMT_I * 128], F16, kind="Internal").ap()
        self.rows_t = nc.dram_tensor("rows_scr", [1 << 17], F32, kind="Internal").ap()
        self.RO = {}
        o = 0
        for nm, sz in (("a0", L), ("b0", L), ("Bm", NST * L), ("Cm", NST * L),
                       ("rsg", L), ("mrsg", L), ("aq", L), ("bq2", L), ("cg", L),
                       ("cu", L), ("s2", L), ("cd", L), ("rb", 8), ("sw", 8)):
            self.RO[nm] = o
            o += sz

    def row(self, nm, a, b2):
        return self.rows_t[self.RO[nm] + a: self.RO[nm] + b2].unsqueeze(0)

    def _alloc(self):
        nc = self.nc
        def sb(name, shape, dt):
            return nc.alloc_sbuf_tensor(name, shape, dt).ap()
        self.X1 = sb("x1", [128, NFT * L], F16)
        self.W = sb("wreg", [128, NFT * (L + 2) + NFT * L], F16)
        self.S = sb("sreg", [128, 33280], F16)
        self.SF = sb("sf32", [128, 5632], F32)
        self.S32 = sb("s32", [128, 2048], F32)
        self.G32 = sb("g32", [128, 512], F32)
        self.GS = sb("gsml", [128, 1920], F16)
        self.I16T = sb("i16", [128, 512], I16)
        self.CAR = sb("carry", [128, 128], F16)
        self.ONE = sb("ones", [128, 8], F16)
        self.IDN = sb("idn", [128, 128], F16)
        self.IDN32 = sb("idn32", [128, 128], F32)
        self.PS = [nc.alloc_psum_tensor("ps%d" % i, [128, 512], F32).ap() for i in range(6)]
        self.PT16 = nc.alloc_psum_tensor("pt16", [128, 512], F16).ap()
        self.PT32 = nc.alloc_psum_tensor("pt32", [128, 512], F32).ap()
        self.GO = {"negA": 0, "D": 128, "b_in": 136, "cb": 144, "b_d2": 152,
                   "b_out": 160, "cI": 168, "c0": 176, "cL": 184, "alpha": 192,
                   "beta": 200, "beta0": 208, "betaL": 216, "SvU": 224, "SvU2": 256,
                   "nwsum": 288, "rb": 296, "bB": 298, "bC": 299, "b_d1": 300,
                   "sw": 304, "half": 312, "eps": 314, "tmp": 320}
        self.TO = {"sum32": 0, "otm": 512, "rowst": 1536, "p1a": 1664, "gn": 1792,
                   "ff": 1856}
        self.GSO = {"wBT": 0, "wCT": 128, "wd1T": 256, "wd2T": 768, "sp": 1792}

    def u_ft(self, ft):
        return self.W[:, ft * (L + 2): (ft + 1) * (L + 2)]

    def vu_ft(self, ft):
        o = NFT * (L + 2)
        return self.W[:, o + ft * L: o + (ft + 1) * L]

    def _enter(self, cm):
        v = cm.__enter__()
        self._ctx.append(cm)
        return v


    def sec(self, eng, fn):
        self._pend.setdefault(eng, []).append(fn)

    def _flushq(self, eng):
        fns = self._pend.get(eng)
        if not fns:
            return
        self._pend[eng] = []
        def big(e, fns=fns):
            for f in fns:
                f(e)
        getattr(self.block, eng)(big)

    def flush_all(self):
        for eng in list(self._pend):
            self._flushq(eng)

    def inc(self, nm, k):
        self.cnt[nm] += k
        return self.cnt[nm]

    def _op(self, eng, nm, fn, inc, skip_own=False):
        before = self.cnt[nm]
        if inc:
            self.inc(nm, inc)
        sem = self.sem[nm]
        def f(e, fn=fn, inc=inc, sem=sem, before=before, skip_own=skip_own):
            if before and not skip_own:
                e.wait_ge(sem, before)
            r = fn(e)
            if inc:
                r.then_inc(sem, inc)
        self.sec(eng, f)

    def vop(self, fn, inc=1):
        self._op("vector", "v", fn, inc)

    def sop(self, fn, inc=1):
        self._op("scalar", "s", fn, inc)

    def gop(self, fn, inc=1):
        self._op("gpsimd", "p", fn, inc)

    def peop(self, fn, inc=1):
        self._op("tensor", "pe", fn, inc, skip_own=True)

    def dma(self, queue, nm, out_ap, in_ap, transpose=False):
        n = self.inc(nm, 16)
        sem = self.sem[nm]
        def f(e, out_ap=out_ap, in_ap=in_ap, sem=sem, transpose=transpose):
            if transpose:
                e.dma_start_transpose(out_ap, in_ap).then_inc(sem, 16)
            else:
                e.dma_start(out=out_ap, in_=in_ap).then_inc(sem, 16)
        self.sec(queue, f)
        return n

    def store(self, out_ap, in_ap, nm="sto0", queue="sync"):
        n = self.inc(nm, 16)
        sem = self.sem[nm]
        def f(e, out_ap=out_ap, in_ap=in_ap, sem=sem):
            e.dma_start(out=out_ap, in_=in_ap).then_inc(sem, 16)
        self.sec(queue, f)
        return n

    def wait(self, eng, nm, val=None):
        val = self.cnt[nm] if val is None else val
        if val <= 0:
            return
        sem = self.sem[nm]
        self.sec(eng, lambda e, sem=sem, val=val: e.wait_ge(sem, val))

    def mark(self, name, nm):
        self.mk[name] = (nm, self.cnt[nm])

    def mwait(self, eng, name):
        if name in self.mk:
            nm, v = self.mk[name]
            self.wait(eng, nm, v)

    def barrier(self):
        self.flush_all()
        self.cnt["bar"] += len(ENGS)
        tgt = self.cnt["bar"]
        sem = self.sem["bar"]
        for eng in ENGS:
            def f(e, sem=sem, tgt=tgt):
                e.sem_inc(sem, 1)
                e.wait_ge(sem, tgt)
            self.sec(eng, f)

    def tap(self, name, src_ap, shape, dtype=None):
        if name not in self.taps:
            return
        dtype = src_ap.dtype if dtype is None else dtype
        t = self.nc.dram_tensor("tap_" + name, list(shape), dtype, kind="ExternalOutput").ap()
        self.tapd[name] = True
        for nm in ("v", "s", "pe", "p", "str", "sts0", "sts1"):
            self.wait("sync", nm)
        self.store(t, src_ap)
        self.wait("sync", "sto0")
        self.wait("gpsimd", "sto0")

    def _prog(self):
        self.block = self._enter(self.nc.Block())
        self.sem = {nm: self._enter(self.nc.semaphore(nm)) for nm in
                    ("l0", "l1", "l2", "l3", "lw", "g0", "g1", "str", "sts0",
                     "sts1", "sto0", "sto1", "pe", "v", "s", "p", "bar")}
        self.cnt = {nm: 0 for nm in self.sem}
        self.prelude()
        self.p0()
        if self.upto == "p0":
            return self.finish()
        for b in range(NB):
            self.barrier()
            self.p1a(b)
            if self.upto == "p1a" and b == 0:
                return self.finish()
            self.barrier()
            self.p1b(b)
            if self.upto == "p1b" and b == 0:
                return self.finish()
            self.chanstats(b)
            self.barrier()
            self.p2(b)
            if self.upto == "p2" and b == 0:
                return self.finish()
            self.barrier()
            self.ffn(b)
            if self.upto == "ffn" and b == 0:
                return self.finish()
        self.finish()

    def prelude(self):
        self.vop(lambda v: v.memset(self.ONE[:], 1.0))
        self.vop(lambda v: v.memset(self.G32[:], 0.0))
        self.vop(lambda v: v.memset(self.S32[:], 0.0))
        self.vop(lambda v: v.memset(
            self.G32[:, self.GO["half"]:self.GO["half"] + 1], 0.5))
        self.vop(lambda v: v.memset(
            self.G32[:, self.GO["eps"]:self.GO["eps"] + 1], EPS), inc=1)
        self.wait("sync", "v")
        self.wait("gpsimd", "v")
        self.vop(lambda v: v.memset(self.CAR[:], 0.0))
        self.vop(lambda v: v.memset(self.IDN[:], 1.0), inc=1)
        self.wait("gpsimd", "v")
        self.gop(lambda g: g.affine_select(self.IDN[:], self.IDN[:], [[1, 128]],
                                           OP.is_equal, 0.0, base=0,
                                           channel_multiplier=-1), inc=1)
        self.wait("vector", "p")
        self.vop(lambda v: v.tensor_copy(self.IDN32[:], self.IDN[:]), inc=1)
        self.tap("idn", self.IDN32[:], (128, 128))

    def p0(self):
        for b2 in range(NB):
            for c in range(4):
                self.dma("gpsimd", "g0", self.x16[b2, c * 512:(c + 1) * 512, :],
                         self.x_h[b2, c * 512:(c + 1) * 512, :])
        self.p0_small()
        self.barrier()
        ga = self.p0_attn_units()
        gq = self.p0_quant_units()
        done = object()
        qi = 0
        alive_a, alive_q = True, True
        while alive_q or alive_a:
            if alive_q:
                alive_q = next(gq, done) is not done
                qi += 1
            if alive_a and (qi % 4 == 0 or not alive_q):
                alive_a = next(ga, done) is not done
        self.p0_dk()
        self.barrier()

    def p0_small(self):
        g = self.G32
        GO = self.GO
        for name, off in (("D", GO["D"]), ("b_in", GO["b_in"]), ("conv_b", GO["cb"]),
                          ("b_d2", GO["b_d2"]), ("b_out", GO["b_out"])):
            for c in range(NFT):
                self.dma("sync", "l0", g[:, off + c: off + c + 1],
                         self.w_h[name][c * 128:(c + 1) * 128].unsqueeze(1))
        self.dma("sync", "l0", g[0:NST, GO["bB"]:GO["bB"] + 1], self.w_h["b_B"].unsqueeze(1))
        self.dma("sync", "l0", g[0:NST, GO["bC"]:GO["bC"] + 1], self.w_h["b_C"].unsqueeze(1))
        self.dma("sync", "l0", g[0:NDLT, GO["b_d1"]:GO["b_d1"] + 1], self.w_h["b_d1"].unsqueeze(1))
        tmp = g[:, GO["tmp"]:GO["tmp"] + NST]
        for c in range(NFT):
            self.mwait("sync", "negA_done")
            self.dma("sync", "l0", tmp, self.w_h["A_log"][c * 128:(c + 1) * 128, :])
            self.wait("scalar", "l0")
            self.sop(lambda s, c=c: s.activation(
                g[:, GO["negA"] + c * NST: GO["negA"] + (c + 1) * NST], tmp,
                AF.Exp), inc=1)
            self.mark("negA_done", "s")
        self.wait("vector", "s")
        self.vop(lambda v: v.tensor_scalar_mul(
            g[:, GO["negA"]:GO["negA"] + 128], g[:, GO["negA"]:GO["negA"] + 128], -1.0),
            inc=1)
        S = self.S
        stg = S[:, 0:1024]
        GSO = self.GSO
        for (wname, gso, m) in (("w_B", "wBT", NST), ("w_C", "wCT", NST), ("w_d1", "wd1T", NDLT)):
            self.mwait("gpsimd", "smallw_done")
            self.dma("gpsimd", "g1", stg[0:m, 0:FD], self.w_h[wname])
            self.wait("tensor", "g1")
            self.wait("tensor", "v")
            for kt in range(NFT):
                self.mwait("tensor", "smallev")
                self.peop(lambda t, kt=kt, m=m: t.transpose(
                    self.PT16[0:128, 0:m], stg[0:m, kt * 128:(kt + 1) * 128], self.IDN[0:m, 0:m]),
                    inc=1)
                self.wait("vector", "pe")
                self.vop(lambda v, kt=kt, gso=gso, m=m: v.tensor_copy(
                    self.GS[:, GSO[gso] + kt * m: GSO[gso] + (kt + 1) * m],
                    self.PT16[0:128, 0:m]), inc=1)
                self.mark("smallev", "v")
            self.mark("smallw_done", "v")
        self.mwait("gpsimd", "smallw_done")
        self.dma("gpsimd", "g1", stg[:, 0:NFT * NDLT].rearrange("p (t d) -> p t d", d=NDLT),
                 self.w_h["w_d2"].rearrange("(t p) d -> p t d", p=128))
        self.wait("tensor", "g1")
        for mt in range(NFT):
            self.mwait("tensor", "smallev")
            self.peop(lambda t, mt=mt: t.transpose(
                self.PT16[0:NDLT, 0:128], stg[:, mt * NDLT:(mt + 1) * NDLT], self.IDN[:]), inc=1)
            self.wait("vector", "pe")
            self.vop(lambda v, mt=mt: v.tensor_copy(
                self.GS[0:NDLT, GSO["wd2T"] + mt * 128: GSO["wd2T"] + (mt + 1) * 128],
                self.PT16[0:NDLT, 0:128]), inc=1)
            self.mark("smallev", "v")
        self.wait("vector", "l0")
        self.vop(lambda v: v.tensor_copy(self.GS[:, GSO["sp"]:GSO["sp"] + 8],
                                         g[:, GO["b_in"]:GO["b_in"] + 8]), inc=1)

    def p0_quant_units(self):
        g = self.G32
        GO = self.GO
        S = self.S
        st32 = [self.SF[:, 0:1024], self.SF[:, 1024:2048], self.SF[:, 4608:5632]]
        u32_b = [self.SF[:, 2048:3072], self.SF[:, 3584:4608]]
        wqn_b = [S[:, 0:1024], S[:, 1024:2048]]
        fb = S[:, 2048:2560]
        mk = S[:, 2560:3072]
        mk2 = S[:, 3072:3584]
        ev16 = [S[:, 3584:4096], S[:, 4096:4608]]
        t0 = g[0:1, GO["tmp"]:GO["tmp"] + 1]
        t1 = g[0:1, GO["tmp"] + 1:GO["tmp"] + 2]
        t2 = g[0:1, GO["tmp"] + 2:GO["tmp"] + 3]
        acc = g[:, GO["tmp"] + 4:GO["tmp"] + 5]
        red = g[:, GO["tmp"] + 5:GO["tmp"] + 6]
        hcol = self.GS[:, self.GSO["sp"] + 8:self.GSO["sp"] + 9]
        for wi, (wname, scr) in enumerate((("w_gate", self.ws_g), ("w_up", self.ws_u),
                                           ("w_down", self.ws_d))):
            wh = self.w_h[wname]
            nrow, ncolw = wh.shape
            ntile = nrow // 128
            csub = ncolw // 1024
            nu = ntile * csub
            ld_cnt = {}

            def issue_p1(u):
                i, cs = divmod(u, csub)
                lsem = "l%d" % (u % 3)
                self.mwait("sync", "q_red%d" % (u % 3))
                self.dma("sync", lsem, st32[u % 3],
                         wh[i * 128:(i + 1) * 128, cs * 1024:(cs + 1) * 1024])
                ld_cnt[u] = self.cnt[lsem]
            self.vop(lambda v: v.memset(acc, 0.0), inc=1)
            issue_p1(0)
            issue_p1(1)
            for u in range(nu):
                st = st32[u % 3]
                if u + 2 < nu:
                    issue_p1(u + 2)
                self.wait("vector", "l%d" % (u % 3), ld_cnt[u])
                self.vop(lambda v, st=st: v.tensor_reduce(
                    red, st, AX.X, OP.add, apply_absolute_value=True))
                self.vop(lambda v: v.tensor_tensor(acc, acc, red, OP.add), inc=1)
                self.mark("q_red%d" % (u % 3), "v")
                yield
            self.vop(lambda v: v.tensor_copy(hcol, acc), inc=1)
            self.wait("tensor", "v")
            self.peop(lambda t: t.matmul(self.PS[4][0:1, 0:1], self.ONE[:, 0:1], hcol,
                                         start=True, stop=True), inc=1)
            self.wait("vector", "pe")
            self.vop(lambda v: v.tensor_copy(t0, self.PS[4][0:1, 0:1]), inc=1)
            self.wait("tensor", "v")
            self.vop(lambda v, n=nrow * ncolw: v.tensor_scalar(t1, t0, 1.0 / n, 1e-5, OP.mult, OP.max))
            self.vop(lambda v: v.reciprocal(t2, t1), inc=1)
            self.wait("sync", "v")
            self.store(self.rows_t[self.RO["sw"] + wi:self.RO["sw"] + wi + 1].unsqueeze(0), t2, nm="str")
            self.store(self.rows_t[self.RO["sw"] + 3 + wi:self.RO["sw"] + 4 + wi].unsqueeze(0), t1, nm="str")
            self.wait("gpsimd", "str")
            self.dma("gpsimd", "g0", g[:, GO["sw"] + wi:GO["sw"] + wi + 1],
                     self.rows_t[self.RO["sw"] + wi:self.RO["sw"] + wi + 1]
                     .unsqueeze(0).partition_broadcast(128))
            self.dma("gpsimd", "g0", g[:, GO["sw"] + 3 + wi:GO["sw"] + 4 + wi],
                     self.rows_t[self.RO["sw"] + 3 + wi:self.RO["sw"] + 4 + wi]
                     .unsqueeze(0).partition_broadcast(128))
            self.wait("scalar", "g0")
            self.wait("vector", "g0")
            def issue_p2(u):
                i, cs = divmod(u, csub)
                lsem = "l%d" % (u % 3)
                self.mwait("sync", "q_quant%d" % (u % 3))
                self.dma("sync", lsem, st32[u % 3],
                         wh[i * 128:(i + 1) * 128, cs * 1024:(cs + 1) * 1024])
                ld_cnt[u] = self.cnt[lsem]
            chain_cnt = {}

            def evac_spill(u):
                i, cs = divmod(u, csub)
                sl0 = cs * 1024
                wqn = wqn_b[u % 2]
                self.wait("tensor", "v", chain_cnt[u])
                for half in range(2):
                    self.mwait("tensor", "q_ev")
                    for j in range(4):
                        blk = half * 4 + j
                        self.peop(lambda t, blk=blk, j=j, wqn=wqn: t.transpose(
                            self.PT16[:, j * 128:(j + 1) * 128],
                            wqn[:, blk * 128:(blk + 1) * 128], self.IDN[:]))
                    if half == 1:
                        self.mark("q_wq%d" % (u % 2), "pe")
                    self.wait("scalar", "pe", self.cnt["pe"])
                    self.mwait("scalar", "q_st%d" % half)
                    self.sop(lambda s, half=half: s.copy(ev16[half], self.PT16[:, 0:512]), inc=1)
                    self.mark("q_ev", "s")
                    self.wait("sync", "s", self.cnt["s"])
                    bj0 = sl0 // 128 + half * 4
                    self.store(scr[i, :, bj0 * 128:bj0 * 128 + 512],
                               ev16[half][:, 0:512], nm="sts%d" % half)
                    self.mark("q_st%d" % half, "sts%d" % half)
            issue_p2(0)
            issue_p2(1)
            for u in range(nu):
                i, cs = divmod(u, csub)
                st = st32[u % 3]
                u32 = u32_b[u % 2]
                wqn = wqn_b[u % 2]
                if u + 2 < nu:
                    issue_p2(u + 2)
                if True:
                    self.wait("scalar", "l%d" % (u % 3), ld_cnt[u])
                    self.mwait("scalar", "q_u%d" % (u % 2))
                    self.sop(lambda s, st=st, wi=wi, u32=u32: s.activation(
                        u32, st, AF.Identity,
                        bias=g[:, GO["half"]:GO["half"] + 1],
                        scale=g[:, GO["sw"] + wi:GO["sw"] + wi + 1]), inc=1)
                    self.mark("q_quant%d" % (u % 3), "s")
                    self.wait("vector", "s", self.cnt["s"])
                    self.mwait("vector", "q_wq%d" % (u % 2))
                    for hc in range(2):
                        sl = slice(hc * 512, (hc + 1) * 512)
                        self.vop(lambda v, sl=sl, u32=u32: v.tensor_copy(
                            self.I16T[:, 0:512], u32[:, sl]))
                        self.vop(lambda v: v.tensor_copy(fb, self.I16T[:, 0:512]))
                        self.vop(lambda v, sl=sl, u32=u32: v.tensor_tensor(
                            mk, fb, u32[:, sl], OP.is_gt))
                        self.vop(lambda v, sl=sl, u32=u32: v.scalar_tensor_tensor(
                            mk2, fb, 1.0, u32[:, sl], OP.add, OP.is_le))
                        self.vop(lambda v: v.tensor_tensor(fb, fb, mk, OP.subtract))
                        self.vop(lambda v: v.tensor_tensor(fb, fb, mk2, OP.add))
                        self.vop(lambda v, sl=sl, wqn=wqn: v.tensor_scalar(
                            wqn[:, sl], fb, 1.0, -1.0, OP.min, OP.max), inc=1)
                    self.mark("q_u%d" % (u % 2), "v")
                    chain_cnt[u] = self.cnt["v"]
                    if u > 0:
                        evac_spill(u - 1)
                yield
            evac_spill(nu - 1)

    def p0_attn_units(self):
        g = self.G32
        GO = self.GO
        S = self.S
        nat_b = [S[:, 13312:16384], S[:, 16384:19456]]
        ev16 = [S[:, 19456:19968], S[:, 19968:20480]]
        natsep = S[:, 20480:21504]
        units = [(wname, tag, mt) for (wname, tag) in (("w_in", "in"), ("w_out", "out"))
                 for mt in range(NFT)]
        units += [("conv_w", "conv", mt) for mt in range(NFT)]
        g1_cnt = {}

        AWS = ("g1", "l3", "lw")

        def issue(i):
            wname, tag, mt = units[i]
            nat = nat_b[i % 2]
            gsem = AWS[i % 3]
            self.mwait("gpsimd", "aw_natS%d" % (i % 2))
            self.mwait("gpsimd", "aw_natV%d" % (i % 2))
            if tag == "conv":
                self.dma("gpsimd", gsem, nat[:, 0:3072],
                         self.w_h["conv_w"][mt * 128:(mt + 1) * 128, :, :]
                         .rearrange("p i k -> p (i k)"))
            else:
                self.dma("gpsimd", gsem, nat[:, 0:1024],
                         self.w_h[wname][mt * 128:(mt + 1) * 128, :])
            g1_cnt[i] = self.cnt[gsem]
        issue(0)
        issue(1)
        for i, (wname, tag, mt) in enumerate(units):
            nat = nat_b[i % 2]
            if tag != "conv":
                self.wait("tensor", AWS[i % 3], g1_cnt[i])
                for half in range(2):
                    self.mwait("tensor", "q_ev")
                    for j in range(4):
                        kt = half * 4 + j
                        self.peop(lambda t, kt=kt, j=j, nat=nat: t.transpose(
                            self.PT16[:, j * 128:(j + 1) * 128],
                            nat[:, kt * 128:(kt + 1) * 128], self.IDN[:]))
                    self.wait("scalar", "pe")
                    self.mwait("scalar", "aw_st%d" % half)
                    self.sop(lambda s, half=half, ev16=ev16: s.copy(
                        ev16[half], self.PT16[:, 0:512]), inc=1)
                    self.mark("q_ev", "s")
                    self.wait("gpsimd", "s")
                    kt0 = half * 4
                    if tag == "in":
                        self.store(self.ws_in[:, (mt * 8 + kt0) * 128:(mt * 8 + kt0) * 128 + 512],
                                   ev16[half][:, 0:512], nm="sts%d" % half, queue="gpsimd")
                    else:
                        self.store(self.ws_out[mt, :, kt0 * 128:kt0 * 128 + 512],
                                   ev16[half][:, 0:512], nm="sts%d" % half, queue="gpsimd")
                    self.mark("aw_st%d" % half, "sts%d" % half)
                if tag == "out":
                    self.wait("vector", AWS[i % 3], g1_cnt[i])
                    self.vop(lambda v, mt=mt, nat=nat: v.tensor_reduce(
                        g[:, GO["nwsum"] + mt:GO["nwsum"] + mt + 1], nat[:, 0:1024],
                        AX.X, OP.add, negate=True), inc=1)
                self.mark("aw_natS%d" % (i % 2), "s")
                self.mark("aw_natV%d" % (i % 2), "v")
                if i + 2 < len(units):
                    issue(i + 2)
                yield
            else:
                self.wait("vector", AWS[i % 3], g1_cnt[i])
                for k in range(3):
                    self.mwait("vector", "aw_sep")
                    self.vop(lambda v, k=k, nat=nat, natsep=natsep: v.tensor_copy(
                        natsep, nat[:, k:3072:3]), inc=1)
                    self.wait("tensor", "v")
                    for half in range(2):
                        self.mwait("tensor", "q_ev")
                        for j in range(4):
                            kt = half * 4 + j
                            self.peop(lambda t, kt=kt, j=j, natsep=natsep: t.transpose(
                                self.PT16[:, j * 128:(j + 1) * 128],
                                natsep[:, kt * 128:(kt + 1) * 128], self.IDN[:]))
                        if half == 1:
                            self.mark("aw_sep", "pe")
                        self.wait("scalar", "pe")
                        self.mwait("scalar", "aw_st%d" % half)
                        self.sop(lambda s, half=half, ev16=ev16: s.copy(
                            ev16[half], self.PT16[:, 0:512]), inc=1)
                        self.mark("q_ev", "s")
                        self.wait("gpsimd", "s")
                        kt0 = half * 4
                        self.store(self.ws_conv[mt, :, (k * 8 + kt0) * 128:(k * 8 + kt0) * 128 + 512],
                                   ev16[half][:, 0:512], nm="sts%d" % half, queue="gpsimd")
                        self.mark("aw_st%d" % half, "sts%d" % half)
                    yield
                self.mark("aw_natS%d" % (i % 2), "s")
                self.mark("aw_natV%d" % (i % 2), "v")
                if i + 2 < len(units):
                    issue(i + 2)

    def p0_dk(self):
        g = self.G32
        GO = self.GO
        S = self.S
        self.wait("sync", "sts0")
        self.wait("sync", "sts1")
        self.wait("tensor", "v")
        binh = self.GS[:, self.GSO["sp"]:self.GSO["sp"] + 8]
        cw1 = S[:, 20480:21504]
        for k in range(3):
            for mt in range(NFT):
                self.mwait("sync", "dk_mm")
                self.dma("sync", "l0", cw1, self.ws_conv[mt, :, k * 1024:(k + 1) * 1024])
                self.wait("tensor", "l0")
                for kt in range(NFT):
                    self.peop(lambda t, kt=kt, mt=mt: t.matmul(
                        self.PS[4][:, mt:mt + 1], cw1[:, kt * 128:(kt + 1) * 128],
                        binh[:, kt:kt + 1], start=(kt == 0), stop=(kt == NFT - 1)))
                self.mark("dk_mm", "pe")
            self.wait("vector", "pe")
            self.vop(lambda v, k=k: v.tensor_copy(
                g[:, GO["tmp"] + 8 * k:GO["tmp"] + 8 * (k + 1)], self.PS[4][:, 0:8]), inc=1)
            self.wait("tensor", "v")
        d0 = g[:, GO["tmp"]:GO["tmp"] + 8]
        d1 = g[:, GO["tmp"] + 8:GO["tmp"] + 16]
        d2 = g[:, GO["tmp"] + 16:GO["tmp"] + 24]
        cI = g[:, GO["cI"]:GO["cI"] + 8]
        c0 = g[:, GO["c0"]:GO["c0"] + 8]
        cL = g[:, GO["cL"]:GO["cL"] + 8]
        cb = g[:, GO["cb"]:GO["cb"] + 8]
        self.wait("vector", "l0")
        self.vop(lambda v: v.tensor_tensor(cI, cb, d0, OP.add))
        self.vop(lambda v: v.tensor_tensor(cI, cI, d1, OP.add))
        self.vop(lambda v: v.tensor_tensor(cI, cI, d2, OP.add))
        self.vop(lambda v: v.tensor_tensor(c0, cI, d0, OP.subtract))
        self.vop(lambda v: v.tensor_tensor(cL, cI, d2, OP.subtract), inc=1)
        self.tap("d0", d0, (128, 8))
        self.tap("cb", cb, (128, 8))
        self.tap("cI", cI, (128, 8))
        self.tap("negA", g[:, GO["negA"]:GO["negA"] + 128], (128, 128))
        self.tap("nwsum", g[:, GO["nwsum"]:GO["nwsum"] + 8], (128, 8))

    def p1a(self, b):
        g = self.G32
        GO = self.GO
        S = self.S
        s32 = self.S32
        T = self.TO["p1a"]
        buf_b = [S[:, 0:1024], S[:, 1024:2048]]
        sq_b = [S[:, 2048:3072], S[:, 3072:4096]]
        sx = s32[:, T:T + 16]
        sxx = s32[:, T + 16:T + 32]
        a0 = s32[:, T + 32:T + 48]
        b0 = s32[:, T + 48:T + 64]
        w = s32[:, T + 64:T + 80]
        mu = s32[:, T + 80:T + 96]
        var = s32[:, T + 96:T + 112]
        tt = s32[:, T + 112:T + 128]
        self.wait("sync", "g0")
        self.wait("sync", "g1")
        ld = {}

        def issue(t):
            lsem = "l%d" % (t % 2)
            self.mwait("sync", "p1a_buf%d" % (t % 2))
            self.dma("sync", lsem, buf_b[t % 2], self.x16[b, t * 128:(t + 1) * 128, :])
            ld[t] = self.cnt[lsem]
        issue(0)
        issue(1)
        for t in range(16):
            buf = buf_b[t % 2]
            sq = sq_b[t % 2]
            self.wait("scalar", "l%d" % (t % 2), ld[t])
            self.mwait("scalar", "p1a_sq%d" % (t % 2))
            self.sop(lambda s, buf=buf, sq=sq: s.activation(sq, buf, AF.Square), inc=1)
            self.wait("vector", "l%d" % (t % 2), ld[t])
            self.wait("vector", "s")
            self.vop(lambda v, buf=buf, t=t: v.tensor_reduce(
                sx[:, t:t + 1], buf, AX.X, OP.add))
            self.vop(lambda v, sq=sq, t=t: v.tensor_reduce(
                sxx[:, t:t + 1], sq, AX.X, OP.add), inc=1)
            self.mark("p1a_buf%d" % (t % 2), "v")
            self.mark("p1a_sq%d" % (t % 2), "v")
            if t + 2 < 16:
                issue(t + 2)
        self.wait("tensor", "v")
        self.vop(lambda v: v.tensor_scalar_mul(mu, sx, 1.0 / FD))
        self.vop(lambda v: v.tensor_scalar_mul(var, sxx, 1.0 / FD))
        self.vop(lambda v: v.tensor_tensor(tt[:, 0:16], mu, mu, OP.mult))
        self.vop(lambda v: v.tensor_tensor(var, var, tt[:, 0:16], OP.subtract), inc=1)
        self.wait("scalar", "v")
        self.sop(lambda s: s.activation(tt[:, 0:16], var, AF.Sqrt, bias=g[:, GO["eps"]:GO["eps"] + 1]), inc=1)
        self.wait("vector", "s")
        self.vop(lambda v: v.reciprocal(a0, tt[:, 0:16]))
        self.vop(lambda v: v.tensor_tensor(b0, mu, a0, OP.mult))
        self.vop(lambda v: v.tensor_scalar_mul(b0, b0, -1.0))
        self.vop(lambda v: v.tensor_scalar_add(tt[:, 0:16], var, EPS))
        self.vop(lambda v: v.reciprocal(tt[:, 0:16], tt[:, 0:16]))
        self.vop(lambda v: v.tensor_tensor(w, var, tt[:, 0:16], OP.mult))
        self.vop(lambda v: v.tensor_reduce(tt[:, 0:1], w, AX.X, OP.add))
        self.vop(lambda v: v.tensor_copy(self.GS[:, self.GSO["sp"] + 8:self.GSO["sp"] + 9],
                                         tt[:, 0:1]), inc=1)
        self.wait("tensor", "v")
        self.peop(lambda t: t.matmul(self.PS[4][0:1, 16:17], self.ONE[:, 0:1],
                                     self.GS[:, self.GSO["sp"] + 8:self.GSO["sp"] + 9],
                                     start=True, stop=True), inc=1)
        self.wait("vector", "pe")
        rbt = tt[0:1, 4:5]
        self.vop(lambda v, rbt=rbt: v.tensor_copy(rbt, self.PS[4][0:1, 16:17]))
        self.vop(lambda v, rbt=rbt: v.tensor_scalar_mul(rbt, rbt, 1.0 / L), inc=1)
        self.wait("scalar", "v")
        self.sop(lambda s, rbt=rbt: s.activation(rbt, rbt, AF.Sqrt,
                 bias=g[0:1, GO["eps"]:GO["eps"] + 1]), inc=1)
        self.wait("vector", "s")
        self.vop(lambda v, rbt=rbt: v.reciprocal(rbt, rbt), inc=1)
        self.wait("tensor", "v")
        self.peop(lambda t: t.transpose(self.PT32[0:32, 0:128], s32[:, T + 32:T + 64],
                                        self.IDN32[:]), inc=1)
        self.wait("vector", "pe")
        rowst = s32[0:32, self.TO["rowst"]:self.TO["rowst"] + 128]
        self.vop(lambda v, rowst=rowst: v.tensor_copy(rowst, self.PT32[0:32, 0:128]), inc=1)
        self.wait("sync", "v")
        self.store(self.row("a0", 0, L), rowst[0:16, :], nm="str")
        self.store(self.row("b0", 0, L), rowst[16:32, :], nm="str")
        self.store(self.row("rb", 0, 1), tt[0:1, 4:5], nm="str")
        self.wait("gpsimd", "str")
        self.dma("gpsimd", "g0", g[:, GO["rb"]:GO["rb"] + 1],
                 self.row("rb", 0, 1).partition_broadcast(128))
        self.wait("vector", "g0")
        self.vop(lambda v: v.tensor_tensor(g[:, GO["rb"] + 1:GO["rb"] + 2],
                                           g[:, GO["rb"]:GO["rb"] + 1],
                                           g[:, GO["rb"]:GO["rb"] + 1], OP.mult), inc=1)
        self.tap("a0row", self.row("a0", 0, L), (1, L))
        self.tap("b0row", self.row("b0", 0, L), (1, L))
        self.tap("rb", g[:, GO["rb"]:GO["rb"] + 2], (128, 2))

    def p1b(self, b):
        g = self.G32
        GO = self.GO
        S = self.S
        x2 = S[:, 0:4096]
        xz = [S[:, 4096 + i * 4096: 4096 + (i + 1) * 4096] for i in range(2)]
        bc = [S[:, 12288 + i * 512: 12288 + (i + 1) * 512] for i in range(4)]
        w_inT = S[:, 14336:22528]
        cring = [S[:, 22528 + i * 3072: 22528 + (i + 1) * 3072] for i in range(2)]
        self.wait("sync", "sts0")
        self.wait("sync", "sts1")
        for ft in range(NFT):
            self.vop(lambda v, ft=ft: v.memset(self.u_ft(ft)[:, 0:1], 0.0))
            self.vop(lambda v, ft=ft: v.memset(self.u_ft(ft)[:, L + 1:L + 2], 0.0))
        self.dma("sync", "lw", w_inT, self.ws_in)
        w_inT_ld = self.cnt["lw"]
        for c in range(NCH1):
            buf = xz[c % 2]
            lsem = "l%d" % (c % 2)
            gsem = "g%d" % (c % 2)
            self.mwait("sync", "p1b_buf%d" % (c % 2))
            for ft in range(NFT):
                self.dma("sync", lsem, buf[:, ft * CH1:(ft + 1) * CH1],
                         self.x16[b, c * CH1:(c + 1) * CH1, ft * 128:(ft + 1) * 128],
                         transpose=True)
            ba = bc[2 * (c % 2)]
            bb2 = bc[2 * (c % 2) + 1]
            self.mwait("gpsimd", "p1b_bc%d" % (c % 2))
            self.dma("gpsimd", gsem, ba, self.row("a0", c * CH1, (c + 1) * CH1).partition_broadcast(128))
            self.dma("gpsimd", gsem, bb2, self.row("b0", c * CH1, (c + 1) * CH1).partition_broadcast(128))
            self.wait("vector", lsem)
            self.wait("vector", gsem)
            for ft in range(NFT):
                sl = slice(ft * CH1, (ft + 1) * CH1)
                self.vop(lambda v, buf=buf, sl=sl, ba=ba: v.tensor_tensor(
                    buf[:, sl], buf[:, sl], ba, OP.mult))
                self.vop(lambda v, buf=buf, sl=sl, bb2=bb2: v.tensor_tensor(
                    buf[:, sl], buf[:, sl], bb2, OP.add))
            self.mark("p1b_bc%d" % (c % 2), "v")
            zdone = self.cnt["v"]
            self.wait("tensor", "v", zdone)
            self.wait("tensor", "lw", w_inT_ld)
            for mt in range(NFT):
                ps = self.PS[mt % 2]
                self.mwait("tensor", "p1b_ps%d" % (mt % 2))
                for kt in range(NFT):
                    self.peop(lambda t, mt=mt, kt=kt, buf=buf, ps=ps: t.matmul(
                        ps[:, 0:CH1], w_inT[:, (mt * 8 + kt) * 128:(mt * 8 + kt + 1) * 128],
                        buf[:, kt * CH1:(kt + 1) * CH1],
                        start=(kt == 0), stop=(kt == NFT - 1)))
                self.wait("scalar", "pe")
                self.sop(lambda s, mt=mt, c=c, ps=ps: s.copy(
                    self.u_ft(mt)[:, 1 + c * CH1: 1 + (c + 1) * CH1], ps[:, 0:CH1]), inc=1)
                self.mark("p1b_ps%d" % (mt % 2), "s")
            self.mark("p1b_buf%d" % (c % 2), "pe")
        u_done = self.cnt["s"]
        for mt in range(NFT):
            ring = cring[mt % 2]
            lsem = "l%d" % (2 + mt % 2)
            self.mwait("sync", "p1b_ring%d" % (mt % 2))
            self.dma("sync", lsem, ring, self.ws_conv[mt])
            self.wait("tensor", lsem)
            if mt == 0:
                self.wait("tensor", "s", u_done)
                self.wait("tensor", "v")
            for c in range(NCH1):
                ps = self.PS[2 + (c % 2)]
                self.mwait("tensor", "p1b_cps%d" % (c % 2))
                for k in range(3):
                    for kt in range(NFT):
                        off = 1 + c * CH1 + (k - 1)
                        self.peop(lambda t, k=k, kt=kt, off=off, ps=ps, ring=ring: t.matmul(
                            ps[:, 0:CH1], ring[:, (k * 8 + kt) * 128:(k * 8 + kt + 1) * 128],
                            self.u_ft(kt)[:, off:off + CH1],
                            start=(k == 0 and kt == 0), stop=(k == 2 and kt == NFT - 1)))
                self.wait("scalar", "pe")
                self.mwait("scalar", "p1b_sq%d" % (c % 2))
                self.sop(lambda s, mt=mt, c=c, ps=ps: s.activation(
                    self.vu_ft(mt)[:, c * CH1:(c + 1) * CH1], ps[:, 0:CH1], AF.Identity,
                    accum_out=g[:, GO["SvU"] + mt * 4 + c:GO["SvU"] + mt * 4 + c + 1]))
                self.sop(lambda s, mt=mt, c=c, ps=ps: s.activation(
                    x2[:, (c % 2) * 512:(c % 2) * 512 + CH1], ps[:, 0:CH1], AF.Square,
                    accum_out=g[:, GO["SvU2"] + mt * 4 + c:GO["SvU2"] + mt * 4 + c + 1]), inc=1)
                self.mark("p1b_cps%d" % (c % 2), "s")
                self.mark("p1b_sq%d" % (c % 2), "s")
            self.mark("p1b_ring%d" % (mt % 2), "pe")
        self.tap("u", self.U[:, 0:2050], (128, 2050)) if False else None
        self.tap("u", self.u_ft(0)[:, 0:2050], (128, 2050))
        self.tap("vU", self.vu_ft(0), (128, L))
        self.tap("SvU", g[:, GO["SvU"]:GO["SvU"] + 64], (128, 64))

    @property
    def U(self):
        return self.W

    def chanstats(self, b):
        g = self.G32
        GO = self.GO
        self.wait("vector", "s")
        t = lambda i: g[:, GO["tmp"] + 8 * i:GO["tmp"] + 8 * (i + 1)]
        vu0, vuL, sv, sv2, S1, S2, mu, tA, tB = (t(i) for i in range(9))
        rb = g[:, GO["rb"]:GO["rb"] + 1]
        rb2 = g[:, GO["rb"] + 1:GO["rb"] + 2]
        cI = g[:, GO["cI"]:GO["cI"] + 8]
        c0 = g[:, GO["c0"]:GO["c0"] + 8]
        cL = g[:, GO["cL"]:GO["cL"] + 8]
        al = g[:, GO["alpha"]:GO["alpha"] + 8]
        be = g[:, GO["beta"]:GO["beta"] + 8]
        be0 = g[:, GO["beta0"]:GO["beta0"] + 8]
        beL = g[:, GO["betaL"]:GO["betaL"] + 8]
        for ft in range(NFT):
            self.vop(lambda v, ft=ft: v.tensor_copy(vu0[:, ft:ft + 1], self.vu_ft(ft)[:, 0:1]))
            self.vop(lambda v, ft=ft: v.tensor_copy(vuL[:, ft:ft + 1], self.vu_ft(ft)[:, L - 1:L]))
        sAll = g[:, GO["SvU"]:GO["SvU"] + 32].rearrange("p (m c) -> p m c", c=4)
        s2All = g[:, GO["SvU2"]:GO["SvU2"] + 32].rearrange("p (m c) -> p m c", c=4)
        self.vop(lambda v: v.tensor_tensor(sv, sAll[:, :, 0], sAll[:, :, 1], OP.add))
        self.vop(lambda v: v.tensor_tensor(sv, sv, sAll[:, :, 2], OP.add))
        self.vop(lambda v: v.tensor_tensor(sv, sv, sAll[:, :, 3], OP.add))
        self.vop(lambda v: v.tensor_tensor(sv2, s2All[:, :, 0], s2All[:, :, 1], OP.add))
        self.vop(lambda v: v.tensor_tensor(sv2, sv2, s2All[:, :, 2], OP.add))
        self.vop(lambda v: v.tensor_tensor(sv2, sv2, s2All[:, :, 3], OP.add))
        self.vop(lambda v: v.tensor_scalar_mul(tA, cI, float(L - 2)))
        self.vop(lambda v: v.scalar_tensor_tensor(S1, sv, rb, tA, OP.mult, OP.add))
        self.vop(lambda v: v.tensor_tensor(S1, S1, c0, OP.add))
        self.vop(lambda v: v.tensor_tensor(S1, S1, cL, OP.add))
        self.vop(lambda v: v.tensor_tensor(tA, sv, vu0, OP.subtract))
        self.vop(lambda v: v.tensor_tensor(tA, tA, vuL, OP.subtract))
        self.vop(lambda v: v.tensor_tensor(tA, tA, cI, OP.mult))
        self.vop(lambda v: v.tensor_tensor(tB, c0, vu0, OP.mult))
        self.vop(lambda v: v.tensor_tensor(tA, tA, tB, OP.add))
        self.vop(lambda v: v.tensor_tensor(tB, cL, vuL, OP.mult))
        self.vop(lambda v: v.tensor_tensor(tA, tA, tB, OP.add))
        self.vop(lambda v: v.tensor_scalar_mul(tA, tA, 2.0))
        self.vop(lambda v: v.tensor_scalar(tA, tA, rb, None, OP.mult))
        self.vop(lambda v: v.scalar_tensor_tensor(S2, sv2, rb2, tA, OP.mult, OP.add))
        self.vop(lambda v: v.tensor_tensor(tB, cI, cI, OP.mult))
        self.vop(lambda v: v.tensor_scalar_mul(tB, tB, float(L - 2)))
        self.vop(lambda v: v.tensor_tensor(S2, S2, tB, OP.add))
        self.vop(lambda v: v.tensor_tensor(tB, c0, c0, OP.mult))
        self.vop(lambda v: v.tensor_tensor(S2, S2, tB, OP.add))
        self.vop(lambda v: v.tensor_tensor(tB, cL, cL, OP.mult))
        self.vop(lambda v: v.tensor_tensor(S2, S2, tB, OP.add))
        self.vop(lambda v: v.tensor_scalar_mul(mu, S1, 1.0 / L))
        self.vop(lambda v: v.tensor_scalar_mul(tB, S2, 1.0 / L))
        self.vop(lambda v: v.tensor_tensor(tA, mu, mu, OP.mult))
        self.vop(lambda v: v.tensor_tensor(tB, tB, tA, OP.subtract), inc=1)
        self.wait("scalar", "v")
        self.sop(lambda s: s.activation(tB, tB, AF.Sqrt, bias=g[:, GO["eps"]:GO["eps"] + 1]), inc=1)
        self.wait("vector", "s")
        self.vop(lambda v: v.reciprocal(tA, tB))
        self.vop(lambda v: v.tensor_scalar(al, tA, rb, None, OP.mult))
        self.vop(lambda v: v.tensor_tensor(tB, cI, mu, OP.subtract))
        self.vop(lambda v: v.tensor_tensor(be, tB, tA, OP.mult))
        self.vop(lambda v: v.tensor_tensor(tB, c0, mu, OP.subtract))
        self.vop(lambda v: v.tensor_tensor(be0, tB, tA, OP.mult))
        self.vop(lambda v: v.tensor_tensor(tB, cL, mu, OP.subtract))
        self.vop(lambda v: v.tensor_tensor(beL, tB, tA, OP.mult), inc=1)
        self.tap("alpha", al, (128, 8))
        self.tap("beta", be, (128, 8))

    def p2(self, b):
        g = self.G32
        GO = self.GO
        S = self.S
        s32 = self.S32
        GN = self.TO["gn"]
        O = {"h": 0, "dlt": 8 * CH2, "dH": 16 * CH2, "y": 24 * CH2, "ab0": 32 * CH2,
             "ab1": 33 * CH2, "bx": 34 * CH2, "so": 35 * CH2, "tm": 36 * CH2,
             "bc": 37 * CH2, "wout": 41 * CH2, "evt": 45 * CH2}
        h_t = lambda ft: S[:, O["h"] + ft * CH2: O["h"] + (ft + 1) * CH2]
        d_t = lambda ft: S[:, O["dlt"] + ft * CH2: O["dlt"] + (ft + 1) * CH2]
        dh_t = lambda ft: S[:, O["dH"] + ft * CH2: O["dH"] + (ft + 1) * CH2]
        y_t = lambda ft: S[:, O["y"] + ft * CH2: O["y"] + (ft + 1) * CH2]
        ab_ = [S[:, O["ab0"]:O["ab0"] + CH2], S[:, O["ab1"]:O["ab1"] + CH2]]
        bx_ = S[:, O["bx"]:O["bx"] + CH2]
        so_ = S[:, O["so"]:O["so"] + CH2]
        tm_ = S[:, O["tm"]:O["tm"] + CH2]
        bcp = [S[:, O["bc"] + i * CH2:O["bc"] + (i + 1) * CH2] for i in range(4)]
        wring = [S[:, O["wout"] + i * 1024: O["wout"] + (i + 1) * 1024] for i in range(2)]
        ev1 = S[:, O["evt"]:O["evt"] + 512]
        ev2 = S[:, O["evt"] + 512:O["evt"] + 1024]
        u2s = S[:, O["evt"] + 1024:O["evt"] + 1536]
        gsq = S[:, O["ab0"]:O["ab0"] + CH2]
        h_cnt = {}
        h_pe_free = {0: 0}

        def emit_h(ch2):
            self.wait("scalar", "pe", h_pe_free[ch2])
            self.mwait("scalar", "p2_hfree")
            for ft in range(NFT):
                self.sop(lambda s, ft=ft, ch2=ch2: s.activation(
                    h_t(ft), self.vu_ft(ft)[:, ch2 * CH2:(ch2 + 1) * CH2], AF.Silu,
                    scale=g[:, GO["alpha"] + ft:GO["alpha"] + ft + 1],
                    bias=g[:, GO["beta"] + ft:GO["beta"] + ft + 1]))
                if ch2 == 0:
                    self.sop(lambda s, ft=ft: s.activation(
                        h_t(ft)[:, 0:1], self.vu_ft(ft)[:, 0:1], AF.Silu,
                        scale=g[:, GO["alpha"] + ft:GO["alpha"] + ft + 1],
                        bias=g[:, GO["beta0"] + ft:GO["beta0"] + ft + 1]))
                if ch2 == NCH2 - 1:
                    self.sop(lambda s, ft=ft: s.activation(
                        h_t(ft)[:, CH2 - 1:CH2], self.vu_ft(ft)[:, L - 1:L], AF.Silu,
                        scale=g[:, GO["alpha"] + ft:GO["alpha"] + ft + 1],
                        bias=g[:, GO["betaL"] + ft:GO["betaL"] + ft + 1]))
                elif ft == NFT - 1:
                    self.sop(lambda s: s.copy(ev1[:, 0:1], h_t(0)[:, 0:1]), inc=1)
            h_cnt[ch2] = self.cnt["s"]

        def emit_bmd(ch2):
            self.wait("tensor", "s", h_cnt[ch2])
            stg = s32[0:NST, self.TO["sum32"]:self.TO["sum32"] + 512]
            for (gso, bof, ro) in (("wBT", GO["bB"], "Bm"), ("wCT", GO["bC"], "Cm")):
                for gg in range(CH2 // 512):
                    self.mwait("tensor", "p2_ps1")
                    for kt in range(NFT):
                        self.peop(lambda t, gso=gso, kt=kt, gg=gg, ch2=ch2: t.matmul(
                            self.PS[1][0:NST, 0:512],
                            self.GS[:, self.GSO[gso] + kt * NST: self.GSO[gso] + (kt + 1) * NST],
                            h_t(kt)[:, gg * 512:(gg + 1) * 512],
                            start=(kt == 0), stop=(kt == NFT - 1)))
                    self.wait("vector", "pe")
                    self.mwait("vector", "p2_stg")
                    self.vop(lambda v, bof=bof, stg=stg: v.tensor_scalar(
                        stg, self.PS[1][0:NST, 0:512], g[0:NST, bof:bof + 1], None, OP.add),
                        inc=1)
                    self.mark("p2_ps1", "v")
                    self.wait("sync", "v")
                    ro_ap = self.rows_t[self.RO[ro]:self.RO[ro] + NST * L].rearrange(
                        "(n l) -> n l", n=NST)[:, ch2 * CH2 + gg * 512: ch2 * CH2 + (gg + 1) * 512]
                    self.store(ro_ap, stg, nm="str")
                    self.mark("p2_stg", "str")
            for gg in range(CH2 // 512):
                self.mwait("tensor", "p2_ps1")
                for kt in range(NFT):
                    self.peop(lambda t, kt=kt, gg=gg: t.matmul(
                        self.PS[1][0:NDLT, 0:512],
                        self.GS[:, self.GSO["wd1T"] + kt * NDLT: self.GSO["wd1T"] + (kt + 1) * NDLT],
                        h_t(kt)[:, gg * 512:(gg + 1) * 512],
                        start=(kt == 0), stop=(kt == NFT - 1)))
                self.wait("vector", "pe")
                self.mwait("vector", "p2_u2s")
                self.vop(lambda v: v.tensor_scalar(
                    u2s[0:NDLT, :], self.PS[1][0:NDLT, 0:512],
                    g[0:NDLT, GO["b_d1"]:GO["b_d1"] + 1], None, OP.add), inc=1)
                self.mark("p2_ps1", "v")
                u2done = self.cnt["v"]
                self.wait("tensor", "v", u2done)
                for mt in range(NFT):
                    ps = self.PS[0]
                    self.mwait("tensor", "p2_dpsa")
                    self.peop(lambda t, mt=mt, ps=ps: t.matmul(
                        ps[:, 0:512],
                        self.GS[0:NDLT, self.GSO["wd2T"] + mt * 128: self.GSO["wd2T"] + (mt + 1) * 128],
                        u2s[0:NDLT, 0:512], start=True, stop=True), inc=1)
                    if mt == NFT - 1:
                        self.mark("p2_u2s", "pe")
                    self.wait("scalar", "pe")
                    self.mwait("scalar", "p2_dev")
                    self.sop(lambda s, mt=mt, ps=ps: s.activation(
                        ev1, ps[:, 0:512], AF.Abs,
                        bias=g[:, GO["b_d2"] + mt:GO["b_d2"] + mt + 1]))
                    self.sop(lambda s: s.activation(ev2, ev1, AF.Exp, scale=-1.0))
                    self.sop(lambda s: s.activation(ev2, ev2, AF.Ln, bias=1.0), inc=1)
                    self.wait("vector", "s")
                    self.vop(lambda v, mt=mt, ps=ps: v.tensor_scalar(
                        ev1, ps[:, 0:512], g[:, GO["b_d2"] + mt:GO["b_d2"] + mt + 1], 0.0,
                        OP.add, OP.max))
                    self.vop(lambda v, mt=mt, gg=gg: v.tensor_tensor(
                        d_t(mt)[:, gg * 512:(gg + 1) * 512], ev1, ev2, OP.add), inc=1)
                    self.mark("p2_dpsa", "v")
                    self.mark("p2_dev", "v")
            h_pe_free[ch2 + 1] = self.cnt["pe"]
        emit_h(0)
        emit_bmd(0)
        for ch in range(NCH2):
            self.wait("vector", "pe")
            self.wait("scalar", "pe")
            self.wait("vector", "s", h_cnt[ch])
            for ft in range(NFT):
                self.vop(lambda v, ft=ft: v.tensor_tensor(dh_t(ft), d_t(ft), h_t(ft), OP.mult))
                self.vop(lambda v, ft=ft: v.tensor_scalar(
                    y_t(ft), h_t(ft), g[:, GO["D"] + ft:GO["D"] + ft + 1], None, OP.mult))
            dh_done = self.cnt["v"]
            self.mark("p2_hfree", "v")
            self.mwait("scalar", "p2_evdone")
            self.mwait("vector", "p2_evdone")
            self.wait("gpsimd", "str")
            self.wait("scalar", "v", dh_done)
            NPV = 16
            SO1, TMPP, YPO = 28672, 29184, 24576
            so_b = [so_, S[:, SO1:SO1 + CH2]]
            tm_p = S[:, TMPP:TMPP + CH2]
            yp_t = lambda ft: S[:, YPO + ft * CH2: YPO + (ft + 1) * CH2]
            bcp6 = bcp + [S[:, SO1 + 1024 + i * CH2: SO1 + 1024 + (i + 1) * CH2]
                          for i in range(2)]
            bc_cnt = {}
            BCS = ("g0", "g1", "l3")

            def issue_bc(n):
                bmb = bcp6[2 * (n % 3)]
                cmb = bcp6[2 * (n % 3) + 1]
                gsem = BCS[n % 3]
                self.mwait("gpsimd", "p2_bcv%d" % (n % 3))
                self.mwait("gpsimd", "p2_bcp%d" % (n % 3))
                self.dma("gpsimd", gsem, bmb, self.rows_t[
                    self.RO["Bm"] + n * L + ch * CH2: self.RO["Bm"] + n * L + (ch + 1) * CH2]
                    .unsqueeze(0).partition_broadcast(128))
                self.dma("gpsimd", gsem, cmb, self.rows_t[
                    self.RO["Cm"] + n * L + ch * CH2: self.RO["Cm"] + n * L + (ch + 1) * CH2]
                    .unsqueeze(0).partition_broadcast(128))
                bc_cnt[n] = self.cnt[gsem]
            self.mwait("gpsimd", "p2_ypfree")
            issue_bc(0)
            issue_bc(1)
            it = 0
            for n in range(NST):
                bmb = bcp6[2 * (n % 3)]
                cmb = bcp6[2 * (n % 3) + 1]
                gsem = BCS[n % 3]
                if n + 2 < NST:
                    issue_bc(n + 2)
                self.wait("vector", gsem, bc_cnt[n])
                on_pool = n >= NPV
                if on_pool:
                    self.wait("gpsimd", gsem, bc_cnt[n])
                for ft in range(NFT):
                    ab = ab_[it % 2]
                    so_c = so_b[it % 2]
                    self.mwait("scalar", "p2_ab%d" % (it % 2))
                    self.sop(lambda s, ab=ab, ft=ft, n=n: s.activation(
                        ab, d_t(ft), AF.Exp,
                        scale=g[:, GO["negA"] + ft * NST + n: GO["negA"] + ft * NST + n + 1]),
                        inc=1)
                    sdone = self.cnt["s"]
                    self.wait("vector", "s", sdone)
                    self.mwait("vector", "p2_so%d" % (it % 2))
                    self.vop(lambda v, ft=ft, bmb=bmb: v.tensor_tensor(bx_, dh_t(ft), bmb, OP.mult))
                    ini = 0.0 if ch == 0 else self.CAR[:, n * NFT + ft:n * NFT + ft + 1]
                    self.vop(lambda v, ab=ab, ini=ini, so_c=so_c: v.tensor_tensor_scan(
                        so_c, ab, bx_, ini, OP.mult, OP.add), inc=1)
                    self.mark("p2_ab%d" % (it % 2), "v")
                    tts_v = self.cnt["v"]
                    if ch < NCH2 - 1:
                        self.vop(lambda v, n=n, ft=ft, so_c=so_c: v.tensor_copy(
                            self.CAR[:, n * NFT + ft:n * NFT + ft + 1], so_c[:, CH2 - 1:CH2]))
                    if on_pool:
                        self.wait("gpsimd", "v", tts_v)
                        if n == NPV:
                            self.gop(lambda p, ft=ft, so_c=so_c, cmb=cmb: p.tensor_tensor(
                                yp_t(ft), so_c, cmb, OP.mult))
                        else:
                            self.gop(lambda p, so_c=so_c, cmb=cmb: p.tensor_tensor(
                                tm_p, so_c, cmb, OP.mult))
                            self.gop(lambda p, ft=ft: p.tensor_tensor(
                                yp_t(ft), yp_t(ft), tm_p, OP.add))
                        self.mark("p2_so%d" % (it % 2), "p")
                        if ft == NFT - 1:
                            self.mark("p2_bcp%d" % (n % 3), "p")
                    else:
                        self.vop(lambda v, cmb=cmb, so_c=so_c: v.tensor_tensor(
                            tm_, so_c, cmb, OP.mult))
                        self.vop(lambda v, ft=ft: v.tensor_tensor(y_t(ft), y_t(ft), tm_, OP.add))
                    it += 1
                self.mark("p2_bcv%d" % (n % 3), "v")
                if not on_pool:
                    self.mark("p2_bcp%d" % (n % 3), "p")
            if NPV < NST:
                pool_done = self.cnt["p"]
                self.wait("vector", "p", pool_done)
                for ft in range(NFT):
                    self.vop(lambda v, ft=ft: v.tensor_tensor(y_t(ft), y_t(ft), yp_t(ft), OP.add))
            merge_done = self.cnt["v"]
            if ch + 1 < NCH2:
                emit_h(ch + 1)
            for ft in range(NFT):
                self.sop(lambda s, ft=ft, ch=ch: s.activation(
                    d_t(ft), self.u_ft(ft)[:, 1 + ch * CH2: 1 + (ch + 1) * CH2], AF.Silu,
                    scale=g[:, GO["rb"]:GO["rb"] + 1],
                    bias=g[:, GO["b_in"] + ft:GO["b_in"] + ft + 1]))
            self.wait("vector", "s")
            for ft in range(NFT):
                self.vop(lambda v, ft=ft: v.tensor_tensor(dh_t(ft), y_t(ft), d_t(ft), OP.mult))
            g_done = self.cnt["v"]
            g_t = dh_t
            self.wait("tensor", "v", g_done)
            self.wait("scalar", "v", g_done)
            ncol = CH2 // 128
            for tb in range(ncol):
                for ft in range(NFT):
                    self.peop(lambda t, ft=ft, tb=tb: t.matmul(
                        self.PS[4][:, tb:tb + 1], g_t(ft)[:, tb * 128:(tb + 1) * 128],
                        self.ONE[:, 0:1], start=(ft == 0), stop=(ft == NFT - 1)))
            for ft in range(NFT):
                self.mwait("scalar", "p2_gsq")
                self.sop(lambda s, ft=ft: s.activation(gsq, g_t(ft), AF.Square), inc=1)
                self.wait("tensor", "s")
                for tb in range(ncol):
                    self.peop(lambda t, ft=ft, tb=tb, ncol=ncol: t.matmul(
                        self.PS[5][:, ft * ncol + tb: ft * ncol + tb + 1],
                        gsq[:, tb * 128:(tb + 1) * 128],
                        self.ONE[:, 0:1], start=True, stop=True))
                self.mark("p2_gsq", "pe")
            self.wait("vector", "pe")
            sg = s32[:, GN:GN + ncol]
            sgg = s32[:, GN + 8:GN + 8 + ncol]
            mug = s32[:, GN + 16:GN + 16 + ncol]
            rsg = s32[:, GN + 24:GN + 24 + ncol]
            mrs = s32[:, GN + 32:GN + 32 + ncol]
            tg = s32[:, GN + 40:GN + 40 + ncol]
            self.vop(lambda v: v.tensor_copy(sg, self.PS[4][:, 0:ncol]))
            self.vop(lambda v: v.tensor_copy(sgg, self.PS[5][:, 0:ncol]))
            for ft in range(1, NFT):
                self.vop(lambda v, ft=ft, ncol=ncol: v.tensor_tensor(
                    sgg, sgg, self.PS[5][:, ft * ncol:(ft + 1) * ncol], OP.add))
            self.vop(lambda v: v.tensor_copy(sg, sg), inc=1)
            self.wait("tensor", "v")
            self.vop(lambda v: v.tensor_scalar_mul(mug, sg, 1.0 / FD))
            self.vop(lambda v: v.tensor_scalar_mul(tg, sgg, 1.0 / FD))
            self.vop(lambda v: v.tensor_tensor(rsg, mug, mug, OP.mult))
            self.vop(lambda v: v.tensor_tensor(tg, tg, rsg, OP.subtract), inc=1)
            self.wait("scalar", "v")
            self.sop(lambda s: s.activation(tg, tg, AF.Sqrt, bias=g[:, GO["eps"]:GO["eps"] + 1]), inc=1)
            self.wait("vector", "s")
            self.vop(lambda v: v.reciprocal(rsg, tg))
            self.vop(lambda v: v.tensor_tensor(mrs, mug, rsg, OP.mult), inc=1)
            self.wait("tensor", "v")
            self.peop(lambda t: t.transpose(self.PT32[0:16, 0:128], s32[:, GN + 24:GN + 40],
                                            self.IDN32[:]), inc=1)
            self.wait("vector", "pe")
            rowst = s32[0:16, self.TO["rowst"]:self.TO["rowst"] + 128]
            self.vop(lambda v, rowst=rowst: v.tensor_copy(rowst, self.PT32[0:16, 0:128]), inc=1)
            self.wait("sync", "v")
            self.store(self.row("rsg", ch * CH2, (ch + 1) * CH2), rowst[0:ncol, :], nm="str")
            self.store(self.row("mrsg", ch * CH2, (ch + 1) * CH2), rowst[8:8 + ncol, :], nm="str")
            rows_st = self.cnt["str"]
            if ch + 1 < NCH2:
                emit_bmd(ch + 1)
            self.wait("gpsimd", "str", rows_st)
            self.mwait("gpsimd", "p2_bc0")
            self.mwait("gpsimd", "p2_bc1")
            rsb = self.SF[:, 0:CH2]
            mrb = self.SF[:, CH2:2 * CH2]
            a1b = self.SF[:, 2 * CH2:3 * CH2]
            b0b = self.SF[:, 3 * CH2:4 * CH2]
            self.dma("gpsimd", "g0", rsb, self.row("rsg", ch * CH2, (ch + 1) * CH2).partition_broadcast(128))
            self.dma("gpsimd", "g0", mrb, self.row("mrsg", ch * CH2, (ch + 1) * CH2).partition_broadcast(128))
            self.dma("gpsimd", "g0", a1b, self.row("a0", ch * CH2, (ch + 1) * CH2).partition_broadcast(128))
            self.dma("gpsimd", "g0", b0b, self.row("b0", ch * CH2, (ch + 1) * CH2).partition_broadcast(128))
            self.wait("vector", "g0")
            self.vop(lambda v, a1b=a1b: v.tensor_scalar_add(a1b, a1b, 1.0), inc=1)
            self.wait("sync", "v", merge_done)
            for ft in range(NFT):
                self.dma("sync", "l2", yp_t(ft),
                         self.x16[b, ch * CH2:(ch + 1) * CH2, ft * 128:(ft + 1) * 128],
                         transpose=True)
            xres_ld = self.cnt["l2"]
            x_t = yp_t
            for mt in range(NFT):
                ring = wring[mt % 2]
                lsem = "l%d" % (mt % 2)
                self.mwait("sync", "p2_wr%d" % (mt % 2))
                self.dma("sync", lsem, ring, self.ws_out[mt])
                self.wait("tensor", lsem)
                for gg in range(CH2 // 512):
                    ps = self.PS[2 + (mt % 2)]
                    self.mwait("tensor", "p2_wps%d" % (mt % 2))
                    for kt in range(NFT):
                        self.peop(lambda t, kt=kt, gg=gg, ring=ring, ps=ps: t.matmul(
                            ps[:, 0:512], ring[:, kt * 128:(kt + 1) * 128],
                            g_t(kt)[:, gg * 512:(gg + 1) * 512],
                            start=(kt == 0), stop=(kt == NFT - 1)))
                    if gg == CH2 // 512 - 1:
                        self.mark("p2_wr%d" % (mt % 2), "pe")
                    self.wait("vector", "pe")
                    if mt == 0 and gg == 0:
                        self.wait("vector", "l2", xres_ld)
                    gsl = slice(gg * 512, (gg + 1) * 512)
                    e132 = self.SF[:, 4 * CH2:4 * CH2 + 512]
                    e232 = self.SF[:, 4 * CH2 + 512:4 * CH2 + 1024]
                    self.mwait("vector", "p2_ev")
                    self.vop(lambda v, ps=ps, rsb=rsb, gsl=gsl, e132=e132: v.tensor_tensor(
                        e132, ps[:, 0:512], rsb[:, gsl], OP.mult))
                    self.vop(lambda v, mt=mt, mrb=mrb, gsl=gsl, e132=e132: v.scalar_tensor_tensor(
                        e132, mrb[:, gsl], g[:, GO["nwsum"] + mt:GO["nwsum"] + mt + 1], e132,
                        OP.mult, OP.add), inc=1)
                    self.mark("p2_wps%d" % (mt % 2), "v")
                    self.vop(lambda v, mt=mt, a1b=a1b, gsl=gsl, e232=e232: v.tensor_tensor(
                        e232, x_t(mt)[:, gsl], a1b[:, gsl], OP.mult))
                    self.vop(lambda v, b0b=b0b, gsl=gsl, e232=e232: v.tensor_tensor(
                        e232, e232, b0b[:, gsl], OP.add))
                    self.vop(lambda v, mt=mt, ch=ch, gsl=gsl, e132=e132, e232=e232:
                             v.scalar_tensor_tensor(
                        self.X1[:, mt * L + ch * CH2 + gsl.start: mt * L + ch * CH2 + gsl.stop],
                        e132, g[:, GO["b_out"] + mt:GO["b_out"] + mt + 1], e232, OP.add, OP.add),
                        inc=1)
                    self.mark("p2_ev", "v")
            self.mark("p2_ypfree", "v")
            self.mark("p2_bc0", "v")
            self.mark("p2_bc1", "v")
            self.mark("p2_evdone", "v")
            self.flush_all()
        self.tap("h", S[:, O["dH"]:O["dH"] + CH2], (128, CH2))
        self.tap("y", S[:, O["y"]:O["y"] + CH2], (128, CH2))
        self.tap("x1", self.X1[:, 0:L], (128, L))
        self.tap("x1full", self.X1[:], (128, NFT * L))

    def ffn(self, b):
        g = self.G32
        GO = self.GO
        S = self.S
        s32 = self.S32
        FF = self.TO["ff"]
        O = {"xq": 0, "gu": 4096, "wq": 20480, "tq": 28672, "amax": 30208,
             "ev": 30720, "sq": 31744}
        xq_t = lambda kt: S[:, O["xq"] + kt * TFC: O["xq"] + (kt + 1) * TFC]
        gu_t = lambda mt: S[:, O["gu"] + mt * TFC: O["gu"] + (mt + 1) * TFC]
        wq_ring = [S[:, O["wq"] + i * 1024: O["wq"] + (i + 1) * 1024] for i in range(8)]
        tq = [S[:, O["tq"] + i * 512: O["tq"] + (i + 1) * 512] for i in range(3)]
        amax = S[:, O["amax"]:O["amax"] + 512]
        ev1 = S[:, O["ev"]:O["ev"] + 512]
        ev2 = S[:, O["ev"] + 512:O["ev"] + 1024]
        sqb = S[:, O["sq"]:O["sq"] + 512]
        aq_b = [self.SF[:, 0:512], self.SF[:, 512:1024]]
        bq_b = [self.SF[:, 1024:1536], self.SF[:, 1536:2048]]
        cg_b = [self.SF[:, 2048:2560], self.SF[:, 2560:3072]]
        cu_b = [self.SF[:, 3072:3584], self.SF[:, 3584:4096]]
        s232 = self.SF[:, 4096:4608]
        ev32 = self.SF[:, 4608:5120]
        cdb32 = self.SF[:, 5120:5632]
        aq16 = S[:, 32256:32768]
        bq16 = S[:, 32768:33280]
        s2b16 = tq[0]
        sum32 = s32[:, self.TO["sum32"]:self.TO["sum32"] + 512]
        otm = [s32[:, self.TO["otm"] + i * 512: self.TO["otm"] + (i + 1) * 512] for i in range(2)]
        self.wait("sync", "sts0")
        self.wait("sync", "sts1")
        rowst = s32[0:16, self.TO["rowst"]:self.TO["rowst"] + 128]
        tt = s32[:, FF + 52:FF + 56]
        ab_cnt = {}
        xq_cnt = {}

        def emit_stats(tc2, vguard):
            x1s = lambda ft, tc2=tc2: self.X1[:, ft * L + tc2 * TFC: ft * L + (tc2 + 1) * TFC]
            self.wait("scalar", "v", vguard)
            self.mwait("scalar", "ff_gps0")
            self.mwait("scalar", "ff_gps1")
            for ft in range(NFT):
                self.mwait("scalar", "ff_sq")
                self.sop(lambda s, ft=ft, x1s=x1s: s.activation(sqb, x1s(ft), AF.Square), inc=1)
                self.wait("tensor", "s")
                for tb in range(4):
                    self.peop(lambda t, ft=ft, tb=tb, x1s=x1s: t.matmul(
                        self.PS[0][:, ft * 4 + tb: ft * 4 + tb + 1],
                        x1s(ft)[:, tb * 128:(tb + 1) * 128],
                        self.ONE[:, 0:1], start=True, stop=True))
                for tb in range(4):
                    self.peop(lambda t, ft=ft, tb=tb: t.matmul(
                        self.PS[1][:, ft * 4 + tb: ft * 4 + tb + 1],
                        sqb[:, tb * 128:(tb + 1) * 128],
                        self.ONE[:, 0:1], start=True, stop=True))
                self.mark("ff_sq", "pe")
            self.mwait("vector", "ff_ev")
            self.vop(lambda v, x1s=x1s: v.tensor_tensor(ev1, x1s(0), x1s(1), OP.max))
            self.vop(lambda v, x1s=x1s: v.tensor_tensor(ev2, x1s(0), x1s(1), OP.min))
            for ft in range(2, NFT):
                self.vop(lambda v, ft=ft, x1s=x1s: v.tensor_tensor(ev1, ev1, x1s(ft), OP.max))
                self.vop(lambda v, ft=ft, x1s=x1s: v.tensor_tensor(ev2, ev2, x1s(ft), OP.min))
            mmx_done = self.cnt["v"]
            self.wait("tensor", "v", mmx_done)
            for blk in range(4):
                self.peop(lambda t, blk=blk: t.transpose(
                    self.PT16[:, 0:128], ev1[:, blk * 128:(blk + 1) * 128], self.IDN[:]), inc=1)
                self.wait("vector", "pe")
                self.vop(lambda v, blk=blk: v.tensor_reduce(
                    s32[:, FF + 8 + blk:FF + 9 + blk], self.PT16[:, 0:128], AX.X, OP.max), inc=1)
                self.wait("tensor", "v")
                self.peop(lambda t, blk=blk: t.transpose(
                    self.PT16[:, 0:128], ev2[:, blk * 128:(blk + 1) * 128], self.IDN[:]), inc=1)
                self.wait("vector", "pe")
                self.vop(lambda v, blk=blk: v.tensor_reduce(
                    s32[:, FF + 12 + blk:FF + 13 + blk], self.PT16[:, 0:128], AX.X, OP.min),
                    inc=1)
                self.wait("tensor", "v")
            self.mark("ff_ev", "v")
            sx = s32[:, FF:FF + 4]
            sxx = s32[:, FF + 4:FF + 8]
            mx = s32[:, FF + 8:FF + 12]
            nmn = s32[:, FF + 12:FF + 16]
            mu = s32[:, FF + 16:FF + 20]
            var = s32[:, FF + 20:FF + 24]
            rst = s32[:, FF + 24:FF + 28]
            am = s32[:, FF + 28:FF + 32]
            s1 = s32[:, FF + 32:FF + 36]
            aq = s32[:, FF + 36:FF + 40]
            bq2 = s32[:, FF + 40:FF + 44]
            cgc = s32[:, FF + 44:FF + 48]
            cuc = s32[:, FF + 48:FF + 52]
            tt = s32[:, FF + 52:FF + 56]
            self.vop(lambda v: v.tensor_copy(sx, self.PS[0][:, 0:4]))
            self.vop(lambda v: v.tensor_copy(sxx, self.PS[1][:, 0:4]))
            for ft in range(1, NFT):
                self.vop(lambda v, ft=ft: v.tensor_tensor(
                    sx, sx, self.PS[0][:, ft * 4:(ft + 1) * 4], OP.add))
                self.vop(lambda v, ft=ft: v.tensor_tensor(
                    sxx, sxx, self.PS[1][:, ft * 4:(ft + 1) * 4], OP.add))
            self.vop(lambda v: v.tensor_copy(sx, sx), inc=1)
            self.mark("ff_gps0", "v")
            self.mark("ff_gps1", "v")
            self.wait("tensor", "v")
            self.vop(lambda v: v.tensor_scalar_mul(mu, sx, 1.0 / FD))
            self.vop(lambda v: v.tensor_scalar_mul(var, sxx, 1.0 / FD))
            self.vop(lambda v: v.tensor_tensor(tt, mu, mu, OP.mult))
            self.vop(lambda v: v.tensor_tensor(var, var, tt, OP.subtract), inc=1)
            self.wait("scalar", "v")
            self.sop(lambda s: s.activation(tt, var, AF.Sqrt, bias=g[:, GO["eps"]:GO["eps"] + 1]), inc=1)
            self.wait("vector", "s")
            self.vop(lambda v: v.reciprocal(rst, tt))
            self.vop(lambda v: v.tensor_tensor(am, mx, mu, OP.subtract))
            self.vop(lambda v: v.tensor_tensor(tt, mu, nmn, OP.subtract))
            self.vop(lambda v: v.tensor_tensor(am, am, tt, OP.max))
            self.vop(lambda v: v.tensor_tensor(am, am, rst, OP.mult))
            self.vop(lambda v: v.tensor_scalar_max(am, am, 1e-5))
            self.vop(lambda v: v.reciprocal(tt, am))
            self.vop(lambda v: v.tensor_scalar_mul(s1, tt, 127.0))
            self.vop(lambda v: v.tensor_tensor(aq, rst, s1, OP.mult))
            self.vop(lambda v: v.tensor_tensor(bq2, mu, aq, OP.mult))
            self.vop(lambda v: v.tensor_scalar(bq2, bq2, -1.0, 0.5, OP.mult, OP.add))
            self.vop(lambda v: v.reciprocal(tt, s1))
            self.vop(lambda v: v.tensor_scalar(cgc, tt, g[:, GO["sw"] + 3:GO["sw"] + 4], None, OP.mult))
            self.vop(lambda v: v.tensor_scalar(cuc, tt, g[:, GO["sw"] + 4:GO["sw"] + 5], None, OP.mult),
                     inc=1)
            self.wait("tensor", "v")
            self.peop(lambda t: t.transpose(self.PT32[0:16, 0:128], s32[:, FF + 36:FF + 52],
                                            self.IDN32[:]), inc=1)
            self.wait("vector", "pe")
            rowst = s32[0:16, self.TO["rowst"]:self.TO["rowst"] + 128]
            self.mwait("vector", "ff_rowst")
            self.vop(lambda v, rowst=rowst: v.tensor_copy(rowst, self.PT32[0:16, 0:128]), inc=1)
            self.wait("sync", "v")
            for (i, nm) in ((0, "aq"), (4, "bq2"), (8, "cg"), (12, "cu")):
                self.store(self.row(nm, tc2 * TFC, (tc2 + 1) * TFC), rowst[i:i + 4, :], nm="str")
            self.mark("ff_rowst", "str")
            self.wait("gpsimd", "str", self.cnt["str"])
            self.mwait("gpsimd", "ff_ab%d" % (tc2 % 2))
            for (bt, nm) in ((aq_b, "aq"), (bq_b, "bq2"), (cg_b, "cg"), (cu_b, "cu")):
                self.dma("gpsimd", "g0", bt[tc2 % 2],
                         self.row(nm, tc2 * TFC, (tc2 + 1) * TFC).partition_broadcast(128))
            ab_cnt[tc2] = self.cnt["g0"]

        def emit_xq(tc2):
            x1s = lambda ft, tc2=tc2: self.X1[:, ft * L + tc2 * TFC: ft * L + (tc2 + 1) * TFC]
            aqb, bqb = aq_b[tc2 % 2], bq_b[tc2 % 2]
            self.wait("vector", "g0", ab_cnt[tc2])
            self.vop(lambda v, aqb=aqb: v.tensor_copy(aq16, aqb))
            self.vop(lambda v, bqb=bqb: v.tensor_copy(bq16, bqb), inc=1)
            for ft in range(NFT):
                self.vop(lambda v, ft=ft, x1s=x1s: v.tensor_tensor(tq[2], x1s(ft), aq16, OP.mult))
                self.vop(lambda v: v.tensor_tensor(tq[2], tq[2], bq16, OP.add))
                self.vop(lambda v: v.tensor_copy(self.I16T[:, 0:512], tq[2]))
                self.vop(lambda v: v.tensor_copy(ev2, self.I16T[:, 0:512]))
                self.vop(lambda v: v.tensor_tensor(sqb, ev2, tq[2], OP.is_gt))
                self.vop(lambda v: v.scalar_tensor_tensor(ev1, ev2, 1.0, tq[2], OP.add, OP.is_le))
                self.vop(lambda v: v.tensor_tensor(ev2, ev2, sqb, OP.subtract))
                self.vop(lambda v, ft=ft: v.tensor_tensor(xq_t(ft), ev2, ev1, OP.add))
            xq_cnt[tc2] = self.cnt["v"]
        emit_stats(0, self.cnt["v"])
        emit_xq(0)
        for tc in range(NTFC):
            x1_t = lambda ft: self.X1[:, ft * L + tc * TFC: ft * L + (tc + 1) * TFC]
            self.wait("scalar", "v")
            self.wait("scalar", "pe")
            self.wait("vector", "pe")
            cgb, cub = cg_b[tc % 2], cu_b[tc % 2]
            if tc > 0:
                emit_xq(tc)
            self.wait("tensor", "v", xq_cnt[tc])
            ring_cnt = {}

            def issue_gu(mt):
                lsem = "l%d" % (mt % 4)
                self.mwait("sync", "ff_ring%d" % (mt % 4))
                self.dma("sync", lsem, wq_ring[(2 * mt) % 8], self.ws_g[mt])
                self.dma("sync", lsem, wq_ring[(2 * mt + 1) % 8], self.ws_u[mt])
                ring_cnt[mt] = self.cnt[lsem]
            issue_gu(0)
            issue_gu(1)
            issue_gu(2)
            for mt in range(NMT_I):
                rg = wq_ring[(2 * mt) % 8]
                ru = wq_ring[(2 * mt + 1) % 8]
                lsem = "l%d" % (mt % 4)
                if mt + 3 < NMT_I:
                    issue_gu(mt + 3)
                self.wait("tensor", lsem, ring_cnt[mt])
                psg = self.PS[mt % 2]
                psu = self.PS[2 + mt % 2]
                self.mwait("tensor", "ff_gps%d" % (mt % 2))
                for kt in range(NFT):
                    self.peop(lambda t, kt=kt, rg=rg, psg=psg: t.matmul(
                        psg[:, 0:512], rg[:, kt * 128:(kt + 1) * 128], xq_t(kt),
                        start=(kt == 0), stop=(kt == NFT - 1)))
                for kt in range(NFT):
                    self.peop(lambda t, kt=kt, ru=ru, psu=psu: t.matmul(
                        psu[:, 0:512], ru[:, kt * 128:(kt + 1) * 128], xq_t(kt),
                        start=(kt == 0), stop=(kt == NFT - 1)))
                self.mark("ff_ring%d" % (mt % 4), "pe")
                self.wait("vector", "pe")
                self.mwait("vector", "ff_gev")
                self.vop(lambda v, psg=psg, cgb=cgb: v.tensor_tensor(ev1, psg[:, 0:512], cgb, OP.mult),
                         inc=1)
                self.wait("scalar", "v")
                self.sop(lambda s: s.activation(ev2, ev1, AF.Silu), inc=1)
                self.vop(lambda v, psu=psu, cub=cub: v.tensor_tensor(sqb, psu[:, 0:512], cub, OP.mult),
                         inc=1)
                self.mark("ff_gps%d" % (mt % 2), "v")
                self.wait("vector", "s")
                self.vop(lambda v, mt=mt: v.tensor_tensor(gu_t(mt), ev2, sqb, OP.mult))
                amin = tq[1]
                if mt == 0:
                    self.vop(lambda v: v.tensor_copy(amax, gu_t(0)))
                    self.vop(lambda v, amin=amin: v.tensor_copy(amin, gu_t(0)), inc=1)
                else:
                    self.vop(lambda v, mt=mt: v.tensor_tensor(amax, amax, gu_t(mt), OP.max))
                    self.vop(lambda v, mt=mt, amin=amin: v.tensor_tensor(
                        amin, amin, gu_t(mt), OP.min), inc=1)
            self.vop(lambda v: v.tensor_scalar_mul(tq[1], tq[1], -1.0))
            self.vop(lambda v: v.tensor_tensor(amax, amax, tq[1], OP.max), inc=1)
            gu_done = self.cnt["v"]
            self.mark("ff_cgcu", "v")
            self.mark("ff_s2g", "v")
            self.mark("ff_ab%d" % (tc % 2), "v")
            if tc == NTFC - 1 and ("gupre" in self.taps):
                self.barrier()
                self.tap("gupre", S[:, O["gu"]:O["gu"] + TFC], (128, TFC))
                self.tap("gupre7", S[:, O["gu"] + 7 * TFC:O["gu"] + 8 * TFC], (128, TFC))
                self.barrier()
            self.wait("tensor", "v", gu_done)
            for blk in range(4):
                self.peop(lambda t, blk=blk: t.transpose(
                    self.PT16[:, 0:128], amax[:, blk * 128:(blk + 1) * 128], self.IDN[:]), inc=1)
                self.wait("vector", "pe")
                self.vop(lambda v, blk=blk: v.tensor_reduce(
                    s32[:, FF + 56 + blk:FF + 57 + blk], self.PT16[:, 0:128], AX.X, OP.max), inc=1)
                self.wait("tensor", "v")
            am2 = s32[:, FF + 56:FF + 60]
            s2c = s32[:, FF + 60:FF + 64]
            cdc = s32[:, FF + 64:FF + 68]
            self.vop(lambda v: v.tensor_scalar_max(am2, am2, 1e-5))
            self.vop(lambda v: v.reciprocal(tt, am2))
            self.vop(lambda v: v.tensor_scalar_mul(s2c, tt, 127.0))
            self.vop(lambda v: v.reciprocal(tt, s2c))
            self.vop(lambda v: v.tensor_scalar(cdc, tt, g[:, GO["sw"] + 5:GO["sw"] + 6], None, OP.mult),
                     inc=1)
            self.wait("tensor", "v")
            self.peop(lambda t: t.transpose(self.PT32[0:8, 0:128], s32[:, FF + 60:FF + 68],
                                            self.IDN32[:]), inc=1)
            self.wait("vector", "pe")
            self.mwait("vector", "ff_rowst")
            self.vop(lambda v, rowst=rowst: v.tensor_copy(rowst[0:8, :], self.PT32[0:8, 0:128]),
                     inc=1)
            self.wait("sync", "v")
            self.store(self.row("s2", tc * TFC, (tc + 1) * TFC), rowst[0:4, :], nm="str")
            self.store(self.row("cd", tc * TFC, (tc + 1) * TFC), rowst[4:8, :], nm="str")
            self.mark("ff_rowst", "str")
            r_st = self.cnt["str"]
            self.wait("gpsimd", "str", r_st)
            s2b = s232
            self.mwait("gpsimd", "ff_s2g")
            self.dma("gpsimd", "g0", s2b, self.row("s2", tc * TFC, (tc + 1) * TFC).partition_broadcast(128))
            self.wait("vector", "g0")
            cdb = cdb32
            self.mwait("gpsimd", "ff_cgcu")
            self.dma("gpsimd", "g1", cdb, self.row("cd", tc * TFC, (tc + 1) * TFC).partition_broadcast(128))
            for par in range(4):
                self.mwait("sync", "ff_ring%d" % par)
            self.mwait("sync", "ff_dr0")
            self.mwait("sync", "ff_dr1")
            for q in range(4):
                self.dma("sync", "l0", wq_ring[2 * q], self.ws_d[0, :, q * 1024:(q + 1) * 1024])
                self.dma("sync", "l1", wq_ring[2 * q + 1], self.ws_d[1, :, q * 1024:(q + 1) * 1024])
            self.wait("vector", "s")
            self.vop(lambda v, s2b=s2b: v.tensor_copy(s2b16, s2b), inc=1)
            gq_cnt = []
            for mt in range(NMT_I):
                self.vop(lambda v, mt=mt: v.tensor_tensor(tq[2], gu_t(mt), s2b16, OP.mult))
                self.vop(lambda v: v.tensor_single_scalar(tq[2], tq[2], 0.5, OP.add))
                self.vop(lambda v: v.tensor_copy(self.I16T[:, 0:512], tq[2]))
                self.vop(lambda v: v.tensor_copy(ev2, self.I16T[:, 0:512]))
                self.vop(lambda v: v.tensor_tensor(sqb, ev2, tq[2], OP.is_gt))
                self.vop(lambda v: v.scalar_tensor_tensor(ev1, ev2, 1.0, tq[2], OP.add, OP.is_le))
                self.vop(lambda v: v.tensor_tensor(ev2, ev2, sqb, OP.subtract))
                self.vop(lambda v, mt=mt: v.tensor_tensor(gu_t(mt), ev2, ev1, OP.add))
                gq_cnt.append(self.cnt["v"])
            gq_done = self.cnt["v"]
            gq_t = gu_t
            self.wait("vector", "g1")

            def epi(mtm, ps):
                self.wait("vector", "pe")
                self.mwait("vector", "ff_sum")
                self.vop(lambda v, ps=ps, cdb=cdb: v.tensor_tensor(
                    sum32, ps[:, 0:512], cdb, OP.mult), inc=1)
                self.mark("ff_dps%d" % (mtm % 2), "v")
                self.vop(lambda v, mtm=mtm, tc=tc: v.tensor_tensor(
                    sum32, sum32,
                    self.X1[:, mtm * L + tc * TFC: mtm * L + (tc + 1) * TFC], OP.add), inc=1)
                sum_done = self.cnt["v"]
                self.wait("tensor", "v", sum_done)
                for tb in range(4):
                    self.peop(lambda t, tb=tb: t.transpose(
                        self.PT32[:, tb * 128:(tb + 1) * 128], sum32[:, tb * 128:(tb + 1) * 128],
                        self.IDN32[:]))
                self.mark("ff_sum", "pe")
                self.wait("vector", "pe")
                ot = otm[mtm % 2]
                self.wait("vector", "sto0")
                self.wait("vector", "sto1")
                self.vop(lambda v, ot=ot: v.tensor_copy(ot, self.PT32[:, 0:512]), inc=1)
                self.wait("tensor", "v")
                self.wait("sync", "v")
                dst = self.out_h[b, tc * TFC:(tc + 1) * TFC, mtm * 128:(mtm + 1) * 128] \
                    .rearrange("(tb p) c -> p tb c", p=128)
                osrc = ot.rearrange("p (tb c) -> p tb c", c=128)
                self.store(dst, osrc, nm="sto%d" % (mtm % 2))
                self.mark("ff_ot%d" % (mtm % 2), "sto%d" % (mtm % 2))

            self.wait("tensor", "l0")
            self.wait("tensor", "l1")
            self.mwait("tensor", "ff_dps0")
            self.mwait("tensor", "ff_dps1")
            for kt in range(NMT_I):
                self.wait("tensor", "v", gq_cnt[kt])
                for w in range(2):
                    ring = wq_ring[2 * (kt // 8) + w]
                    self.peop(lambda t, kt=kt, ring=ring, w=w: t.matmul(
                        self.PS[4 + w][:, 0:512], ring[:, (kt % 8) * 128:((kt % 8) + 1) * 128],
                        gq_t(kt), start=(kt == 0), stop=(kt == NMT_I - 1)))
            self.mark("ff_dr0", "pe")
            self.mark("ff_dr1", "pe")

            def issue_dn(mtm):
                lsem = "l%d" % (mtm % 2)
                self.mwait("sync", "ff_dr%d" % (mtm % 2))
                for q in range(4):
                    self.dma("sync", lsem, wq_ring[2 * q + mtm % 2],
                             self.ws_d[mtm, :, q * 1024:(q + 1) * 1024])
                ring_cnt[("d", mtm)] = self.cnt[lsem]
            issue_dn(2)
            issue_dn(3)
            epi(0, self.PS[4])
            epi(1, self.PS[5])
            if tc + 1 < NTFC:
                emit_stats(tc + 1, gq_done)
            for mtm in range(2, NFT):
                lsem = "l%d" % (mtm % 2)
                self.wait("tensor", lsem, ring_cnt[("d", mtm)])
                ps = self.PS[4 + (mtm % 2)]
                self.mwait("tensor", "ff_dps%d" % (mtm % 2))
                for kt in range(NMT_I):
                    ring = wq_ring[2 * (kt // 8) + mtm % 2]
                    self.peop(lambda t, kt=kt, ring=ring, ps=ps: t.matmul(
                        ps[:, 0:512], ring[:, (kt % 8) * 128:((kt % 8) + 1) * 128], gq_t(kt),
                        start=(kt == 0), stop=(kt == NMT_I - 1)))
                self.mark("ff_dr%d" % (mtm % 2), "pe")
                if mtm + 2 < NFT:
                    issue_dn(mtm + 2)
                epi(mtm, ps)
            self.flush_all()
        self.tap("xq", S[:, O["xq"]:O["xq"] + TFC], (128, TFC))
        self.tap("xq7", S[:, O["xq"] + 7 * TFC:O["xq"] + 8 * TFC], (128, TFC))
        self.tap("m1", sqb, (128, TFC))
        self.tap("m2", tq[2], (128, TFC))
        self.tap("uev", ev32, (128, TFC))
        self.tap("fev", ev2, (128, TFC))
        self.tap("gu", S[:, O["gu"]:O["gu"] + TFC], (128, TFC))
        self.tap("amax", amax, (128, TFC))
        self.tap("s2row", self.row("s2", 0, L), (1, L))
        self.tap("cgrow", self.row("cg", 0, L), (1, L))

    def finish(self):
        for nm in ("sto0", "sto1", "str", "sts0", "sts1", "l0", "l1", "l2", "l3",
                   "lw", "g0", "g1"):
            self.wait("gpsimd", nm)
        self.flush_all()
        for cm in reversed(self._ctx):
            cm.__exit__(None, None, None)


_BLD_CACHE = {}


def build(upto="full", taps=()):
    key = (upto, tuple(sorted(taps)))
    if key not in _BLD_CACHE:
        _BLD_CACHE[key] = B(upto=upto, taps=taps)
    return _BLD_CACHE[key]


def make_in_maps(inputs):
    x = np.ascontiguousarray(inputs["L0_attn_x"], dtype=np.float32)
    wmap = {}
    for k, v in inputs.items():
        if k == "L0_attn_x":
            continue
        nm = k.replace("L0_attn_", "").replace("L0_ffn_", "")
        wmap[nm] = np.ascontiguousarray(v, dtype=np.float32)
    in_maps = []
    for i in range(8):
        m = {"x": x[2 * i:2 * i + 2]}
        m.update(wmap)
        in_maps.append(m)
    return in_maps


def kernel(**inputs):
    bld = build()
    in_maps = make_in_maps(inputs)
    res = run_bass_kernel_spmd(bld.nc, in_maps, core_ids=list(range(8)))
    return np.concatenate([r["out"] for r in res.results], axis=0)


# version 32
# speedup vs baseline: 1.0073x; 1.0016x over previous
import numpy as np
import concourse.bass as bass
import concourse.mybir as mybir
from concourse.bass_utils import run_bass_kernel_spmd

F32 = mybir.dt.float32
F16 = mybir.dt.float16
I16 = mybir.dt.int16
AF = mybir.ActivationFunctionType
OP = mybir.AluOpType
AX = mybir.AxisListType

NB = 2
L = 2048
FD = 1024
NFT = 8
NST = 16
NDLT = 64
INNER = 4096
NMT_I = 32
CH1 = 512
NCH1 = L // CH1
CH2 = 512
NCH2 = L // CH2
TFC = 512
NTFC = L // TFC
EPS = 1e-5
ENGS = ("tensor", "vector", "scalar", "gpsimd", "sync")


class B:
    def __init__(self, upto="full", taps=()):
        self.upto = upto
        self.taps = set(taps)
        self.tapd = {}
        self.nc = bass.Bass("TRN2", target_bir_lowering=False)
        self._ctx = []
        self.mk = {}
        self._pend = {}
        self._io()
        self._alloc()
        self._prog()

    def _io(self):
        nc = self.nc
        self.x_h = nc.dram_tensor("x", [NB, L, FD], F32, kind="ExternalInput").ap()
        shp = {"w_in": [FD, FD], "b_in": [FD], "conv_w": [FD, FD, 3], "conv_b": [FD],
               "A_log": [FD, NST], "D": [FD], "w_B": [NST, FD], "b_B": [NST],
               "w_C": [NST, FD], "b_C": [NST], "w_d1": [NDLT, FD], "b_d1": [NDLT],
               "w_d2": [FD, NDLT], "b_d2": [FD], "w_out": [FD, FD], "b_out": [FD],
               "w_gate": [INNER, FD], "w_up": [INNER, FD], "w_down": [FD, INNER]}
        self.w_h = {n: nc.dram_tensor(n, s, F32, kind="ExternalInput").ap() for n, s in shp.items()}
        self.out_h = nc.dram_tensor("out", [NB, L, FD], F32, kind="ExternalOutput").ap()
        self.x16 = nc.dram_tensor("x16_scr", [NB, L, FD], F16, kind="Internal").ap()
        self.ws_in = nc.dram_tensor("ws_in", [128, NFT * NFT * 128], F16, kind="Internal").ap()
        self.ws_conv = nc.dram_tensor("ws_conv", [NFT, 128, 3 * NFT * 128], F16, kind="Internal").ap()
        self.ws_out = nc.dram_tensor("ws_out", [NFT, 128, NFT * 128], F16, kind="Internal").ap()
        self.ws_g = nc.dram_tensor("ws_g", [NMT_I, 128, NFT * 128], F16, kind="Internal").ap()
        self.ws_u = nc.dram_tensor("ws_u", [NMT_I, 128, NFT * 128], F16, kind="Internal").ap()
        self.ws_d = nc.dram_tensor("ws_d", [NFT, 128, NMT_I * 128], F16, kind="Internal").ap()
        self.rows_t = nc.dram_tensor("rows_scr", [1 << 17], F32, kind="Internal").ap()
        self.RO = {}
        o = 0
        for nm, sz in (("a0", L), ("b0", L), ("Bm", NST * L), ("Cm", NST * L),
                       ("rsg", L), ("mrsg", L), ("aq", L), ("bq2", L), ("cg", L),
                       ("cu", L), ("s2", L), ("cd", L), ("rb", 8), ("sw", 8)):
            self.RO[nm] = o
            o += sz

    def row(self, nm, a, b2):
        return self.rows_t[self.RO[nm] + a: self.RO[nm] + b2].unsqueeze(0)

    def _alloc(self):
        nc = self.nc
        def sb(name, shape, dt):
            return nc.alloc_sbuf_tensor(name, shape, dt).ap()
        self.X1 = sb("x1", [128, NFT * L], F16)
        self.W = sb("wreg", [128, NFT * (L + 2) + NFT * L], F16)
        self.S = sb("sreg", [128, 33280], F16)
        self.SF = sb("sf32", [128, 5632], F32)
        self.S32 = sb("s32", [128, 2048], F32)
        self.G32 = sb("g32", [128, 512], F32)
        self.GS = sb("gsml", [128, 1920], F16)
        self.I16T = sb("i16", [128, 512], I16)
        self.CAR = sb("carry", [128, 128], F16)
        self.ONE = sb("ones", [128, 8], F16)
        self.IDN = sb("idn", [128, 128], F16)
        self.IDN32 = sb("idn32", [128, 128], F32)
        self.PS = [nc.alloc_psum_tensor("ps%d" % i, [128, 512], F32).ap() for i in range(6)]
        self.PT16 = nc.alloc_psum_tensor("pt16", [128, 512], F16).ap()
        self.PT32 = nc.alloc_psum_tensor("pt32", [128, 512], F32).ap()
        self.GO = {"negA": 0, "D": 128, "b_in": 136, "cb": 144, "b_d2": 152,
                   "b_out": 160, "cI": 168, "c0": 176, "cL": 184, "alpha": 192,
                   "beta": 200, "beta0": 208, "betaL": 216, "SvU": 224, "SvU2": 256,
                   "nwsum": 288, "rb": 296, "bB": 298, "bC": 299, "b_d1": 300,
                   "sw": 304, "half": 312, "eps": 314, "tmp": 320}
        self.TO = {"sum32": 0, "otm": 512, "rowst": 1536, "p1a": 1664, "gn": 1792,
                   "ff": 1856}
        self.GSO = {"wBT": 0, "wCT": 128, "wd1T": 256, "wd2T": 768, "sp": 1792}

    def u_ft(self, ft):
        return self.W[:, ft * (L + 2): (ft + 1) * (L + 2)]

    def vu_ft(self, ft):
        o = NFT * (L + 2)
        return self.W[:, o + ft * L: o + (ft + 1) * L]

    def _enter(self, cm):
        v = cm.__enter__()
        self._ctx.append(cm)
        return v


    def sec(self, eng, fn):
        self._pend.setdefault(eng, []).append(fn)

    def _flushq(self, eng):
        fns = self._pend.get(eng)
        if not fns:
            return
        self._pend[eng] = []
        def big(e, fns=fns):
            for f in fns:
                f(e)
        getattr(self.block, eng)(big)

    def flush_all(self):
        for eng in list(self._pend):
            self._flushq(eng)

    def inc(self, nm, k):
        self.cnt[nm] += k
        return self.cnt[nm]

    def _op(self, eng, nm, fn, inc, skip_own=False):
        before = self.cnt[nm]
        if inc:
            self.inc(nm, inc)
        sem = self.sem[nm]
        def f(e, fn=fn, inc=inc, sem=sem, before=before, skip_own=skip_own):
            if before and not skip_own:
                e.wait_ge(sem, before)
            r = fn(e)
            if inc:
                r.then_inc(sem, inc)
        self.sec(eng, f)

    def vop(self, fn, inc=1):
        self._op("vector", "v", fn, inc)

    def sop(self, fn, inc=1):
        self._op("scalar", "s", fn, inc)

    def gop(self, fn, inc=1):
        self._op("gpsimd", "p", fn, inc)

    def peop(self, fn, inc=1):
        self._op("tensor", "pe", fn, inc, skip_own=True)

    def dma(self, queue, nm, out_ap, in_ap, transpose=False):
        n = self.inc(nm, 16)
        sem = self.sem[nm]
        def f(e, out_ap=out_ap, in_ap=in_ap, sem=sem, transpose=transpose):
            if transpose:
                e.dma_start_transpose(out_ap, in_ap).then_inc(sem, 16)
            else:
                e.dma_start(out=out_ap, in_=in_ap).then_inc(sem, 16)
        self.sec(queue, f)
        return n

    def store(self, out_ap, in_ap, nm="sto0", queue="sync"):
        n = self.inc(nm, 16)
        sem = self.sem[nm]
        def f(e, out_ap=out_ap, in_ap=in_ap, sem=sem):
            e.dma_start(out=out_ap, in_=in_ap).then_inc(sem, 16)
        self.sec(queue, f)
        return n

    def wait(self, eng, nm, val=None):
        val = self.cnt[nm] if val is None else val
        if val <= 0:
            return
        sem = self.sem[nm]
        self.sec(eng, lambda e, sem=sem, val=val: e.wait_ge(sem, val))

    def mark(self, name, nm):
        self.mk[name] = (nm, self.cnt[nm])

    def mwait(self, eng, name):
        if name in self.mk:
            nm, v = self.mk[name]
            self.wait(eng, nm, v)

    def barrier(self):
        self.flush_all()
        self.cnt["bar"] += len(ENGS)
        tgt = self.cnt["bar"]
        sem = self.sem["bar"]
        for eng in ENGS:
            def f(e, sem=sem, tgt=tgt):
                e.sem_inc(sem, 1)
                e.wait_ge(sem, tgt)
            self.sec(eng, f)

    def tap(self, name, src_ap, shape, dtype=None):
        if name not in self.taps:
            return
        dtype = src_ap.dtype if dtype is None else dtype
        t = self.nc.dram_tensor("tap_" + name, list(shape), dtype, kind="ExternalOutput").ap()
        self.tapd[name] = True
        for nm in ("v", "s", "pe", "p", "str", "sts0", "sts1"):
            self.wait("sync", nm)
        self.store(t, src_ap)
        self.wait("sync", "sto0")
        self.wait("gpsimd", "sto0")

    def _prog(self):
        self.block = self._enter(self.nc.Block())
        self.sem = {nm: self._enter(self.nc.semaphore(nm)) for nm in
                    ("l0", "l1", "l2", "l3", "lw", "g0", "g1", "str", "sts0",
                     "sts1", "sto0", "sto1", "pe", "v", "s", "p", "bar")}
        self.cnt = {nm: 0 for nm in self.sem}
        self.prelude()
        self.p0()
        if self.upto == "p0":
            return self.finish()
        for b in range(NB):
            self.barrier()
            self.p1a(b)
            if self.upto == "p1a" and b == 0:
                return self.finish()
            self.barrier()
            self.p1b(b)
            if self.upto == "p1b" and b == 0:
                return self.finish()
            self.chanstats(b)
            self.barrier()
            self.p2(b)
            if self.upto == "p2" and b == 0:
                return self.finish()
            self.barrier()
            self.ffn(b)
            if self.upto == "ffn" and b == 0:
                return self.finish()
        self.finish()

    def prelude(self):
        self.vop(lambda v: v.memset(self.ONE[:], 1.0))
        self.vop(lambda v: v.memset(self.G32[:], 0.0))
        self.vop(lambda v: v.memset(self.S32[:], 0.0))
        self.vop(lambda v: v.memset(
            self.G32[:, self.GO["half"]:self.GO["half"] + 1], 0.5))
        self.vop(lambda v: v.memset(
            self.G32[:, self.GO["eps"]:self.GO["eps"] + 1], EPS), inc=1)
        self.wait("sync", "v")
        self.wait("gpsimd", "v")
        self.vop(lambda v: v.memset(self.CAR[:], 0.0))
        self.vop(lambda v: v.memset(self.IDN[:], 1.0), inc=1)
        self.wait("gpsimd", "v")
        self.gop(lambda g: g.affine_select(self.IDN[:], self.IDN[:], [[1, 128]],
                                           OP.is_equal, 0.0, base=0,
                                           channel_multiplier=-1), inc=1)
        self.wait("vector", "p")
        self.vop(lambda v: v.tensor_copy(self.IDN32[:], self.IDN[:]), inc=1)
        self.tap("idn", self.IDN32[:], (128, 128))

    def p0(self):
        for b2 in range(NB):
            for c in range(4):
                self.dma("gpsimd", "g0", self.x16[b2, c * 512:(c + 1) * 512, :],
                         self.x_h[b2, c * 512:(c + 1) * 512, :])
        self.p0_small()
        self.barrier()
        ga = self.p0_attn_units()
        gq = self.p0_quant_units()
        done = object()
        qi = 0
        alive_a, alive_q = True, True
        while alive_q or alive_a:
            if alive_q:
                alive_q = next(gq, done) is not done
                qi += 1
            if alive_a and (qi % 4 == 0 or not alive_q):
                alive_a = next(ga, done) is not done
        self.p0_dk()
        self.barrier()

    def p0_small(self):
        g = self.G32
        GO = self.GO
        for name, off in (("D", GO["D"]), ("b_in", GO["b_in"]), ("conv_b", GO["cb"]),
                          ("b_d2", GO["b_d2"]), ("b_out", GO["b_out"])):
            for c in range(NFT):
                self.dma("sync", "l0", g[:, off + c: off + c + 1],
                         self.w_h[name][c * 128:(c + 1) * 128].unsqueeze(1))
        self.dma("sync", "l0", g[0:NST, GO["bB"]:GO["bB"] + 1], self.w_h["b_B"].unsqueeze(1))
        self.dma("sync", "l0", g[0:NST, GO["bC"]:GO["bC"] + 1], self.w_h["b_C"].unsqueeze(1))
        self.dma("sync", "l0", g[0:NDLT, GO["b_d1"]:GO["b_d1"] + 1], self.w_h["b_d1"].unsqueeze(1))
        tmp = g[:, GO["tmp"]:GO["tmp"] + NST]
        for c in range(NFT):
            self.mwait("sync", "negA_done")
            self.dma("sync", "l0", tmp, self.w_h["A_log"][c * 128:(c + 1) * 128, :])
            self.wait("scalar", "l0")
            self.sop(lambda s, c=c: s.activation(
                g[:, GO["negA"] + c * NST: GO["negA"] + (c + 1) * NST], tmp,
                AF.Exp), inc=1)
            self.mark("negA_done", "s")
        self.wait("vector", "s")
        self.vop(lambda v: v.tensor_scalar_mul(
            g[:, GO["negA"]:GO["negA"] + 128], g[:, GO["negA"]:GO["negA"] + 128], -1.0),
            inc=1)
        S = self.S
        stg = S[:, 0:1024]
        GSO = self.GSO
        for (wname, gso, m) in (("w_B", "wBT", NST), ("w_C", "wCT", NST), ("w_d1", "wd1T", NDLT)):
            self.mwait("gpsimd", "smallw_done")
            self.dma("gpsimd", "g1", stg[0:m, 0:FD], self.w_h[wname])
            self.wait("tensor", "g1")
            self.wait("tensor", "v")
            for kt in range(NFT):
                self.mwait("tensor", "smallev")
                self.peop(lambda t, kt=kt, m=m: t.transpose(
                    self.PT16[0:128, 0:m], stg[0:m, kt * 128:(kt + 1) * 128], self.IDN[0:m, 0:m]),
                    inc=1)
                self.wait("vector", "pe")
                self.vop(lambda v, kt=kt, gso=gso, m=m: v.tensor_copy(
                    self.GS[:, GSO[gso] + kt * m: GSO[gso] + (kt + 1) * m],
                    self.PT16[0:128, 0:m]), inc=1)
                self.mark("smallev", "v")
            self.mark("smallw_done", "v")
        self.mwait("gpsimd", "smallw_done")
        self.dma("gpsimd", "g1", stg[:, 0:NFT * NDLT].rearrange("p (t d) -> p t d", d=NDLT),
                 self.w_h["w_d2"].rearrange("(t p) d -> p t d", p=128))
        self.wait("tensor", "g1")
        for mt in range(NFT):
            self.mwait("tensor", "smallev")
            self.peop(lambda t, mt=mt: t.transpose(
                self.PT16[0:NDLT, 0:128], stg[:, mt * NDLT:(mt + 1) * NDLT], self.IDN[:]), inc=1)
            self.wait("vector", "pe")
            self.vop(lambda v, mt=mt: v.tensor_copy(
                self.GS[0:NDLT, GSO["wd2T"] + mt * 128: GSO["wd2T"] + (mt + 1) * 128],
                self.PT16[0:NDLT, 0:128]), inc=1)
            self.mark("smallev", "v")
        self.wait("vector", "l0")
        self.vop(lambda v: v.tensor_copy(self.GS[:, GSO["sp"]:GSO["sp"] + 8],
                                         g[:, GO["b_in"]:GO["b_in"] + 8]), inc=1)

    def p0_quant_units(self):
        g = self.G32
        GO = self.GO
        S = self.S
        st32 = [self.SF[:, 0:1024], self.SF[:, 1024:2048], self.SF[:, 4608:5632]]
        u32_b = [self.SF[:, 2048:3072], self.SF[:, 3584:4608]]
        wqn_b = [S[:, 0:1024], S[:, 1024:2048]]
        fb = S[:, 2048:2560]
        mk = S[:, 2560:3072]
        mk2 = S[:, 3072:3584]
        ev16 = [S[:, 3584:4096], S[:, 4096:4608]]
        t0 = g[0:1, GO["tmp"]:GO["tmp"] + 1]
        t1 = g[0:1, GO["tmp"] + 1:GO["tmp"] + 2]
        t2 = g[0:1, GO["tmp"] + 2:GO["tmp"] + 3]
        acc = g[:, GO["tmp"] + 4:GO["tmp"] + 5]
        red = g[:, GO["tmp"] + 5:GO["tmp"] + 6]
        hcol = self.GS[:, self.GSO["sp"] + 8:self.GSO["sp"] + 9]
        for wi, (wname, scr) in enumerate((("w_gate", self.ws_g), ("w_up", self.ws_u),
                                           ("w_down", self.ws_d))):
            wh = self.w_h[wname]
            nrow, ncolw = wh.shape
            ntile = nrow // 128
            csub = ncolw // 1024
            nu = ntile * csub
            ld_cnt = {}

            def issue_p1(u):
                i, cs = divmod(u, csub)
                lsem = "l%d" % (u % 3)
                self.mwait("sync", "q_red%d" % (u % 3))
                self.dma("sync", lsem, st32[u % 3],
                         wh[i * 128:(i + 1) * 128, cs * 1024:(cs + 1) * 1024])
                ld_cnt[u] = self.cnt[lsem]
            self.vop(lambda v: v.memset(acc, 0.0), inc=1)
            issue_p1(0)
            issue_p1(1)
            for u in range(nu):
                st = st32[u % 3]
                if u + 2 < nu:
                    issue_p1(u + 2)
                self.wait("vector", "l%d" % (u % 3), ld_cnt[u])
                self.vop(lambda v, st=st: v.tensor_reduce(
                    red, st, AX.X, OP.add, apply_absolute_value=True))
                self.vop(lambda v: v.tensor_tensor(acc, acc, red, OP.add), inc=1)
                self.mark("q_red%d" % (u % 3), "v")
                yield
            self.vop(lambda v: v.tensor_copy(hcol, acc), inc=1)
            self.wait("tensor", "v")
            self.peop(lambda t: t.matmul(self.PS[4][0:1, 0:1], self.ONE[:, 0:1], hcol,
                                         start=True, stop=True), inc=1)
            self.wait("vector", "pe")
            self.vop(lambda v: v.tensor_copy(t0, self.PS[4][0:1, 0:1]), inc=1)
            self.wait("tensor", "v")
            self.vop(lambda v, n=nrow * ncolw: v.tensor_scalar(t1, t0, 1.0 / n, 1e-5, OP.mult, OP.max))
            self.vop(lambda v: v.reciprocal(t2, t1), inc=1)
            self.wait("sync", "v")
            self.store(self.rows_t[self.RO["sw"] + wi:self.RO["sw"] + wi + 1].unsqueeze(0), t2, nm="str")
            self.store(self.rows_t[self.RO["sw"] + 3 + wi:self.RO["sw"] + 4 + wi].unsqueeze(0), t1, nm="str")
            self.wait("gpsimd", "str")
            self.dma("gpsimd", "g0", g[:, GO["sw"] + wi:GO["sw"] + wi + 1],
                     self.rows_t[self.RO["sw"] + wi:self.RO["sw"] + wi + 1]
                     .unsqueeze(0).partition_broadcast(128))
            self.dma("gpsimd", "g0", g[:, GO["sw"] + 3 + wi:GO["sw"] + 4 + wi],
                     self.rows_t[self.RO["sw"] + 3 + wi:self.RO["sw"] + 4 + wi]
                     .unsqueeze(0).partition_broadcast(128))
            self.wait("scalar", "g0")
            self.wait("vector", "g0")
            def issue_p2(u):
                i, cs = divmod(u, csub)
                lsem = "l%d" % (u % 3)
                self.mwait("sync", "q_quant%d" % (u % 3))
                self.dma("sync", lsem, st32[u % 3],
                         wh[i * 128:(i + 1) * 128, cs * 1024:(cs + 1) * 1024])
                ld_cnt[u] = self.cnt[lsem]
            chain_cnt = {}

            def evac_spill(u):
                i, cs = divmod(u, csub)
                sl0 = cs * 1024
                wqn = wqn_b[u % 2]
                self.wait("tensor", "v", chain_cnt[u])
                for half in range(2):
                    self.mwait("tensor", "q_ev")
                    for j in range(4):
                        blk = half * 4 + j
                        self.peop(lambda t, blk=blk, j=j, wqn=wqn: t.transpose(
                            self.PT16[:, j * 128:(j + 1) * 128],
                            wqn[:, blk * 128:(blk + 1) * 128], self.IDN[:]))
                    if half == 1:
                        self.mark("q_wq%d" % (u % 2), "pe")
                    self.wait("scalar", "pe", self.cnt["pe"])
                    self.mwait("scalar", "q_st%d" % half)
                    self.sop(lambda s, half=half: s.copy(ev16[half], self.PT16[:, 0:512]), inc=1)
                    self.mark("q_ev", "s")
                    self.wait("sync", "s", self.cnt["s"])
                    bj0 = sl0 // 128 + half * 4
                    self.store(scr[i, :, bj0 * 128:bj0 * 128 + 512],
                               ev16[half][:, 0:512], nm="sts%d" % half)
                    self.mark("q_st%d" % half, "sts%d" % half)
            issue_p2(0)
            issue_p2(1)
            for u in range(nu):
                i, cs = divmod(u, csub)
                st = st32[u % 3]
                u32 = u32_b[u % 2]
                wqn = wqn_b[u % 2]
                if u + 2 < nu:
                    issue_p2(u + 2)
                if True:
                    self.wait("scalar", "l%d" % (u % 3), ld_cnt[u])
                    self.mwait("scalar", "q_u%d" % (u % 2))
                    self.sop(lambda s, st=st, wi=wi, u32=u32: s.activation(
                        u32, st, AF.Identity,
                        bias=g[:, GO["half"]:GO["half"] + 1],
                        scale=g[:, GO["sw"] + wi:GO["sw"] + wi + 1]), inc=1)
                    self.mark("q_quant%d" % (u % 3), "s")
                    self.wait("vector", "s", self.cnt["s"])
                    self.mwait("vector", "q_wq%d" % (u % 2))
                    for hc in range(2):
                        sl = slice(hc * 512, (hc + 1) * 512)
                        self.vop(lambda v, sl=sl, u32=u32: v.tensor_copy(
                            self.I16T[:, 0:512], u32[:, sl]))
                        self.vop(lambda v: v.tensor_copy(fb, self.I16T[:, 0:512]))
                        self.vop(lambda v, sl=sl, u32=u32: v.tensor_tensor(
                            mk, fb, u32[:, sl], OP.is_gt))
                        self.vop(lambda v, sl=sl, u32=u32: v.scalar_tensor_tensor(
                            mk2, fb, 1.0, u32[:, sl], OP.add, OP.is_le))
                        self.vop(lambda v: v.tensor_tensor(fb, fb, mk, OP.subtract))
                        self.vop(lambda v: v.tensor_tensor(fb, fb, mk2, OP.add))
                        self.vop(lambda v, sl=sl, wqn=wqn: v.tensor_scalar(
                            wqn[:, sl], fb, 1.0, -1.0, OP.min, OP.max), inc=1)
                    self.mark("q_u%d" % (u % 2), "v")
                    chain_cnt[u] = self.cnt["v"]
                    if u > 0:
                        evac_spill(u - 1)
                yield
            evac_spill(nu - 1)

    def p0_attn_units(self):
        g = self.G32
        GO = self.GO
        S = self.S
        nat_b = [S[:, 13312:16384], S[:, 16384:19456]]
        ev16 = [S[:, 19456:19968], S[:, 19968:20480]]
        natsep = S[:, 20480:21504]
        units = [(wname, tag, mt) for (wname, tag) in (("w_in", "in"), ("w_out", "out"))
                 for mt in range(NFT)]
        units += [("conv_w", "conv", mt) for mt in range(NFT)]
        g1_cnt = {}

        AWS = ("g1", "l3", "lw")

        def issue(i):
            wname, tag, mt = units[i]
            nat = nat_b[i % 2]
            gsem = AWS[i % 3]
            self.mwait("gpsimd", "aw_natS%d" % (i % 2))
            self.mwait("gpsimd", "aw_natV%d" % (i % 2))
            if tag == "conv":
                self.dma("gpsimd", gsem, nat[:, 0:3072],
                         self.w_h["conv_w"][mt * 128:(mt + 1) * 128, :, :]
                         .rearrange("p i k -> p (i k)"))
            else:
                self.dma("gpsimd", gsem, nat[:, 0:1024],
                         self.w_h[wname][mt * 128:(mt + 1) * 128, :])
            g1_cnt[i] = self.cnt[gsem]
        issue(0)
        issue(1)
        for i, (wname, tag, mt) in enumerate(units):
            nat = nat_b[i % 2]
            if tag != "conv":
                self.wait("tensor", AWS[i % 3], g1_cnt[i])
                for half in range(2):
                    self.mwait("tensor", "q_ev")
                    for j in range(4):
                        kt = half * 4 + j
                        self.peop(lambda t, kt=kt, j=j, nat=nat: t.transpose(
                            self.PT16[:, j * 128:(j + 1) * 128],
                            nat[:, kt * 128:(kt + 1) * 128], self.IDN[:]))
                    self.wait("scalar", "pe")
                    self.mwait("scalar", "aw_st%d" % half)
                    self.sop(lambda s, half=half, ev16=ev16: s.copy(
                        ev16[half], self.PT16[:, 0:512]), inc=1)
                    self.mark("q_ev", "s")
                    self.wait("gpsimd", "s")
                    kt0 = half * 4
                    if tag == "in":
                        self.store(self.ws_in[:, (mt * 8 + kt0) * 128:(mt * 8 + kt0) * 128 + 512],
                                   ev16[half][:, 0:512], nm="g0", queue="gpsimd")
                    else:
                        self.store(self.ws_out[mt, :, kt0 * 128:kt0 * 128 + 512],
                                   ev16[half][:, 0:512], nm="g0", queue="gpsimd")
                    self.mark("aw_st%d" % half, "g0")
                if tag == "out":
                    self.wait("vector", AWS[i % 3], g1_cnt[i])
                    self.vop(lambda v, mt=mt, nat=nat: v.tensor_reduce(
                        g[:, GO["nwsum"] + mt:GO["nwsum"] + mt + 1], nat[:, 0:1024],
                        AX.X, OP.add, negate=True), inc=1)
                self.mark("aw_natS%d" % (i % 2), "s")
                self.mark("aw_natV%d" % (i % 2), "v")
                if i + 2 < len(units):
                    issue(i + 2)
                yield
            else:
                self.wait("vector", AWS[i % 3], g1_cnt[i])
                for k in range(3):
                    self.mwait("vector", "aw_sep")
                    self.vop(lambda v, k=k, nat=nat, natsep=natsep: v.tensor_copy(
                        natsep, nat[:, k:3072:3]), inc=1)
                    self.wait("tensor", "v")
                    for half in range(2):
                        self.mwait("tensor", "q_ev")
                        for j in range(4):
                            kt = half * 4 + j
                            self.peop(lambda t, kt=kt, j=j, natsep=natsep: t.transpose(
                                self.PT16[:, j * 128:(j + 1) * 128],
                                natsep[:, kt * 128:(kt + 1) * 128], self.IDN[:]))
                        if half == 1:
                            self.mark("aw_sep", "pe")
                        self.wait("scalar", "pe")
                        self.mwait("scalar", "aw_st%d" % half)
                        self.sop(lambda s, half=half, ev16=ev16: s.copy(
                            ev16[half], self.PT16[:, 0:512]), inc=1)
                        self.mark("q_ev", "s")
                        self.wait("gpsimd", "s")
                        kt0 = half * 4
                        self.store(self.ws_conv[mt, :, (k * 8 + kt0) * 128:(k * 8 + kt0) * 128 + 512],
                                   ev16[half][:, 0:512], nm="g0", queue="gpsimd")
                        self.mark("aw_st%d" % half, "g0")
                    yield
                self.mark("aw_natS%d" % (i % 2), "s")
                self.mark("aw_natV%d" % (i % 2), "v")
                if i + 2 < len(units):
                    issue(i + 2)

    def p0_dk(self):
        g = self.G32
        GO = self.GO
        S = self.S
        self.wait("sync", "g0")
        self.wait("sync", "sts0")
        self.wait("sync", "sts1")
        self.wait("tensor", "v")
        binh = self.GS[:, self.GSO["sp"]:self.GSO["sp"] + 8]
        cw1 = S[:, 20480:21504]
        for k in range(3):
            for mt in range(NFT):
                self.mwait("sync", "dk_mm")
                self.dma("sync", "l0", cw1, self.ws_conv[mt, :, k * 1024:(k + 1) * 1024])
                self.wait("tensor", "l0")
                for kt in range(NFT):
                    self.peop(lambda t, kt=kt, mt=mt: t.matmul(
                        self.PS[4][:, mt:mt + 1], cw1[:, kt * 128:(kt + 1) * 128],
                        binh[:, kt:kt + 1], start=(kt == 0), stop=(kt == NFT - 1)))
                self.mark("dk_mm", "pe")
            self.wait("vector", "pe")
            self.vop(lambda v, k=k: v.tensor_copy(
                g[:, GO["tmp"] + 8 * k:GO["tmp"] + 8 * (k + 1)], self.PS[4][:, 0:8]), inc=1)
            self.wait("tensor", "v")
        d0 = g[:, GO["tmp"]:GO["tmp"] + 8]
        d1 = g[:, GO["tmp"] + 8:GO["tmp"] + 16]
        d2 = g[:, GO["tmp"] + 16:GO["tmp"] + 24]
        cI = g[:, GO["cI"]:GO["cI"] + 8]
        c0 = g[:, GO["c0"]:GO["c0"] + 8]
        cL = g[:, GO["cL"]:GO["cL"] + 8]
        cb = g[:, GO["cb"]:GO["cb"] + 8]
        self.wait("vector", "l0")
        self.vop(lambda v: v.tensor_tensor(cI, cb, d0, OP.add))
        self.vop(lambda v: v.tensor_tensor(cI, cI, d1, OP.add))
        self.vop(lambda v: v.tensor_tensor(cI, cI, d2, OP.add))
        self.vop(lambda v: v.tensor_tensor(c0, cI, d0, OP.subtract))
        self.vop(lambda v: v.tensor_tensor(cL, cI, d2, OP.subtract), inc=1)
        self.tap("d0", d0, (128, 8))
        self.tap("cb", cb, (128, 8))
        self.tap("cI", cI, (128, 8))
        self.tap("negA", g[:, GO["negA"]:GO["negA"] + 128], (128, 128))
        self.tap("nwsum", g[:, GO["nwsum"]:GO["nwsum"] + 8], (128, 8))

    def p1a(self, b):
        g = self.G32
        GO = self.GO
        S = self.S
        s32 = self.S32
        T = self.TO["p1a"]
        buf_b = [S[:, 0:1024], S[:, 1024:2048]]
        sq_b = [S[:, 2048:3072], S[:, 3072:4096]]
        sx = s32[:, T:T + 16]
        sxx = s32[:, T + 16:T + 32]
        a0 = s32[:, T + 32:T + 48]
        b0 = s32[:, T + 48:T + 64]
        w = s32[:, T + 64:T + 80]
        mu = s32[:, T + 80:T + 96]
        var = s32[:, T + 96:T + 112]
        tt = s32[:, T + 112:T + 128]
        self.wait("sync", "g0")
        self.wait("sync", "g1")
        ld = {}

        def issue(t):
            lsem = "l%d" % (t % 2)
            self.mwait("sync", "p1a_buf%d" % (t % 2))
            self.dma("sync", lsem, buf_b[t % 2], self.x16[b, t * 128:(t + 1) * 128, :])
            ld[t] = self.cnt[lsem]
        issue(0)
        issue(1)
        for t in range(16):
            buf = buf_b[t % 2]
            sq = sq_b[t % 2]
            self.wait("scalar", "l%d" % (t % 2), ld[t])
            self.mwait("scalar", "p1a_sq%d" % (t % 2))
            self.sop(lambda s, buf=buf, sq=sq: s.activation(sq, buf, AF.Square), inc=1)
            self.wait("vector", "l%d" % (t % 2), ld[t])
            self.wait("vector", "s")
            self.vop(lambda v, buf=buf, t=t: v.tensor_reduce(
                sx[:, t:t + 1], buf, AX.X, OP.add))
            self.vop(lambda v, sq=sq, t=t: v.tensor_reduce(
                sxx[:, t:t + 1], sq, AX.X, OP.add), inc=1)
            self.mark("p1a_buf%d" % (t % 2), "v")
            self.mark("p1a_sq%d" % (t % 2), "v")
            if t + 2 < 16:
                issue(t + 2)
        self.wait("tensor", "v")
        self.vop(lambda v: v.tensor_scalar_mul(mu, sx, 1.0 / FD))
        self.vop(lambda v: v.tensor_scalar_mul(var, sxx, 1.0 / FD))
        self.vop(lambda v: v.tensor_tensor(tt[:, 0:16], mu, mu, OP.mult))
        self.vop(lambda v: v.tensor_tensor(var, var, tt[:, 0:16], OP.subtract), inc=1)
        self.wait("scalar", "v")
        self.sop(lambda s: s.activation(tt[:, 0:16], var, AF.Sqrt, bias=g[:, GO["eps"]:GO["eps"] + 1]), inc=1)
        self.wait("vector", "s")
        self.vop(lambda v: v.reciprocal(a0, tt[:, 0:16]))
        self.vop(lambda v: v.tensor_tensor(b0, mu, a0, OP.mult))
        self.vop(lambda v: v.tensor_scalar_mul(b0, b0, -1.0))
        self.vop(lambda v: v.tensor_scalar_add(tt[:, 0:16], var, EPS))
        self.vop(lambda v: v.reciprocal(tt[:, 0:16], tt[:, 0:16]))
        self.vop(lambda v: v.tensor_tensor(w, var, tt[:, 0:16], OP.mult))
        self.vop(lambda v: v.tensor_reduce(tt[:, 0:1], w, AX.X, OP.add))
        self.vop(lambda v: v.tensor_copy(self.GS[:, self.GSO["sp"] + 8:self.GSO["sp"] + 9],
                                         tt[:, 0:1]), inc=1)
        self.wait("tensor", "v")
        self.peop(lambda t: t.matmul(self.PS[4][0:1, 16:17], self.ONE[:, 0:1],
                                     self.GS[:, self.GSO["sp"] + 8:self.GSO["sp"] + 9],
                                     start=True, stop=True), inc=1)
        self.wait("vector", "pe")
        rbt = tt[0:1, 4:5]
        self.vop(lambda v, rbt=rbt: v.tensor_copy(rbt, self.PS[4][0:1, 16:17]))
        self.vop(lambda v, rbt=rbt: v.tensor_scalar_mul(rbt, rbt, 1.0 / L), inc=1)
        self.wait("scalar", "v")
        self.sop(lambda s, rbt=rbt: s.activation(rbt, rbt, AF.Sqrt,
                 bias=g[0:1, GO["eps"]:GO["eps"] + 1]), inc=1)
        self.wait("vector", "s")
        self.vop(lambda v, rbt=rbt: v.reciprocal(rbt, rbt), inc=1)
        self.wait("tensor", "v")
        self.peop(lambda t: t.transpose(self.PT32[0:32, 0:128], s32[:, T + 32:T + 64],
                                        self.IDN32[:]), inc=1)
        self.wait("vector", "pe")
        rowst = s32[0:32, self.TO["rowst"]:self.TO["rowst"] + 128]
        self.vop(lambda v, rowst=rowst: v.tensor_copy(rowst, self.PT32[0:32, 0:128]), inc=1)
        self.wait("sync", "v")
        self.store(self.row("a0", 0, L), rowst[0:16, :], nm="str")
        self.store(self.row("b0", 0, L), rowst[16:32, :], nm="str")
        self.store(self.row("rb", 0, 1), tt[0:1, 4:5], nm="str")
        self.wait("gpsimd", "str")
        self.dma("gpsimd", "g0", g[:, GO["rb"]:GO["rb"] + 1],
                 self.row("rb", 0, 1).partition_broadcast(128))
        self.wait("vector", "g0")
        self.vop(lambda v: v.tensor_tensor(g[:, GO["rb"] + 1:GO["rb"] + 2],
                                           g[:, GO["rb"]:GO["rb"] + 1],
                                           g[:, GO["rb"]:GO["rb"] + 1], OP.mult), inc=1)
        self.tap("a0row", self.row("a0", 0, L), (1, L))
        self.tap("b0row", self.row("b0", 0, L), (1, L))
        self.tap("rb", g[:, GO["rb"]:GO["rb"] + 2], (128, 2))

    def p1b(self, b):
        g = self.G32
        GO = self.GO
        S = self.S
        x2 = S[:, 0:4096]
        xz = [S[:, 4096 + i * 4096: 4096 + (i + 1) * 4096] for i in range(2)]
        bc = [S[:, 12288 + i * 512: 12288 + (i + 1) * 512] for i in range(4)]
        w_inT = S[:, 14336:22528]
        cring = [S[:, 22528 + i * 3072: 22528 + (i + 1) * 3072] for i in range(2)]
        self.wait("sync", "g0")
        self.wait("sync", "sts0")
        self.wait("sync", "sts1")
        for ft in range(NFT):
            self.vop(lambda v, ft=ft: v.memset(self.u_ft(ft)[:, 0:1], 0.0))
            self.vop(lambda v, ft=ft: v.memset(self.u_ft(ft)[:, L + 1:L + 2], 0.0))
        self.dma("sync", "lw", w_inT, self.ws_in)
        w_inT_ld = self.cnt["lw"]
        for c in range(NCH1):
            buf = xz[c % 2]
            lsem = "l%d" % (c % 2)
            gsem = "g%d" % (c % 2)
            self.mwait("sync", "p1b_buf%d" % (c % 2))
            for ft in range(NFT):
                self.dma("sync", lsem, buf[:, ft * CH1:(ft + 1) * CH1],
                         self.x16[b, c * CH1:(c + 1) * CH1, ft * 128:(ft + 1) * 128],
                         transpose=True)
            ba = bc[2 * (c % 2)]
            bb2 = bc[2 * (c % 2) + 1]
            self.mwait("gpsimd", "p1b_bc%d" % (c % 2))
            self.dma("gpsimd", gsem, ba, self.row("a0", c * CH1, (c + 1) * CH1).partition_broadcast(128))
            self.dma("gpsimd", gsem, bb2, self.row("b0", c * CH1, (c + 1) * CH1).partition_broadcast(128))
            self.wait("vector", lsem)
            self.wait("vector", gsem)
            for ft in range(NFT):
                sl = slice(ft * CH1, (ft + 1) * CH1)
                self.vop(lambda v, buf=buf, sl=sl, ba=ba: v.tensor_tensor(
                    buf[:, sl], buf[:, sl], ba, OP.mult))
                self.vop(lambda v, buf=buf, sl=sl, bb2=bb2: v.tensor_tensor(
                    buf[:, sl], buf[:, sl], bb2, OP.add))
            self.mark("p1b_bc%d" % (c % 2), "v")
            zdone = self.cnt["v"]
            self.wait("tensor", "v", zdone)
            self.wait("tensor", "lw", w_inT_ld)
            for mt in range(NFT):
                ps = self.PS[mt % 2]
                self.mwait("tensor", "p1b_ps%d" % (mt % 2))
                for kt in range(NFT):
                    self.peop(lambda t, mt=mt, kt=kt, buf=buf, ps=ps: t.matmul(
                        ps[:, 0:CH1], w_inT[:, (mt * 8 + kt) * 128:(mt * 8 + kt + 1) * 128],
                        buf[:, kt * CH1:(kt + 1) * CH1],
                        start=(kt == 0), stop=(kt == NFT - 1)))
                self.wait("scalar", "pe")
                self.sop(lambda s, mt=mt, c=c, ps=ps: s.copy(
                    self.u_ft(mt)[:, 1 + c * CH1: 1 + (c + 1) * CH1], ps[:, 0:CH1]), inc=1)
                self.mark("p1b_ps%d" % (mt % 2), "s")
            self.mark("p1b_buf%d" % (c % 2), "pe")
        u_done = self.cnt["s"]
        for mt in range(NFT):
            ring = cring[mt % 2]
            lsem = "l%d" % (2 + mt % 2)
            self.mwait("sync", "p1b_ring%d" % (mt % 2))
            self.dma("sync", lsem, ring, self.ws_conv[mt])
            self.wait("tensor", lsem)
            if mt == 0:
                self.wait("tensor", "s", u_done)
                self.wait("tensor", "v")
            for c in range(NCH1):
                ps = self.PS[2 + (c % 2)]
                self.mwait("tensor", "p1b_cps%d" % (c % 2))
                for k in range(3):
                    for kt in range(NFT):
                        off = 1 + c * CH1 + (k - 1)
                        self.peop(lambda t, k=k, kt=kt, off=off, ps=ps, ring=ring: t.matmul(
                            ps[:, 0:CH1], ring[:, (k * 8 + kt) * 128:(k * 8 + kt + 1) * 128],
                            self.u_ft(kt)[:, off:off + CH1],
                            start=(k == 0 and kt == 0), stop=(k == 2 and kt == NFT - 1)))
                self.wait("scalar", "pe")
                self.mwait("scalar", "p1b_sq%d" % (c % 2))
                self.sop(lambda s, mt=mt, c=c, ps=ps: s.activation(
                    self.vu_ft(mt)[:, c * CH1:(c + 1) * CH1], ps[:, 0:CH1], AF.Identity,
                    accum_out=g[:, GO["SvU"] + mt * 4 + c:GO["SvU"] + mt * 4 + c + 1]))
                self.sop(lambda s, mt=mt, c=c, ps=ps: s.activation(
                    x2[:, (c % 2) * 512:(c % 2) * 512 + CH1], ps[:, 0:CH1], AF.Square,
                    accum_out=g[:, GO["SvU2"] + mt * 4 + c:GO["SvU2"] + mt * 4 + c + 1]), inc=1)
                self.mark("p1b_cps%d" % (c % 2), "s")
                self.mark("p1b_sq%d" % (c % 2), "s")
            self.mark("p1b_ring%d" % (mt % 2), "pe")
        self.tap("u", self.U[:, 0:2050], (128, 2050)) if False else None
        self.tap("u", self.u_ft(0)[:, 0:2050], (128, 2050))
        self.tap("vU", self.vu_ft(0), (128, L))
        self.tap("SvU", g[:, GO["SvU"]:GO["SvU"] + 64], (128, 64))

    @property
    def U(self):
        return self.W

    def chanstats(self, b):
        g = self.G32
        GO = self.GO
        self.wait("vector", "s")
        t = lambda i: g[:, GO["tmp"] + 8 * i:GO["tmp"] + 8 * (i + 1)]
        vu0, vuL, sv, sv2, S1, S2, mu, tA, tB = (t(i) for i in range(9))
        rb = g[:, GO["rb"]:GO["rb"] + 1]
        rb2 = g[:, GO["rb"] + 1:GO["rb"] + 2]
        cI = g[:, GO["cI"]:GO["cI"] + 8]
        c0 = g[:, GO["c0"]:GO["c0"] + 8]
        cL = g[:, GO["cL"]:GO["cL"] + 8]
        al = g[:, GO["alpha"]:GO["alpha"] + 8]
        be = g[:, GO["beta"]:GO["beta"] + 8]
        be0 = g[:, GO["beta0"]:GO["beta0"] + 8]
        beL = g[:, GO["betaL"]:GO["betaL"] + 8]
        for ft in range(NFT):
            self.vop(lambda v, ft=ft: v.tensor_copy(vu0[:, ft:ft + 1], self.vu_ft(ft)[:, 0:1]))
            self.vop(lambda v, ft=ft: v.tensor_copy(vuL[:, ft:ft + 1], self.vu_ft(ft)[:, L - 1:L]))
        sAll = g[:, GO["SvU"]:GO["SvU"] + 32].rearrange("p (m c) -> p m c", c=4)
        s2All = g[:, GO["SvU2"]:GO["SvU2"] + 32].rearrange("p (m c) -> p m c", c=4)
        self.vop(lambda v: v.tensor_tensor(sv, sAll[:, :, 0], sAll[:, :, 1], OP.add))
        self.vop(lambda v: v.tensor_tensor(sv, sv, sAll[:, :, 2], OP.add))
        self.vop(lambda v: v.tensor_tensor(sv, sv, sAll[:, :, 3], OP.add))
        self.vop(lambda v: v.tensor_tensor(sv2, s2All[:, :, 0], s2All[:, :, 1], OP.add))
        self.vop(lambda v: v.tensor_tensor(sv2, sv2, s2All[:, :, 2], OP.add))
        self.vop(lambda v: v.tensor_tensor(sv2, sv2, s2All[:, :, 3], OP.add))
        self.vop(lambda v: v.tensor_scalar_mul(tA, cI, float(L - 2)))
        self.vop(lambda v: v.scalar_tensor_tensor(S1, sv, rb, tA, OP.mult, OP.add))
        self.vop(lambda v: v.tensor_tensor(S1, S1, c0, OP.add))
        self.vop(lambda v: v.tensor_tensor(S1, S1, cL, OP.add))
        self.vop(lambda v: v.tensor_tensor(tA, sv, vu0, OP.subtract))
        self.vop(lambda v: v.tensor_tensor(tA, tA, vuL, OP.subtract))
        self.vop(lambda v: v.tensor_tensor(tA, tA, cI, OP.mult))
        self.vop(lambda v: v.tensor_tensor(tB, c0, vu0, OP.mult))
        self.vop(lambda v: v.tensor_tensor(tA, tA, tB, OP.add))
        self.vop(lambda v: v.tensor_tensor(tB, cL, vuL, OP.mult))
        self.vop(lambda v: v.tensor_tensor(tA, tA, tB, OP.add))
        self.vop(lambda v: v.tensor_scalar_mul(tA, tA, 2.0))
        self.vop(lambda v: v.tensor_scalar(tA, tA, rb, None, OP.mult))
        self.vop(lambda v: v.scalar_tensor_tensor(S2, sv2, rb2, tA, OP.mult, OP.add))
        self.vop(lambda v: v.tensor_tensor(tB, cI, cI, OP.mult))
        self.vop(lambda v: v.tensor_scalar_mul(tB, tB, float(L - 2)))
        self.vop(lambda v: v.tensor_tensor(S2, S2, tB, OP.add))
        self.vop(lambda v: v.tensor_tensor(tB, c0, c0, OP.mult))
        self.vop(lambda v: v.tensor_tensor(S2, S2, tB, OP.add))
        self.vop(lambda v: v.tensor_tensor(tB, cL, cL, OP.mult))
        self.vop(lambda v: v.tensor_tensor(S2, S2, tB, OP.add))
        self.vop(lambda v: v.tensor_scalar_mul(mu, S1, 1.0 / L))
        self.vop(lambda v: v.tensor_scalar_mul(tB, S2, 1.0 / L))
        self.vop(lambda v: v.tensor_tensor(tA, mu, mu, OP.mult))
        self.vop(lambda v: v.tensor_tensor(tB, tB, tA, OP.subtract), inc=1)
        self.wait("scalar", "v")
        self.sop(lambda s: s.activation(tB, tB, AF.Sqrt, bias=g[:, GO["eps"]:GO["eps"] + 1]), inc=1)
        self.wait("vector", "s")
        self.vop(lambda v: v.reciprocal(tA, tB))
        self.vop(lambda v: v.tensor_scalar(al, tA, rb, None, OP.mult))
        self.vop(lambda v: v.tensor_tensor(tB, cI, mu, OP.subtract))
        self.vop(lambda v: v.tensor_tensor(be, tB, tA, OP.mult))
        self.vop(lambda v: v.tensor_tensor(tB, c0, mu, OP.subtract))
        self.vop(lambda v: v.tensor_tensor(be0, tB, tA, OP.mult))
        self.vop(lambda v: v.tensor_tensor(tB, cL, mu, OP.subtract))
        self.vop(lambda v: v.tensor_tensor(beL, tB, tA, OP.mult), inc=1)
        self.tap("alpha", al, (128, 8))
        self.tap("beta", be, (128, 8))

    def p2(self, b):
        g = self.G32
        GO = self.GO
        S = self.S
        s32 = self.S32
        GN = self.TO["gn"]
        O = {"h": 0, "dlt": 8 * CH2, "dH": 16 * CH2, "y": 24 * CH2, "ab0": 32 * CH2,
             "ab1": 33 * CH2, "bx": 34 * CH2, "so": 35 * CH2, "tm": 36 * CH2,
             "bc": 37 * CH2, "wout": 41 * CH2, "evt": 45 * CH2}
        h_t = lambda ft: S[:, O["h"] + ft * CH2: O["h"] + (ft + 1) * CH2]
        d_t = lambda ft: S[:, O["dlt"] + ft * CH2: O["dlt"] + (ft + 1) * CH2]
        dh_t = lambda ft: S[:, O["dH"] + ft * CH2: O["dH"] + (ft + 1) * CH2]
        y_t = lambda ft: S[:, O["y"] + ft * CH2: O["y"] + (ft + 1) * CH2]
        ab_ = [S[:, O["ab0"]:O["ab0"] + CH2], S[:, O["ab1"]:O["ab1"] + CH2]]
        bx_ = S[:, O["bx"]:O["bx"] + CH2]
        so_ = S[:, O["so"]:O["so"] + CH2]
        tm_ = S[:, O["tm"]:O["tm"] + CH2]
        bcp = [S[:, O["bc"] + i * CH2:O["bc"] + (i + 1) * CH2] for i in range(4)]
        wring = [S[:, O["wout"] + i * 1024: O["wout"] + (i + 1) * 1024] for i in range(2)]
        ev1 = S[:, O["evt"]:O["evt"] + 512]
        ev2 = S[:, O["evt"] + 512:O["evt"] + 1024]
        u2s = S[:, O["evt"] + 1024:O["evt"] + 1536]
        gsq = S[:, O["ab0"]:O["ab0"] + CH2]
        h_cnt = {}
        h_pe_free = {0: 0}

        def emit_h(ch2):
            self.wait("scalar", "pe", h_pe_free[ch2])
            self.mwait("scalar", "p2_hfree")
            for ft in range(NFT):
                self.sop(lambda s, ft=ft, ch2=ch2: s.activation(
                    h_t(ft), self.vu_ft(ft)[:, ch2 * CH2:(ch2 + 1) * CH2], AF.Silu,
                    scale=g[:, GO["alpha"] + ft:GO["alpha"] + ft + 1],
                    bias=g[:, GO["beta"] + ft:GO["beta"] + ft + 1]))
                if ch2 == 0:
                    self.sop(lambda s, ft=ft: s.activation(
                        h_t(ft)[:, 0:1], self.vu_ft(ft)[:, 0:1], AF.Silu,
                        scale=g[:, GO["alpha"] + ft:GO["alpha"] + ft + 1],
                        bias=g[:, GO["beta0"] + ft:GO["beta0"] + ft + 1]))
                if ch2 == NCH2 - 1:
                    self.sop(lambda s, ft=ft: s.activation(
                        h_t(ft)[:, CH2 - 1:CH2], self.vu_ft(ft)[:, L - 1:L], AF.Silu,
                        scale=g[:, GO["alpha"] + ft:GO["alpha"] + ft + 1],
                        bias=g[:, GO["betaL"] + ft:GO["betaL"] + ft + 1]))
                elif ft == NFT - 1:
                    self.sop(lambda s: s.copy(ev1[:, 0:1], h_t(0)[:, 0:1]), inc=1)
            h_cnt[ch2] = self.cnt["s"]

        def emit_bmd(ch2):
            self.wait("tensor", "s", h_cnt[ch2])
            stg = s32[0:NST, self.TO["sum32"]:self.TO["sum32"] + 512]
            for (gso, bof, ro) in (("wBT", GO["bB"], "Bm"), ("wCT", GO["bC"], "Cm")):
                for gg in range(CH2 // 512):
                    self.mwait("tensor", "p2_ps1")
                    for kt in range(NFT):
                        self.peop(lambda t, gso=gso, kt=kt, gg=gg, ch2=ch2: t.matmul(
                            self.PS[1][0:NST, 0:512],
                            self.GS[:, self.GSO[gso] + kt * NST: self.GSO[gso] + (kt + 1) * NST],
                            h_t(kt)[:, gg * 512:(gg + 1) * 512],
                            start=(kt == 0), stop=(kt == NFT - 1)))
                    self.wait("vector", "pe")
                    self.mwait("vector", "p2_stg")
                    self.vop(lambda v, bof=bof, stg=stg: v.tensor_scalar(
                        stg, self.PS[1][0:NST, 0:512], g[0:NST, bof:bof + 1], None, OP.add),
                        inc=1)
                    self.mark("p2_ps1", "v")
                    self.wait("sync", "v")
                    ro_ap = self.rows_t[self.RO[ro]:self.RO[ro] + NST * L].rearrange(
                        "(n l) -> n l", n=NST)[:, ch2 * CH2 + gg * 512: ch2 * CH2 + (gg + 1) * 512]
                    self.store(ro_ap, stg, nm="str")
                    self.mark("p2_stg", "str")
            for gg in range(CH2 // 512):
                self.mwait("tensor", "p2_ps1")
                for kt in range(NFT):
                    self.peop(lambda t, kt=kt, gg=gg: t.matmul(
                        self.PS[1][0:NDLT, 0:512],
                        self.GS[:, self.GSO["wd1T"] + kt * NDLT: self.GSO["wd1T"] + (kt + 1) * NDLT],
                        h_t(kt)[:, gg * 512:(gg + 1) * 512],
                        start=(kt == 0), stop=(kt == NFT - 1)))
                self.wait("vector", "pe")
                self.mwait("vector", "p2_u2s")
                self.vop(lambda v: v.tensor_scalar(
                    u2s[0:NDLT, :], self.PS[1][0:NDLT, 0:512],
                    g[0:NDLT, GO["b_d1"]:GO["b_d1"] + 1], None, OP.add), inc=1)
                self.mark("p2_ps1", "v")
                u2done = self.cnt["v"]
                self.wait("tensor", "v", u2done)
                for mt in range(NFT):
                    ps = self.PS[0]
                    self.mwait("tensor", "p2_dpsa")
                    self.peop(lambda t, mt=mt, ps=ps: t.matmul(
                        ps[:, 0:512],
                        self.GS[0:NDLT, self.GSO["wd2T"] + mt * 128: self.GSO["wd2T"] + (mt + 1) * 128],
                        u2s[0:NDLT, 0:512], start=True, stop=True), inc=1)
                    if mt == NFT - 1:
                        self.mark("p2_u2s", "pe")
                    self.wait("scalar", "pe")
                    self.mwait("scalar", "p2_dev")
                    self.sop(lambda s, mt=mt, ps=ps: s.activation(
                        ev1, ps[:, 0:512], AF.Abs,
                        bias=g[:, GO["b_d2"] + mt:GO["b_d2"] + mt + 1]))
                    self.sop(lambda s: s.activation(ev2, ev1, AF.Exp, scale=-1.0))
                    self.sop(lambda s: s.activation(ev2, ev2, AF.Ln, bias=1.0), inc=1)
                    self.wait("vector", "s")
                    self.vop(lambda v, mt=mt, ps=ps: v.tensor_scalar(
                        ev1, ps[:, 0:512], g[:, GO["b_d2"] + mt:GO["b_d2"] + mt + 1], 0.0,
                        OP.add, OP.max))
                    self.vop(lambda v, mt=mt, gg=gg: v.tensor_tensor(
                        d_t(mt)[:, gg * 512:(gg + 1) * 512], ev1, ev2, OP.add), inc=1)
                    self.mark("p2_dpsa", "v")
                    self.mark("p2_dev", "v")
            h_pe_free[ch2 + 1] = self.cnt["pe"]
        emit_h(0)
        emit_bmd(0)
        for ch in range(NCH2):
            self.wait("vector", "pe")
            self.wait("scalar", "pe")
            self.wait("vector", "s", h_cnt[ch])
            for ft in range(NFT):
                self.vop(lambda v, ft=ft: v.tensor_tensor(dh_t(ft), d_t(ft), h_t(ft), OP.mult))
                self.vop(lambda v, ft=ft: v.tensor_scalar(
                    y_t(ft), h_t(ft), g[:, GO["D"] + ft:GO["D"] + ft + 1], None, OP.mult))
            dh_done = self.cnt["v"]
            self.mark("p2_hfree", "v")
            self.mwait("scalar", "p2_evdone")
            self.mwait("vector", "p2_evdone")
            self.wait("gpsimd", "str")
            self.wait("scalar", "v", dh_done)
            NPV = 16
            SO1, TMPP, YPO = 28672, 29184, 24576
            so_b = [so_, S[:, SO1:SO1 + CH2]]
            tm_p = S[:, TMPP:TMPP + CH2]
            yp_t = lambda ft: S[:, YPO + ft * CH2: YPO + (ft + 1) * CH2]
            bcp6 = bcp + [S[:, SO1 + 1024 + i * CH2: SO1 + 1024 + (i + 1) * CH2]
                          for i in range(2)]
            bc_cnt = {}
            BCS = ("g0", "g1", "l3")

            def issue_bc(n):
                bmb = bcp6[2 * (n % 3)]
                cmb = bcp6[2 * (n % 3) + 1]
                gsem = BCS[n % 3]
                self.mwait("gpsimd", "p2_bcv%d" % (n % 3))
                self.mwait("gpsimd", "p2_bcp%d" % (n % 3))
                self.dma("gpsimd", gsem, bmb, self.rows_t[
                    self.RO["Bm"] + n * L + ch * CH2: self.RO["Bm"] + n * L + (ch + 1) * CH2]
                    .unsqueeze(0).partition_broadcast(128))
                self.dma("gpsimd", gsem, cmb, self.rows_t[
                    self.RO["Cm"] + n * L + ch * CH2: self.RO["Cm"] + n * L + (ch + 1) * CH2]
                    .unsqueeze(0).partition_broadcast(128))
                bc_cnt[n] = self.cnt[gsem]
            self.mwait("gpsimd", "p2_ypfree")
            issue_bc(0)
            issue_bc(1)
            it = 0
            for n in range(NST):
                bmb = bcp6[2 * (n % 3)]
                cmb = bcp6[2 * (n % 3) + 1]
                gsem = BCS[n % 3]
                if n + 2 < NST:
                    issue_bc(n + 2)
                self.wait("vector", gsem, bc_cnt[n])
                on_pool = n >= NPV
                if on_pool:
                    self.wait("gpsimd", gsem, bc_cnt[n])
                for ft in range(NFT):
                    ab = ab_[it % 2]
                    so_c = so_b[it % 2]
                    self.mwait("scalar", "p2_ab%d" % (it % 2))
                    self.sop(lambda s, ab=ab, ft=ft, n=n: s.activation(
                        ab, d_t(ft), AF.Exp,
                        scale=g[:, GO["negA"] + ft * NST + n: GO["negA"] + ft * NST + n + 1]),
                        inc=1)
                    sdone = self.cnt["s"]
                    self.wait("vector", "s", sdone)
                    self.mwait("vector", "p2_so%d" % (it % 2))
                    self.vop(lambda v, ft=ft, bmb=bmb: v.tensor_tensor(bx_, dh_t(ft), bmb, OP.mult))
                    ini = 0.0 if ch == 0 else self.CAR[:, n * NFT + ft:n * NFT + ft + 1]
                    self.vop(lambda v, ab=ab, ini=ini, so_c=so_c: v.tensor_tensor_scan(
                        so_c, ab, bx_, ini, OP.mult, OP.add), inc=1)
                    self.mark("p2_ab%d" % (it % 2), "v")
                    tts_v = self.cnt["v"]
                    if ch < NCH2 - 1:
                        self.vop(lambda v, n=n, ft=ft, so_c=so_c: v.tensor_copy(
                            self.CAR[:, n * NFT + ft:n * NFT + ft + 1], so_c[:, CH2 - 1:CH2]))
                    if on_pool:
                        self.wait("gpsimd", "v", tts_v)
                        if n == NPV:
                            self.gop(lambda p, ft=ft, so_c=so_c, cmb=cmb: p.tensor_tensor(
                                yp_t(ft), so_c, cmb, OP.mult))
                        else:
                            self.gop(lambda p, so_c=so_c, cmb=cmb: p.tensor_tensor(
                                tm_p, so_c, cmb, OP.mult))
                            self.gop(lambda p, ft=ft: p.tensor_tensor(
                                yp_t(ft), yp_t(ft), tm_p, OP.add))
                        self.mark("p2_so%d" % (it % 2), "p")
                        if ft == NFT - 1:
                            self.mark("p2_bcp%d" % (n % 3), "p")
                    else:
                        self.vop(lambda v, cmb=cmb, so_c=so_c: v.tensor_tensor(
                            tm_, so_c, cmb, OP.mult))
                        self.vop(lambda v, ft=ft: v.tensor_tensor(y_t(ft), y_t(ft), tm_, OP.add))
                    it += 1
                self.mark("p2_bcv%d" % (n % 3), "v")
                if not on_pool:
                    self.mark("p2_bcp%d" % (n % 3), "p")
            if NPV < NST:
                pool_done = self.cnt["p"]
                self.wait("vector", "p", pool_done)
                for ft in range(NFT):
                    self.vop(lambda v, ft=ft: v.tensor_tensor(y_t(ft), y_t(ft), yp_t(ft), OP.add))
            merge_done = self.cnt["v"]
            if ch + 1 < NCH2:
                emit_h(ch + 1)
            for ft in range(NFT):
                self.sop(lambda s, ft=ft, ch=ch: s.activation(
                    d_t(ft), self.u_ft(ft)[:, 1 + ch * CH2: 1 + (ch + 1) * CH2], AF.Silu,
                    scale=g[:, GO["rb"]:GO["rb"] + 1],
                    bias=g[:, GO["b_in"] + ft:GO["b_in"] + ft + 1]))
            self.wait("vector", "s")
            for ft in range(NFT):
                self.vop(lambda v, ft=ft: v.tensor_tensor(dh_t(ft), y_t(ft), d_t(ft), OP.mult))
            g_done = self.cnt["v"]
            g_t = dh_t
            self.wait("tensor", "v", g_done)
            self.wait("scalar", "v", g_done)
            ncol = CH2 // 128
            for tb in range(ncol):
                for ft in range(NFT):
                    self.peop(lambda t, ft=ft, tb=tb: t.matmul(
                        self.PS[4][:, tb:tb + 1], g_t(ft)[:, tb * 128:(tb + 1) * 128],
                        self.ONE[:, 0:1], start=(ft == 0), stop=(ft == NFT - 1)))
            for ft in range(NFT):
                self.mwait("scalar", "p2_gsq")
                self.sop(lambda s, ft=ft: s.activation(gsq, g_t(ft), AF.Square), inc=1)
                self.wait("tensor", "s")
                for tb in range(ncol):
                    self.peop(lambda t, ft=ft, tb=tb, ncol=ncol: t.matmul(
                        self.PS[5][:, ft * ncol + tb: ft * ncol + tb + 1],
                        gsq[:, tb * 128:(tb + 1) * 128],
                        self.ONE[:, 0:1], start=True, stop=True))
                self.mark("p2_gsq", "pe")
            self.wait("vector", "pe")
            sg = s32[:, GN:GN + ncol]
            sgg = s32[:, GN + 8:GN + 8 + ncol]
            mug = s32[:, GN + 16:GN + 16 + ncol]
            rsg = s32[:, GN + 24:GN + 24 + ncol]
            mrs = s32[:, GN + 32:GN + 32 + ncol]
            tg = s32[:, GN + 40:GN + 40 + ncol]
            self.vop(lambda v: v.tensor_copy(sg, self.PS[4][:, 0:ncol]))
            self.vop(lambda v: v.tensor_copy(sgg, self.PS[5][:, 0:ncol]))
            for ft in range(1, NFT):
                self.vop(lambda v, ft=ft, ncol=ncol: v.tensor_tensor(
                    sgg, sgg, self.PS[5][:, ft * ncol:(ft + 1) * ncol], OP.add))
            self.vop(lambda v: v.tensor_copy(sg, sg), inc=1)
            self.wait("tensor", "v")
            self.vop(lambda v: v.tensor_scalar_mul(mug, sg, 1.0 / FD))
            self.vop(lambda v: v.tensor_scalar_mul(tg, sgg, 1.0 / FD))
            self.vop(lambda v: v.tensor_tensor(rsg, mug, mug, OP.mult))
            self.vop(lambda v: v.tensor_tensor(tg, tg, rsg, OP.subtract), inc=1)
            self.wait("scalar", "v")
            self.sop(lambda s: s.activation(tg, tg, AF.Sqrt, bias=g[:, GO["eps"]:GO["eps"] + 1]), inc=1)
            self.wait("vector", "s")
            self.vop(lambda v: v.reciprocal(rsg, tg))
            self.vop(lambda v: v.tensor_tensor(mrs, mug, rsg, OP.mult), inc=1)
            self.wait("tensor", "v")
            self.peop(lambda t: t.transpose(self.PT32[0:16, 0:128], s32[:, GN + 24:GN + 40],
                                            self.IDN32[:]), inc=1)
            self.wait("vector", "pe")
            rowst = s32[0:16, self.TO["rowst"]:self.TO["rowst"] + 128]
            self.vop(lambda v, rowst=rowst: v.tensor_copy(rowst, self.PT32[0:16, 0:128]), inc=1)
            self.wait("sync", "v")
            self.store(self.row("rsg", ch * CH2, (ch + 1) * CH2), rowst[0:ncol, :], nm="str")
            self.store(self.row("mrsg", ch * CH2, (ch + 1) * CH2), rowst[8:8 + ncol, :], nm="str")
            rows_st = self.cnt["str"]
            if ch + 1 < NCH2:
                emit_bmd(ch + 1)
            self.wait("gpsimd", "str", rows_st)
            self.mwait("gpsimd", "p2_bc0")
            self.mwait("gpsimd", "p2_bc1")
            rsb = self.SF[:, 0:CH2]
            mrb = self.SF[:, CH2:2 * CH2]
            a1b = self.SF[:, 2 * CH2:3 * CH2]
            b0b = self.SF[:, 3 * CH2:4 * CH2]
            self.dma("gpsimd", "g0", rsb, self.row("rsg", ch * CH2, (ch + 1) * CH2).partition_broadcast(128))
            self.dma("gpsimd", "g0", mrb, self.row("mrsg", ch * CH2, (ch + 1) * CH2).partition_broadcast(128))
            self.dma("gpsimd", "g0", a1b, self.row("a0", ch * CH2, (ch + 1) * CH2).partition_broadcast(128))
            self.dma("gpsimd", "g0", b0b, self.row("b0", ch * CH2, (ch + 1) * CH2).partition_broadcast(128))
            self.wait("vector", "g0")
            self.vop(lambda v, a1b=a1b: v.tensor_scalar_add(a1b, a1b, 1.0), inc=1)
            self.wait("sync", "v", merge_done)
            for ft in range(NFT):
                self.dma("sync", "l2", yp_t(ft),
                         self.x16[b, ch * CH2:(ch + 1) * CH2, ft * 128:(ft + 1) * 128],
                         transpose=True)
            xres_ld = self.cnt["l2"]
            x_t = yp_t
            for mt in range(NFT):
                ring = wring[mt % 2]
                lsem = "l%d" % (mt % 2)
                self.mwait("sync", "p2_wr%d" % (mt % 2))
                self.dma("sync", lsem, ring, self.ws_out[mt])
                self.wait("tensor", lsem)
                for gg in range(CH2 // 512):
                    ps = self.PS[2 + (mt % 2)]
                    self.mwait("tensor", "p2_wps%d" % (mt % 2))
                    for kt in range(NFT):
                        self.peop(lambda t, kt=kt, gg=gg, ring=ring, ps=ps: t.matmul(
                            ps[:, 0:512], ring[:, kt * 128:(kt + 1) * 128],
                            g_t(kt)[:, gg * 512:(gg + 1) * 512],
                            start=(kt == 0), stop=(kt == NFT - 1)))
                    if gg == CH2 // 512 - 1:
                        self.mark("p2_wr%d" % (mt % 2), "pe")
                    self.wait("vector", "pe")
                    if mt == 0 and gg == 0:
                        self.wait("vector", "l2", xres_ld)
                    gsl = slice(gg * 512, (gg + 1) * 512)
                    e132 = self.SF[:, 4 * CH2:4 * CH2 + 512]
                    e232 = self.SF[:, 4 * CH2 + 512:4 * CH2 + 1024]
                    self.mwait("vector", "p2_ev")
                    self.vop(lambda v, ps=ps, rsb=rsb, gsl=gsl, e132=e132: v.tensor_tensor(
                        e132, ps[:, 0:512], rsb[:, gsl], OP.mult))
                    self.vop(lambda v, mt=mt, mrb=mrb, gsl=gsl, e132=e132: v.scalar_tensor_tensor(
                        e132, mrb[:, gsl], g[:, GO["nwsum"] + mt:GO["nwsum"] + mt + 1], e132,
                        OP.mult, OP.add), inc=1)
                    self.mark("p2_wps%d" % (mt % 2), "v")
                    self.vop(lambda v, mt=mt, a1b=a1b, gsl=gsl, e232=e232: v.tensor_tensor(
                        e232, x_t(mt)[:, gsl], a1b[:, gsl], OP.mult))
                    self.vop(lambda v, b0b=b0b, gsl=gsl, e232=e232: v.tensor_tensor(
                        e232, e232, b0b[:, gsl], OP.add))
                    self.vop(lambda v, mt=mt, ch=ch, gsl=gsl, e132=e132, e232=e232:
                             v.scalar_tensor_tensor(
                        self.X1[:, mt * L + ch * CH2 + gsl.start: mt * L + ch * CH2 + gsl.stop],
                        e132, g[:, GO["b_out"] + mt:GO["b_out"] + mt + 1], e232, OP.add, OP.add),
                        inc=1)
                    self.mark("p2_ev", "v")
            self.mark("p2_ypfree", "v")
            self.mark("p2_bc0", "v")
            self.mark("p2_bc1", "v")
            self.mark("p2_evdone", "v")
            self.flush_all()
        self.tap("h", S[:, O["dH"]:O["dH"] + CH2], (128, CH2))
        self.tap("y", S[:, O["y"]:O["y"] + CH2], (128, CH2))
        self.tap("x1", self.X1[:, 0:L], (128, L))
        self.tap("x1full", self.X1[:], (128, NFT * L))

    def ffn(self, b):
        g = self.G32
        GO = self.GO
        S = self.S
        s32 = self.S32
        FF = self.TO["ff"]
        O = {"xq": 0, "gu": 4096, "wq": 20480, "tq": 28672, "amax": 30208,
             "ev": 30720, "sq": 31744}
        xq_t = lambda kt: S[:, O["xq"] + kt * TFC: O["xq"] + (kt + 1) * TFC]
        gu_t = lambda mt: S[:, O["gu"] + mt * TFC: O["gu"] + (mt + 1) * TFC]
        wq_ring = [S[:, O["wq"] + i * 1024: O["wq"] + (i + 1) * 1024] for i in range(8)]
        tq = [S[:, O["tq"] + i * 512: O["tq"] + (i + 1) * 512] for i in range(3)]
        amax = S[:, O["amax"]:O["amax"] + 512]
        ev1 = S[:, O["ev"]:O["ev"] + 512]
        ev2 = S[:, O["ev"] + 512:O["ev"] + 1024]
        sqb = S[:, O["sq"]:O["sq"] + 512]
        aq_b = [self.SF[:, 0:512], self.SF[:, 512:1024]]
        bq_b = [self.SF[:, 1024:1536], self.SF[:, 1536:2048]]
        cg_b = [self.SF[:, 2048:2560], self.SF[:, 2560:3072]]
        cu_b = [self.SF[:, 3072:3584], self.SF[:, 3584:4096]]
        s232 = self.SF[:, 4096:4608]
        ev32 = self.SF[:, 4608:5120]
        cdb32 = self.SF[:, 5120:5632]
        aq16 = S[:, 32256:32768]
        bq16 = S[:, 32768:33280]
        s2b16 = tq[0]
        sum32 = s32[:, self.TO["sum32"]:self.TO["sum32"] + 512]
        otm = [s32[:, self.TO["otm"] + i * 512: self.TO["otm"] + (i + 1) * 512] for i in range(2)]
        self.wait("sync", "sts0")
        self.wait("sync", "sts1")
        rowst = s32[0:16, self.TO["rowst"]:self.TO["rowst"] + 128]
        tt = s32[:, FF + 52:FF + 56]
        ab_cnt = {}
        xq_cnt = {}

        def emit_stats(tc2, vguard):
            x1s = lambda ft, tc2=tc2: self.X1[:, ft * L + tc2 * TFC: ft * L + (tc2 + 1) * TFC]
            self.wait("scalar", "v", vguard)
            self.mwait("scalar", "ff_gps0")
            self.mwait("scalar", "ff_gps1")
            for ft in range(NFT):
                self.mwait("scalar", "ff_sq")
                self.sop(lambda s, ft=ft, x1s=x1s: s.activation(sqb, x1s(ft), AF.Square), inc=1)
                self.wait("tensor", "s")
                for tb in range(4):
                    self.peop(lambda t, ft=ft, tb=tb, x1s=x1s: t.matmul(
                        self.PS[0][:, ft * 4 + tb: ft * 4 + tb + 1],
                        x1s(ft)[:, tb * 128:(tb + 1) * 128],
                        self.ONE[:, 0:1], start=True, stop=True))
                for tb in range(4):
                    self.peop(lambda t, ft=ft, tb=tb: t.matmul(
                        self.PS[1][:, ft * 4 + tb: ft * 4 + tb + 1],
                        sqb[:, tb * 128:(tb + 1) * 128],
                        self.ONE[:, 0:1], start=True, stop=True))
                self.mark("ff_sq", "pe")
            self.mwait("vector", "ff_ev")
            self.vop(lambda v, x1s=x1s: v.tensor_tensor(ev1, x1s(0), x1s(1), OP.max))
            self.vop(lambda v, x1s=x1s: v.tensor_tensor(ev2, x1s(0), x1s(1), OP.min))
            for ft in range(2, NFT):
                self.vop(lambda v, ft=ft, x1s=x1s: v.tensor_tensor(ev1, ev1, x1s(ft), OP.max))
                self.vop(lambda v, ft=ft, x1s=x1s: v.tensor_tensor(ev2, ev2, x1s(ft), OP.min))
            mmx_done = self.cnt["v"]
            self.wait("tensor", "v", mmx_done)
            for blk in range(4):
                self.peop(lambda t, blk=blk: t.transpose(
                    self.PT16[:, 0:128], ev1[:, blk * 128:(blk + 1) * 128], self.IDN[:]), inc=1)
                self.wait("vector", "pe")
                self.vop(lambda v, blk=blk: v.tensor_reduce(
                    s32[:, FF + 8 + blk:FF + 9 + blk], self.PT16[:, 0:128], AX.X, OP.max), inc=1)
                self.wait("tensor", "v")
                self.peop(lambda t, blk=blk: t.transpose(
                    self.PT16[:, 0:128], ev2[:, blk * 128:(blk + 1) * 128], self.IDN[:]), inc=1)
                self.wait("vector", "pe")
                self.vop(lambda v, blk=blk: v.tensor_reduce(
                    s32[:, FF + 12 + blk:FF + 13 + blk], self.PT16[:, 0:128], AX.X, OP.min),
                    inc=1)
                self.wait("tensor", "v")
            self.mark("ff_ev", "v")
            sx = s32[:, FF:FF + 4]
            sxx = s32[:, FF + 4:FF + 8]
            mx = s32[:, FF + 8:FF + 12]
            nmn = s32[:, FF + 12:FF + 16]
            mu = s32[:, FF + 16:FF + 20]
            var = s32[:, FF + 20:FF + 24]
            rst = s32[:, FF + 24:FF + 28]
            am = s32[:, FF + 28:FF + 32]
            s1 = s32[:, FF + 32:FF + 36]
            aq = s32[:, FF + 36:FF + 40]
            bq2 = s32[:, FF + 40:FF + 44]
            cgc = s32[:, FF + 44:FF + 48]
            cuc = s32[:, FF + 48:FF + 52]
            tt = s32[:, FF + 52:FF + 56]
            self.vop(lambda v: v.tensor_copy(sx, self.PS[0][:, 0:4]))
            self.vop(lambda v: v.tensor_copy(sxx, self.PS[1][:, 0:4]))
            for ft in range(1, NFT):
                self.vop(lambda v, ft=ft: v.tensor_tensor(
                    sx, sx, self.PS[0][:, ft * 4:(ft + 1) * 4], OP.add))
                self.vop(lambda v, ft=ft: v.tensor_tensor(
                    sxx, sxx, self.PS[1][:, ft * 4:(ft + 1) * 4], OP.add))
            self.vop(lambda v: v.tensor_copy(sx, sx), inc=1)
            self.mark("ff_gps0", "v")
            self.mark("ff_gps1", "v")
            self.wait("tensor", "v")
            self.vop(lambda v: v.tensor_scalar_mul(mu, sx, 1.0 / FD))
            self.vop(lambda v: v.tensor_scalar_mul(var, sxx, 1.0 / FD))
            self.vop(lambda v: v.tensor_tensor(tt, mu, mu, OP.mult))
            self.vop(lambda v: v.tensor_tensor(var, var, tt, OP.subtract), inc=1)
            self.wait("scalar", "v")
            self.sop(lambda s: s.activation(tt, var, AF.Sqrt, bias=g[:, GO["eps"]:GO["eps"] + 1]), inc=1)
            self.wait("vector", "s")
            self.vop(lambda v: v.reciprocal(rst, tt))
            self.vop(lambda v: v.tensor_tensor(am, mx, mu, OP.subtract))
            self.vop(lambda v: v.tensor_tensor(tt, mu, nmn, OP.subtract))
            self.vop(lambda v: v.tensor_tensor(am, am, tt, OP.max))
            self.vop(lambda v: v.tensor_tensor(am, am, rst, OP.mult))
            self.vop(lambda v: v.tensor_scalar_max(am, am, 1e-5))
            self.vop(lambda v: v.reciprocal(tt, am))
            self.vop(lambda v: v.tensor_scalar_mul(s1, tt, 127.0))
            self.vop(lambda v: v.tensor_tensor(aq, rst, s1, OP.mult))
            self.vop(lambda v: v.tensor_tensor(bq2, mu, aq, OP.mult))
            self.vop(lambda v: v.tensor_scalar(bq2, bq2, -1.0, 0.5, OP.mult, OP.add))
            self.vop(lambda v: v.reciprocal(tt, s1))
            self.vop(lambda v: v.tensor_scalar(cgc, tt, g[:, GO["sw"] + 3:GO["sw"] + 4], None, OP.mult))
            self.vop(lambda v: v.tensor_scalar(cuc, tt, g[:, GO["sw"] + 4:GO["sw"] + 5], None, OP.mult),
                     inc=1)
            self.wait("tensor", "v")
            self.peop(lambda t: t.transpose(self.PT32[0:16, 0:128], s32[:, FF + 36:FF + 52],
                                            self.IDN32[:]), inc=1)
            self.wait("vector", "pe")
            rowst = s32[0:16, self.TO["rowst"]:self.TO["rowst"] + 128]
            self.mwait("vector", "ff_rowst")
            self.vop(lambda v, rowst=rowst: v.tensor_copy(rowst, self.PT32[0:16, 0:128]), inc=1)
            self.wait("sync", "v")
            for (i, nm) in ((0, "aq"), (4, "bq2"), (8, "cg"), (12, "cu")):
                self.store(self.row(nm, tc2 * TFC, (tc2 + 1) * TFC), rowst[i:i + 4, :], nm="str")
            self.mark("ff_rowst", "str")
            self.wait("gpsimd", "str", self.cnt["str"])
            self.mwait("gpsimd", "ff_ab%d" % (tc2 % 2))
            for (bt, nm) in ((aq_b, "aq"), (bq_b, "bq2"), (cg_b, "cg"), (cu_b, "cu")):
                self.dma("gpsimd", "g0", bt[tc2 % 2],
                         self.row(nm, tc2 * TFC, (tc2 + 1) * TFC).partition_broadcast(128))
            ab_cnt[tc2] = self.cnt["g0"]

        def emit_xq(tc2):
            x1s = lambda ft, tc2=tc2: self.X1[:, ft * L + tc2 * TFC: ft * L + (tc2 + 1) * TFC]
            aqb, bqb = aq_b[tc2 % 2], bq_b[tc2 % 2]
            self.wait("vector", "g0", ab_cnt[tc2])
            self.vop(lambda v, aqb=aqb: v.tensor_copy(aq16, aqb))
            self.vop(lambda v, bqb=bqb: v.tensor_copy(bq16, bqb), inc=1)
            for ft in range(NFT):
                self.vop(lambda v, ft=ft, x1s=x1s: v.tensor_tensor(tq[2], x1s(ft), aq16, OP.mult))
                self.vop(lambda v: v.tensor_tensor(tq[2], tq[2], bq16, OP.add))
                self.vop(lambda v: v.tensor_copy(self.I16T[:, 0:512], tq[2]))
                self.vop(lambda v: v.tensor_copy(ev2, self.I16T[:, 0:512]))
                self.vop(lambda v: v.tensor_tensor(sqb, ev2, tq[2], OP.is_gt))
                self.vop(lambda v: v.scalar_tensor_tensor(ev1, ev2, 1.0, tq[2], OP.add, OP.is_le))
                self.vop(lambda v: v.tensor_tensor(ev2, ev2, sqb, OP.subtract))
                self.vop(lambda v, ft=ft: v.tensor_tensor(xq_t(ft), ev2, ev1, OP.add))
            xq_cnt[tc2] = self.cnt["v"]
        emit_stats(0, self.cnt["v"])
        emit_xq(0)
        for tc in range(NTFC):
            x1_t = lambda ft: self.X1[:, ft * L + tc * TFC: ft * L + (tc + 1) * TFC]
            self.wait("scalar", "v")
            self.wait("scalar", "pe")
            self.wait("vector", "pe")
            cgb, cub = cg_b[tc % 2], cu_b[tc % 2]
            if tc > 0:
                emit_xq(tc)
            self.wait("tensor", "v", xq_cnt[tc])
            ring_cnt = {}

            def issue_gu(mt):
                lsem = "l%d" % (mt % 4)
                self.mwait("sync", "ff_ring%d" % (mt % 4))
                self.dma("sync", lsem, wq_ring[(2 * mt) % 8], self.ws_g[mt])
                self.dma("sync", lsem, wq_ring[(2 * mt + 1) % 8], self.ws_u[mt])
                ring_cnt[mt] = self.cnt[lsem]
            issue_gu(0)
            issue_gu(1)
            issue_gu(2)
            for mt in range(NMT_I):
                rg = wq_ring[(2 * mt) % 8]
                ru = wq_ring[(2 * mt + 1) % 8]
                lsem = "l%d" % (mt % 4)
                if mt + 3 < NMT_I:
                    issue_gu(mt + 3)
                self.wait("tensor", lsem, ring_cnt[mt])
                psg = self.PS[mt % 2]
                psu = self.PS[2 + mt % 2]
                self.mwait("tensor", "ff_gps%d" % (mt % 2))
                for kt in range(NFT):
                    self.peop(lambda t, kt=kt, rg=rg, psg=psg: t.matmul(
                        psg[:, 0:512], rg[:, kt * 128:(kt + 1) * 128], xq_t(kt),
                        start=(kt == 0), stop=(kt == NFT - 1)))
                for kt in range(NFT):
                    self.peop(lambda t, kt=kt, ru=ru, psu=psu: t.matmul(
                        psu[:, 0:512], ru[:, kt * 128:(kt + 1) * 128], xq_t(kt),
                        start=(kt == 0), stop=(kt == NFT - 1)))
                self.mark("ff_ring%d" % (mt % 4), "pe")
                self.wait("vector", "pe")
                self.mwait("vector", "ff_gev")
                self.vop(lambda v, psg=psg, cgb=cgb: v.tensor_tensor(ev1, psg[:, 0:512], cgb, OP.mult),
                         inc=1)
                self.wait("scalar", "v")
                self.sop(lambda s: s.activation(ev2, ev1, AF.Silu), inc=1)
                self.vop(lambda v, psu=psu, cub=cub: v.tensor_tensor(sqb, psu[:, 0:512], cub, OP.mult),
                         inc=1)
                self.mark("ff_gps%d" % (mt % 2), "v")
                self.wait("vector", "s")
                self.vop(lambda v, mt=mt: v.tensor_tensor(gu_t(mt), ev2, sqb, OP.mult))
                amin = tq[1]
                if mt == 0:
                    self.vop(lambda v: v.tensor_copy(amax, gu_t(0)))
                    self.vop(lambda v, amin=amin: v.tensor_copy(amin, gu_t(0)), inc=1)
                else:
                    self.vop(lambda v, mt=mt: v.tensor_tensor(amax, amax, gu_t(mt), OP.max))
                    self.vop(lambda v, mt=mt, amin=amin: v.tensor_tensor(
                        amin, amin, gu_t(mt), OP.min), inc=1)
            self.vop(lambda v: v.tensor_scalar_mul(tq[1], tq[1], -1.0))
            self.vop(lambda v: v.tensor_tensor(amax, amax, tq[1], OP.max), inc=1)
            gu_done = self.cnt["v"]
            self.mark("ff_cgcu", "v")
            self.mark("ff_s2g", "v")
            self.mark("ff_ab%d" % (tc % 2), "v")
            if tc == NTFC - 1 and ("gupre" in self.taps):
                self.barrier()
                self.tap("gupre", S[:, O["gu"]:O["gu"] + TFC], (128, TFC))
                self.tap("gupre7", S[:, O["gu"] + 7 * TFC:O["gu"] + 8 * TFC], (128, TFC))
                self.barrier()
            self.wait("tensor", "v", gu_done)
            for blk in range(4):
                self.peop(lambda t, blk=blk: t.transpose(
                    self.PT16[:, 0:128], amax[:, blk * 128:(blk + 1) * 128], self.IDN[:]), inc=1)
                self.wait("vector", "pe")
                self.vop(lambda v, blk=blk: v.tensor_reduce(
                    s32[:, FF + 56 + blk:FF + 57 + blk], self.PT16[:, 0:128], AX.X, OP.max), inc=1)
                self.wait("tensor", "v")
            am2 = s32[:, FF + 56:FF + 60]
            s2c = s32[:, FF + 60:FF + 64]
            cdc = s32[:, FF + 64:FF + 68]
            self.vop(lambda v: v.tensor_scalar_max(am2, am2, 1e-5))
            self.vop(lambda v: v.reciprocal(tt, am2))
            self.vop(lambda v: v.tensor_scalar_mul(s2c, tt, 127.0))
            self.vop(lambda v: v.reciprocal(tt, s2c))
            self.vop(lambda v: v.tensor_scalar(cdc, tt, g[:, GO["sw"] + 5:GO["sw"] + 6], None, OP.mult),
                     inc=1)
            self.wait("tensor", "v")
            self.peop(lambda t: t.transpose(self.PT32[0:8, 0:128], s32[:, FF + 60:FF + 68],
                                            self.IDN32[:]), inc=1)
            self.wait("vector", "pe")
            self.mwait("vector", "ff_rowst")
            self.vop(lambda v, rowst=rowst: v.tensor_copy(rowst[0:8, :], self.PT32[0:8, 0:128]),
                     inc=1)
            self.wait("sync", "v")
            self.store(self.row("s2", tc * TFC, (tc + 1) * TFC), rowst[0:4, :], nm="str")
            self.store(self.row("cd", tc * TFC, (tc + 1) * TFC), rowst[4:8, :], nm="str")
            self.mark("ff_rowst", "str")
            r_st = self.cnt["str"]
            self.wait("gpsimd", "str", r_st)
            s2b = s232
            self.mwait("gpsimd", "ff_s2g")
            self.dma("gpsimd", "g0", s2b, self.row("s2", tc * TFC, (tc + 1) * TFC).partition_broadcast(128))
            self.wait("vector", "g0")
            cdb = cdb32
            self.mwait("gpsimd", "ff_cgcu")
            self.dma("gpsimd", "g1", cdb, self.row("cd", tc * TFC, (tc + 1) * TFC).partition_broadcast(128))
            for par in range(4):
                self.mwait("sync", "ff_ring%d" % par)
            self.mwait("sync", "ff_dr0")
            self.mwait("sync", "ff_dr1")
            for q in range(4):
                self.dma("sync", "l0", wq_ring[2 * q], self.ws_d[0, :, q * 1024:(q + 1) * 1024])
                self.dma("sync", "l1", wq_ring[2 * q + 1], self.ws_d[1, :, q * 1024:(q + 1) * 1024])
            self.wait("vector", "s")
            self.vop(lambda v, s2b=s2b: v.tensor_copy(s2b16, s2b), inc=1)
            gq_cnt = []
            for mt in range(NMT_I):
                self.vop(lambda v, mt=mt: v.tensor_tensor(tq[2], gu_t(mt), s2b16, OP.mult))
                self.vop(lambda v: v.tensor_single_scalar(tq[2], tq[2], 0.5, OP.add))
                self.vop(lambda v: v.tensor_copy(self.I16T[:, 0:512], tq[2]))
                self.vop(lambda v: v.tensor_copy(ev2, self.I16T[:, 0:512]))
                self.vop(lambda v: v.tensor_tensor(sqb, ev2, tq[2], OP.is_gt))
                self.vop(lambda v: v.scalar_tensor_tensor(ev1, ev2, 1.0, tq[2], OP.add, OP.is_le))
                self.vop(lambda v: v.tensor_tensor(ev2, ev2, sqb, OP.subtract))
                self.vop(lambda v, mt=mt: v.tensor_tensor(gu_t(mt), ev2, ev1, OP.add))
                gq_cnt.append(self.cnt["v"])
            gq_done = self.cnt["v"]
            gq_t = gu_t
            self.wait("vector", "g1")

            def epi(mtm, ps):
                self.wait("vector", "pe")
                self.mwait("vector", "ff_sum")
                self.vop(lambda v, ps=ps, cdb=cdb: v.tensor_tensor(
                    sum32, ps[:, 0:512], cdb, OP.mult), inc=1)
                self.mark("ff_dps%d" % (mtm % 2), "v")
                self.vop(lambda v, mtm=mtm, tc=tc: v.tensor_tensor(
                    sum32, sum32,
                    self.X1[:, mtm * L + tc * TFC: mtm * L + (tc + 1) * TFC], OP.add), inc=1)
                sum_done = self.cnt["v"]
                self.wait("tensor", "v", sum_done)
                for tb in range(4):
                    self.peop(lambda t, tb=tb: t.transpose(
                        self.PT32[:, tb * 128:(tb + 1) * 128], sum32[:, tb * 128:(tb + 1) * 128],
                        self.IDN32[:]))
                self.mark("ff_sum", "pe")
                self.wait("vector", "pe")
                ot = otm[mtm % 2]
                self.wait("vector", "sto0")
                self.wait("vector", "sto1")
                self.vop(lambda v, ot=ot: v.tensor_copy(ot, self.PT32[:, 0:512]), inc=1)
                self.wait("tensor", "v")
                self.wait("sync", "v")
                dst = self.out_h[b, tc * TFC:(tc + 1) * TFC, mtm * 128:(mtm + 1) * 128] \
                    .rearrange("(tb p) c -> p tb c", p=128)
                osrc = ot.rearrange("p (tb c) -> p tb c", c=128)
                self.store(dst, osrc, nm="sto%d" % (mtm % 2))
                self.mark("ff_ot%d" % (mtm % 2), "sto%d" % (mtm % 2))

            self.wait("tensor", "l0")
            self.wait("tensor", "l1")
            self.mwait("tensor", "ff_dps0")
            self.mwait("tensor", "ff_dps1")
            for kt in range(NMT_I):
                self.wait("tensor", "v", gq_cnt[kt])
                for w in range(2):
                    ring = wq_ring[2 * (kt // 8) + w]
                    self.peop(lambda t, kt=kt, ring=ring, w=w: t.matmul(
                        self.PS[4 + w][:, 0:512], ring[:, (kt % 8) * 128:((kt % 8) + 1) * 128],
                        gq_t(kt), start=(kt == 0), stop=(kt == NMT_I - 1)))
            self.mark("ff_dr0", "pe")
            self.mark("ff_dr1", "pe")

            def issue_dn(mtm):
                lsem = "l%d" % (mtm % 2)
                self.mwait("sync", "ff_dr%d" % (mtm % 2))
                for q in range(4):
                    self.dma("sync", lsem, wq_ring[2 * q + mtm % 2],
                             self.ws_d[mtm, :, q * 1024:(q + 1) * 1024])
                ring_cnt[("d", mtm)] = self.cnt[lsem]
            issue_dn(2)
            issue_dn(3)
            epi(0, self.PS[4])
            epi(1, self.PS[5])
            if tc + 1 < NTFC:
                emit_stats(tc + 1, gq_done)
            for mtm in range(2, NFT):
                lsem = "l%d" % (mtm % 2)
                self.wait("tensor", lsem, ring_cnt[("d", mtm)])
                ps = self.PS[4 + (mtm % 2)]
                self.mwait("tensor", "ff_dps%d" % (mtm % 2))
                for kt in range(NMT_I):
                    ring = wq_ring[2 * (kt // 8) + mtm % 2]
                    self.peop(lambda t, kt=kt, ring=ring, ps=ps: t.matmul(
                        ps[:, 0:512], ring[:, (kt % 8) * 128:((kt % 8) + 1) * 128], gq_t(kt),
                        start=(kt == 0), stop=(kt == NMT_I - 1)))
                self.mark("ff_dr%d" % (mtm % 2), "pe")
                if mtm + 2 < NFT:
                    issue_dn(mtm + 2)
                epi(mtm, ps)
            self.flush_all()
        self.tap("xq", S[:, O["xq"]:O["xq"] + TFC], (128, TFC))
        self.tap("xq7", S[:, O["xq"] + 7 * TFC:O["xq"] + 8 * TFC], (128, TFC))
        self.tap("m1", sqb, (128, TFC))
        self.tap("m2", tq[2], (128, TFC))
        self.tap("uev", ev32, (128, TFC))
        self.tap("fev", ev2, (128, TFC))
        self.tap("gu", S[:, O["gu"]:O["gu"] + TFC], (128, TFC))
        self.tap("amax", amax, (128, TFC))
        self.tap("s2row", self.row("s2", 0, L), (1, L))
        self.tap("cgrow", self.row("cg", 0, L), (1, L))

    def finish(self):
        for nm in ("sto0", "sto1", "str", "sts0", "sts1", "l0", "l1", "l2", "l3",
                   "lw", "g0", "g1"):
            self.wait("gpsimd", nm)
        self.flush_all()
        for cm in reversed(self._ctx):
            cm.__exit__(None, None, None)


_BLD_CACHE = {}


def build(upto="full", taps=()):
    key = (upto, tuple(sorted(taps)))
    if key not in _BLD_CACHE:
        _BLD_CACHE[key] = B(upto=upto, taps=taps)
    return _BLD_CACHE[key]


def make_in_maps(inputs):
    x = np.ascontiguousarray(inputs["L0_attn_x"], dtype=np.float32)
    wmap = {}
    for k, v in inputs.items():
        if k == "L0_attn_x":
            continue
        nm = k.replace("L0_attn_", "").replace("L0_ffn_", "")
        wmap[nm] = np.ascontiguousarray(v, dtype=np.float32)
    in_maps = []
    for i in range(8):
        m = {"x": x[2 * i:2 * i + 2]}
        m.update(wmap)
        in_maps.append(m)
    return in_maps


def kernel(**inputs):
    bld = build()
    in_maps = make_in_maps(inputs)
    res = run_bass_kernel_spmd(bld.nc, in_maps, core_ids=list(range(8)))
    return np.concatenate([r["out"] for r in res.results], axis=0)
